# version 01
# baseline (speedup 1.0000x reference)
import math

import numpy as np
import ml_dtypes
import concourse.bass as bass
import concourse.mybir as mybir
from concourse.bass_utils import run_bass_kernel_spmd

F32 = mybir.dt.float32
BF16 = mybir.dt.bfloat16
AF = mybir.ActivationFunctionType
ALU = mybir.AluOpType
AX = mybir.AxisListType

NCORE = 8
T = 1024
D = 4096
NT = 32
TT = 8
H, P, NS, G, KC = 64, 64, 128, 8, 4
I_ = H * P
GN = G * NS
CONV_DIM = I_ + 2 * GN
PROJ = I_ + CONV_DIM + H
CT_PROJ = 81
Q = 128
NCH = T // Q
EPS = 1e-5

bf16 = ml_dtypes.bfloat16


class Prog:
    ENGINES = ("sync", "gpsimd", "tensor", "vector", "scalar")
    SELF_ORDERED = {"tensor": True, "vector": False, "scalar": False,
                    "gpsimd": False, "sync": False}
    SEMS = ("sync", "gpsimd", "tensor", "vector", "scalar", "syncd", "gpsimdd")

    def __init__(self):
        self.items = {e: [] for e in self.ENGINES}
        self.count = {s: 0 for s in self.SEMS}
        self.last_w = {}
        self.readers = {}
        self.pending = {}
        self.known = {e: {s: 0 for s in self.SEMS} for e in self.ENGINES}

    def op(self, eng, fn, reads=(), writes=(), dma=False):
        sem = eng + "d" if dma else eng
        inc = 16 if dma else 1
        need = {}
        def _need(e2, s2, v2):
            if e2 == eng and s2 == eng and self.SELF_ORDERED[eng]:
                return
            need[s2] = max(need.get(s2, 0), v2)
        for r in reads:
            ev = self.last_w.get(r)
            if ev is not None:
                _need(*ev)
        for w in writes:
            ev = self.last_w.get(w)
            if ev is not None:
                _need(*ev)
            for (e2, s2), v2 in self.readers.get(w, {}).items():
                _need(e2, s2, v2)
        for s2, v2 in self.pending.pop(eng, {}).items():
            need[s2] = max(need.get(s2, 0), v2)
        if dma:
            need[sem] = max(need.get(sem, 0), self.count[sem])
        kn = self.known[eng]
        waits = {s2: v2 for s2, v2 in need.items() if v2 > kn[s2]}
        kn.update(waits)
        self.count[sem] += inc
        val = self.count[sem]
        self.items[eng].append((waits, fn, sem, inc))
        for w in writes:
            self.last_w[w] = (eng, sem, val)
            self.readers[w] = {}
        for r in reads:
            d = self.readers.setdefault(r, {})
            k2 = (eng, sem)
            d[k2] = max(d.get(k2, 0), val)
        return (eng, sem, val)

    def barrier(self):
        snapshot = dict(self.count)
        for e in self.ENGINES:
            kn = self.known[e]
            for s in self.SEMS:
                if snapshot[s] > kn[s]:
                    self.pending.setdefault(e, {})[s] = max(
                        self.pending.get(e, {}).get(s, 0), snapshot[s])

    def dma(self, eng, out, in_, reads=(), writes=(), **kw):
        return self.op(eng, lambda E, o=out, i=in_, kw=kw: E.dma_start(out=o, in_=i, **kw),
                       reads=reads, writes=writes, dma=True)

    def finalize(self, nc, block, sems):
        counts = dict(self.count)
        def emit(name):
            def inner(eng):
                for waits, fn, sem, inc in self.items[name]:
                    for s2, v2 in sorted(waits.items()):
                        eng.wait_ge(sems[s2], v2)
                    inst = fn(eng)
                    inst.then_inc(sems[sem], inc)
                if name == "vector":
                    for s2 in self.SEMS:
                        if s2 != "vector" and counts[s2] > 0:
                            eng.wait_ge(sems[s2], counts[s2])
            return inner
        if self.items["sync"]:
            block.sync(emit("sync"))
        block.vector(emit("vector"))
        if self.items["scalar"]:
            block.scalar(emit("scalar"))
        if self.items["tensor"]:
            block.tensor(emit("tensor"))
        if self.items["gpsimd"]:
            block.gpsimd(emit("gpsimd"))


_PREP = {}

def _dft_tables():
    N = 4096
    f = np.arange(N, dtype=np.float64)
    k = np.arange(2049, dtype=np.float64)
    ang = np.outer(f, k) * (2 * np.pi / N)
    Fp = np.empty((N, N), np.float64)
    Fp[:, :2049] = np.cos(ang)
    Fp[:, 2049:] = -np.sin(ang[:, 1:2048])
    ck = np.full(2049, 2.0 / N); ck[0] = 1.0 / N; ck[2048] = 1.0 / N
    d = np.arange(N, dtype=np.float64)
    ang2 = np.outer(k, d) * (2 * np.pi / N)
    Gc = np.zeros((2176, N), np.float64)
    Gc[:2049] = ck[:, None] * np.cos(ang2)
    GI = np.empty((N, N), np.float64)
    GI[:2049] = ck[:, None] * np.cos(ang2)
    GI[2049:] = -(ck[1:2048, None]) * np.sin(ang2[1:2048])
    return Fp, Gc, GI

def _tile_kpm(W):
    Kin, Mout = W.shape
    ct = (Mout + 127) // 128
    nt = Kin // 128
    Wp = np.zeros((Kin, ct * 128), W.dtype)
    Wp[:, :Mout] = W
    return np.ascontiguousarray(
        Wp.reshape(nt, 128, ct, 128).transpose(2, 1, 0, 3))

def _prep(inputs):
    key = id(inputs.get("L0_attn_x"))
    if _PREP.get("key") == key:
        return _PREP
    _PREP.clear()
    _PREP["key"] = key
    f32 = np.float32

    Fp, Gc, GI = _dft_tables()
    _PREP["Fp"] = np.ascontiguousarray(
        Fp.reshape(32, 128, 8, 512).transpose(2, 1, 0, 3)).astype(bf16)
    _PREP["Gcp"] = np.ascontiguousarray(
        Gc.reshape(17, 128, 8, 512).transpose(2, 1, 0, 3)).astype(bf16)
    _PREP["GIp"] = np.ascontiguousarray(
        GI.reshape(32, 128, 32, 128).transpose(2, 1, 0, 3)).astype(bf16)
    _PREP["identb"] = np.eye(128, dtype=bf16)
    _PREP["identf"] = np.eye(128, dtype=f32)
    _PREP["zerosb"] = np.zeros((128, 128), bf16)
    _PREP["onesb"] = np.ones((128, 1), bf16)
    _PREP["onesrow"] = np.ones((1, 128), f32)
    _PREP["onesrowb"] = np.ones((1, 128), bf16)
    mk = np.zeros((128, 128), f32)
    tt_, tau_ = np.meshgrid(np.arange(128), np.arange(128))
    mk[tau_ > tt_] = -1e9
    _PREP["maskc"] = mk
    selp = np.zeros((16, 8, 128), bf16)
    for pr_ in range(8):
        selp[2*pr_, pr_, 0:64] = bf16(1.0)
        selp[2*pr_+1, pr_, 64:128] = bf16(1.0)
    _PREP["selp"] = selp
    selE = np.zeros((4, 64), bf16)
    for e in range(4):
        selE[e, 16*e:16*e+16] = bf16(4.0)
    _PREP["selE"] = selE

    A0 = inputs["L0_ffn_A"].astype(f32)
    a0t = np.ascontiguousarray(A0.transpose(2, 0, 1).reshape(4096, 64))
    _PREP["a0p"] = np.ascontiguousarray(
        a0t.reshape(32, 128, 64).transpose(1, 0, 2))
    B0 = inputs["L0_ffn_B"].astype(f32)
    b0p = np.zeros((64, 32, 128), f32)
    for pair in range(2):
        for ep in range(2):
            e = 2 * pair + ep
            blk = B0[e].reshape(32, 128, 16).transpose(0, 2, 1)
            b0p[32*pair+16*ep:32*pair+16*ep+16] = blk.transpose(1, 0, 2)
    _PREP["b0p"] = np.ascontiguousarray(b0p)
    Wr0 = inputs["L0_ffn_Wr"].astype(f32)
    _PREP["wr0p"] = np.ascontiguousarray(
        Wr0.T.reshape(32, 128, 4).transpose(1, 0, 2))

    _PREP["wtip"] = _tile_kpm(np.ascontiguousarray(
        inputs["L1_attn_in_proj_w"].astype(f32).T))
    _PREP["wtop"] = _tile_kpm(np.ascontiguousarray(
        inputs["L1_attn_out_proj_w"].astype(f32).T))
    _PREP["convw"] = inputs["L1_attn_conv_weight"].astype(f32).reshape(48, 128, 4)
    _PREP["convb"] = inputs["L1_attn_conv_bias"].astype(f32).reshape(48, 128, 1)
    _PREP["dtb"] = inputs["L1_attn_dt_bias"].astype(f32).reshape(64, 1)
    _PREP["alog"] = inputs["L1_attn_A_log"].astype(f32).reshape(64, 1)
    _PREP["drep"] = np.ascontiguousarray(
        np.repeat(inputs["L1_attn_D"].astype(f32), 64).reshape(32, 128).T)
    _PREP["nwp"] = np.ascontiguousarray(
        inputs["L1_attn_norm_weight"].astype(f32).reshape(32, 128).T)

    _PREP["wtbp"] = _tile_kpm(np.ascontiguousarray(
        inputs["L1_ffn_base_w"].astype(f32).T))
    _PREP["bbp"] = np.ascontiguousarray(
        inputs["L1_ffn_base_b"].astype(f32).reshape(32, 128).T)
    A1 = inputs["L1_ffn_A"].astype(f32)
    _PREP["a1p"] = np.ascontiguousarray(A1.T.reshape(32, 128, 64).transpose(1, 0, 2))
    B1 = inputs["L1_ffn_B"].astype(f32)
    b1p = np.zeros((8, 32, 128, 128), f32)
    for pair in range(8):
        for ep in range(2):
            e = 2 * pair + ep
            blk = B1[e].reshape(32, 128, 64).transpose(0, 2, 1)
            b1p[pair, :, 64*ep:64*ep+64, :] = blk
    _PREP["b1p"] = np.ascontiguousarray(b1p.transpose(1, 2, 0, 3))
    _PREP["r1p"] = np.ascontiguousarray(
        inputs["L1_ffn_router_w"].astype(f32).T.reshape(32, 128, 16).transpose(1, 0, 2))

    x = inputs["L0_attn_x"].astype(f32)
    xfp = []
    for c in range(NCORE):
        b, q = c // 4, c % 4
        xs = np.ascontiguousarray(x[b, q*1024:(q+1)*1024, :].T)
        xfp.append(np.ascontiguousarray(
            xs.reshape(32, 128, 1024).transpose(1, 0, 2)))
    _PREP["xfp"] = xfp
    return _PREP


_L1_CACHE = {}

class _Done(Exception):
    pass


def build_l0(debug=False):
    import os
    lim = int(os.environ.get("L0_PHASES", "99"))
    if ("l0", debug, lim) in _L1_CACHE:
        return _L1_CACHE[("l0", debug, lim)]
    nc = bass.Bass("TRN2", target_bir_lowering=False)
    pr = Prog()

    def din(name, shape, dt=F32):
        return nc.dram_tensor(name, list(shape), dt, kind="ExternalInput").ap()
    def dout(name, shape, dt=F32):
        return nc.dram_tensor(name, list(shape), dt, kind="ExternalOutput").ap()
    def dint(name, shape, dt):
        return nc.dram_tensor(name, list(shape), dt, kind="Internal").ap()

    xfp = din("xfp", (128, 32, 1024))
    Fp = din("Fp", (8, 128, 32, 512), BF16)
    Gcp = din("Gcp", (8, 128, 17, 512), BF16)
    GIp = din("GIp", (32, 128, 32, 128), BF16)
    identb = din("identb", (128, 128), BF16)
    zerosb = din("zerosb", (128, 128), BF16)
    onesb = din("onesb", (128, 1), BF16)
    onesrow = din("onesrow", (1, 128))
    selE = din("selE", (4, 64), BF16)
    a0p = din("a0p", (128, 32, 64))
    b0p = din("b0p", (64, 32, 128))
    wr0p = din("wr0p", (128, 32, 4))

    x2t = dout("x2t", (32, 128, 1024))
    if debug:
        dbg_x1 = dout("dbg_x1", (32, 128, 1024))
        dbg_corr = dout("dbg_corr", (8, 128, 4096))
        dbg_s = dout("dbg_s", (8, 128, 4096))

    Xd = dint("Xd", (8, 128, 4096), BF16)
    Sd = dint("Sd", (8, 128, 4096), BF16)
    r0d = dint("r0d", (32, 128, 1024), F32)
    x1d = dint("x1d", (32, 128, 1024), F32)

    from contextlib import ExitStack
    try:
      with ExitStack() as ctx:
        E = ctx.enter_context
        ar1 = E(nc.sbuf_tensor("ar1", [128, 32 * 1024], BF16))
        ar2 = E(nc.sbuf_tensor("ar2", [128, 32 * 1024], BF16))
        ar3f = E(nc.sbuf_tensor("ar3f", [128, 9280], F32))
        ar3b = ar3f[:].bitcast(BF16)
        xbf = ar1[:]
        corr = ar1[:]
        pT = ar2[:, 0:17408]
        sT = ar2[:]
        x1bf = ar2[:]
        fbuf = [ar3b[:, 0:16384], ar3b[:, 0:16384]]
        gcb = [ar3b[:, 0:8704], ar3b[:, 0:8704]]
        gib = [ar3b[:, 0:4096], ar3b[:, 4096:8192]]
        pfa = ar3f[:, 0:2049]
        pfb = ar3f[:, 2049:4096]
        xt_B1 = ar3b[:, 8192:12288]
        pbf = ar3b[:, 12288:14464]
        e_f = ar3f[:, 0:4096]
        mask_f = ar3f[:, 4096:8192]
        xt_D = ar3b[:, 0:4096]
        st_ = ar3b[:, 4096:8192]
        pfaD = ar3f[:, 4096:6145]
        pfbD = ar3f[:, 6145:8192]
        a0sb = ar3b[:, 0:2048]
        b0sb = ar3b[0:64, 2048:6144]
        wr0sb = ar3b[:, 6144:6272]
        gT = ar3b[0:4, 6272:7296]
        mid_f = ar3f[0:64, 3648:4672]
        midg = ar3b[0:64, 9344:11392]
        evb = [E(nc.sbuf_tensor(f"evb{i}", [128, 512], BF16)) for i in range(4)]
        m8b = E(nc.sbuf_tensor("m8b", [128, 8], BF16))
        m8f = E(nc.sbuf_tensor("m8f", [128, 8], F32))
        e8f = E(nc.sbuf_tensor("e8f", [128, 8], F32))
        negm1 = E(nc.sbuf_tensor("negm1", [128, 1], F32))
        theta = E(nc.sbuf_tensor("theta", [128, 1], F32))
        zrow = E(nc.sbuf_tensor("zrow", [128, 1], F32))
        rz = E(nc.sbuf_tensor("rz", [128, 1], F32))
        ones_b = E(nc.sbuf_tensor("ones_b", [128, 1], BF16))
        ones_r = E(nc.sbuf_tensor("ones_r", [1, 128], F32))
        ident_b = E(nc.sbuf_tensor("ident_b", [128, 128], BF16))
        xfstr = E(nc.sbuf_tensor("xfstr", [128, 1024], F32))
        r0bf = E(nc.sbuf_tensor("r0bf", [128, 1024], BF16))
        sqbf = E(nc.sbuf_tensor("sqbf", [128, 1024], BF16))
        r0t = E(nc.sbuf_tensor("r0t", [128, 1024], F32))
        x1t = E(nc.sbuf_tensor("x1t", [128, 1024], F32))
        mrow = E(nc.sbuf_tensor("mrow", [1, 1024], F32))
        vrow = E(nc.sbuf_tensor("vrow", [1, 1024], F32))
        rsrow = E(nc.sbuf_tensor("rsrow", [1, 1024], F32))
        sel_sb = E(nc.sbuf_tensor("sel_sb", [4, 64], BF16))
        logit = E(nc.sbuf_tensor("logit", [128, 8 * 4], F32))
        gwork = E(nc.sbuf_tensor("gwork", [128, 4], F32))
        gmax = E(nc.sbuf_tensor("gmax", [128, 1], F32))
        gmax2 = E(nc.sbuf_tensor("gmax2", [128, 1], F32))
        gz = E(nc.sbuf_tensor("gz", [128, 1], F32))
        g4 = E(nc.sbuf_tensor("g4", [128, 8 * 4], F32))
        g4b = E(nc.sbuf_tensor("g4b", [128, 4], BF16))
        eps1 = E(nc.sbuf_tensor("eps1", [1, 1], F32))

        ps = [E(nc.psum_tensor(f"ps{i}", [128, 512], F32)) for i in range(4)]
        psTb = [ps[2][:].bitcast(BF16)[:, 0:128], ps[3][:].bitcast(BF16)[:, 0:128]]
        psS1 = E(nc.psum_tensor("psS1", [1, 1024], F32))
        psS2 = E(nc.psum_tensor("psS2", [1, 1024], F32))

        sems = {}
        for e in Prog.SEMS:
            sems[e] = E(nc.semaphore(f"sem_{e}"))
        block = E(nc.Block())

        def ckpt(k):
            if lim <= k:
                pr.finalize(nc, block, sems)
                raise _Done

        pr.op("vector", lambda E_: E_.memset(eps1[:], EPS), writes=["eps1"])
        pr.dma("sync", ones_b[:], onesb, writes=["ones_b"])
        pr.dma("sync", ones_r[:], onesrow, writes=["ones_r"])
        pr.dma("sync", ident_b[:], identb, writes=["ident_b"])
        pr.dma("sync", sel_sb[:], selE, writes=["sel_sb"])
        pr.dma("gpsimd", xbf[:], xfp.rearrange("p ft t -> p (ft t)"), writes=["xbf"])

        for kc in range(8):
            fb = fbuf[kc % 2]
            pr.dma("sync", fb[:], Fp[kc].rearrange("p ft k -> p (ft k)"),
                   reads=[], writes=["fbuf"])
            for half in range(2):
                for tt4 in range(4):
                    tt = half * 4 + tt4
                    pk = ps[tt4]
                    for ft in range(NT):
                        pr.op("tensor",
                              lambda E_, tt=tt, ft=ft, pk=pk, fb=fb, kc=kc:
                              E_.matmul(pk[:], xbf[:, ft*1024 + tt*128: ft*1024 + tt*128 + 128],
                                        fb[:, ft*512: ft*512 + 512],
                                        start=(ft == 0), stop=(ft == NT - 1)),
                              reads=["xbf", "fbuf"], writes=[f"ps{tt4}"])
                    ev = evb[tt4]
                    pr.op("scalar", lambda E_, pk=pk, ev=ev: E_.copy(ev[:], pk[:]),
                          reads=[f"ps{tt4}"], writes=[f"evb{tt4}"])
                    pr.dma("gpsimd", Xd[tt][:, kc*512:(kc+1)*512], ev[:],
                           reads=[f"evb{tt4}"], writes=[f"Xd{tt}{kc}"])

        ckpt(1)
        pr.barrier()
        pr.dma("sync", pbf[:, 2048:2176], zerosb[:, 0:128], writes=["pbf_pad"])
        for tt in range(8):
            pr.dma("sync", xt_B1[:], Xd[tt],
                   reads=[f"Xd{tt}{k}" for k in range(8)], writes=["xt_"])
            pr.op("scalar", lambda E_: E_.activation(pfa[:], xt_B1[:, 0:2049], AF.Square),
                  reads=["xt_"], writes=["pfa"])
            pr.op("scalar", lambda E_: E_.activation(pfb[:], xt_B1[:, 2049:4096], AF.Square),
                  reads=["xt_"], writes=["pfb"])
            pr.op("vector", lambda E_: E_.tensor_copy(pbf[:, 0:1], pfa[:, 0:1]),
                  reads=["pfa"], writes=["pbf_c0"])
            pr.op("vector", lambda E_: E_.tensor_copy(pbf[:, 2048:2049], pfa[:, 2048:2049]),
                  reads=["pfa"], writes=["pbf_c2048"])
            pr.op("vector", lambda E_: E_.tensor_tensor(pbf[:, 1:2048], pfa[:, 1:2048],
                                                        pfb[:], ALU.add),
                  reads=["pfa", "pfb"], writes=["pbf"])
            for kt in range(17):
                pb = psTb[kt % 2]
                pr.op("tensor", lambda E_, kt=kt, pb=pb:
                      E_.transpose(pb[:], pbf[:, kt*128:(kt+1)*128], ident_b[:]),
                      reads=["pbf", "pbf_c0", "pbf_c2048", "pbf_pad", "ident_b"], writes=[f"ps{2+kt%2}"])
                pr.op("scalar", lambda E_, tt=tt, kt=kt, pb=pb:
                      E_.copy(pT[:, tt*2176 + kt*128: tt*2176 + kt*128 + 128], pb[:]),
                      reads=[f"ps{2+kt%2}"], writes=[f"pT{tt}"])

        ckpt(2)
        pr.barrier()
        for dc in range(8):
            gb = gcb[dc % 2]
            pr.dma("sync", gb[:], Gcp[dc].rearrange("p kt k -> p (kt k)"),
                   writes=["gcb"])
            for half in range(2):
                for tt4 in range(4):
                    tt = half * 4 + tt4
                    pk = ps[tt4]
                    for kt in range(17):
                        pr.op("tensor", lambda E_, tt=tt, kt=kt, pk=pk, gb=gb:
                              E_.matmul(pk[:], pT[:, tt*2176 + kt*128: tt*2176 + kt*128 + 128],
                                        gb[:, kt*512:(kt+1)*512],
                                        start=(kt == 0), stop=(kt == 16)),
                              reads=[f"pT{tt}", "gcb"], writes=[f"ps{tt4}"])
                    pr.op("scalar", lambda E_, tt=tt, dc=dc, pk=pk:
                          E_.copy(corr[:, tt*4096 + dc*512: tt*4096 + dc*512 + 512], pk[:]),
                          reads=[f"ps{tt4}"], writes=[f"corr{tt}"])

        ckpt(3)
        pr.barrier()
        for tt in range(8):
            c_sl = corr[:, tt*4096:(tt+1)*4096]
            if debug:
                pr.dma("gpsimd", dbg_corr[tt], c_sl, reads=[f"corr{tt}"],
                       writes=[f"dbgc{tt}"])
            pr.op("vector", lambda E_, c_sl=c_sl: E_.max(out=m8b[:], in_=c_sl),
                  reads=[f"corr{tt}"], writes=["m8b"])
            pr.op("vector", lambda E_: E_.tensor_copy(m8f[:], m8b[:]),
                  reads=["m8b"], writes=["m8f"])
            pr.op("vector", lambda E_: E_.tensor_scalar(negm1[:], m8f[:, 0:1], -1.0, None,
                                                        op0=ALU.mult),
                  reads=["m8f"], writes=["negm1"])
            pr.op("vector", lambda E_: E_.tensor_copy(theta[:], m8f[:, 7:8]),
                  reads=["m8f"], writes=["theta"])
            pr.op("scalar", lambda E_: E_.activation(e8f[:], m8f[:], AF.Exp,
                                                     bias=negm1[:], accum_out=zrow[:]),
                  reads=["m8f", "negm1"], writes=["e8f", "zrow"])
            pr.op("vector", lambda E_: E_.reciprocal(rz[:], zrow[:]),
                  reads=["zrow"], writes=["rz"])
            pr.op("scalar", lambda E_, c_sl=c_sl: E_.activation(e_f[:], c_sl, AF.Exp,
                                                                bias=negm1[:]),
                  reads=[f"corr{tt}", "negm1"], writes=["e_f"])
            pr.op("vector", lambda E_, c_sl=c_sl: E_.tensor_scalar(
                mask_f[:], c_sl, theta[:], None, op0=ALU.is_ge),
                  reads=[f"corr{tt}", "theta"], writes=["mask_f"])
            pr.op("vector", lambda E_, c_sl=c_sl: E_.scalar_tensor_tensor(
                c_sl, e_f[:], rz[:], mask_f[:], op0=ALU.mult, op1=ALU.mult),
                  reads=["e_f", "rz", "mask_f"], writes=[f"corr{tt}"])
            if debug:
                pr.dma("gpsimd", dbg_s[tt], c_sl, reads=[f"corr{tt}"],
                       writes=[f"dbgs{tt}"])
            for ft in range(NT):
                pb = psTb[ft % 2]
                pr.op("tensor", lambda E_, ft=ft, pb=pb, c_sl=c_sl:
                      E_.transpose(pb[:], c_sl[:, ft*128:(ft+1)*128], ident_b[:]),
                      reads=[f"corr{tt}", "ident_b"], writes=[f"ps{2+ft%2}"])
                pr.op("scalar", lambda E_, tt=tt, ft=ft, pb=pb:
                      E_.copy(sT[:, tt*4096 + ft*128: tt*4096 + ft*128 + 128], pb[:]),
                      reads=[f"ps{2+ft%2}"], writes=[f"sT{tt}", "sTbuf"])

        ckpt(4)
        pr.barrier()
        for kc in range(8):
            fb = fbuf[kc % 2]
            pr.dma("sync", fb[:], Fp[kc].rearrange("p ft k -> p (ft k)"),
                   writes=["fbuf"])
            for half in range(2):
                for tt4 in range(4):
                    tt = half * 4 + tt4
                    pk = ps[tt4]
                    for ft in range(NT):
                        pr.op("tensor", lambda E_, tt=tt, ft=ft, pk=pk, fb=fb:
                              E_.matmul(pk[:], sT[:, tt*4096 + ft*128: tt*4096 + ft*128 + 128],
                                        fb[:, ft*512: ft*512 + 512],
                                        start=(ft == 0), stop=(ft == NT - 1)),
                              reads=[f"sT{tt}", "sTbuf", "fbuf"], writes=[f"ps{tt4}"])
                    ev = evb[tt4]
                    pr.op("scalar", lambda E_, pk=pk, ev=ev: E_.copy(ev[:], pk[:]),
                          reads=[f"ps{tt4}"], writes=[f"evb{tt4}"])
                    pr.dma("gpsimd", Sd[tt][:, kc*512:(kc+1)*512], ev[:],
                           reads=[f"evb{tt4}"], writes=[f"Sd{tt}{kc}"])

        ckpt(5)
        pr.barrier()
        for tt in range(8):
            y_sl = corr[:, tt*4096:(tt+1)*4096]
            pr.dma("sync", xt_D[:], Xd[tt],
                   reads=[f"Xd{tt}{k}" for k in range(8)], writes=["xt_"])
            pr.dma("sync", st_[:], Sd[tt],
                   reads=[f"Sd{tt}{k}" for k in range(8)], writes=["st_"])
            pr.op("vector", lambda E_, y=y_sl: E_.tensor_tensor(
                y[:, 0:2049], xt_D[:, 0:2049], st_[:, 0:2049], ALU.mult),
                  reads=["xt_", "st_"], writes=[f"corr{tt}"])
            pr.op("vector", lambda E_: E_.tensor_tensor(
                pfbD[:], xt_D[:, 2049:4096], st_[:, 2049:4096], ALU.mult),
                  reads=["xt_", "st_"], writes=["pfb"])
            pr.op("vector", lambda E_, y=y_sl: E_.tensor_tensor(
                y[:, 1:2048], y[:, 1:2048], pfbD[:], ALU.add),
                  reads=["pfb"], writes=[f"corr{tt}"])
            pr.op("vector", lambda E_: E_.tensor_tensor(
                pfaD[:, 0:2047], xt_D[:, 2049:4096], st_[:, 1:2048], ALU.mult),
                  reads=["xt_", "st_"], writes=["pfa"])
            pr.op("vector", lambda E_: E_.tensor_tensor(
                pfbD[:], xt_D[:, 1:2048], st_[:, 2049:4096], ALU.mult),
                  reads=["xt_", "st_"], writes=["pfb"])
            pr.op("vector", lambda E_, y=y_sl: E_.tensor_tensor(
                y[:, 2049:4096], pfaD[:, 0:2047], pfbD[:], ALU.subtract),
                  reads=["pfa", "pfb"], writes=[f"corr{tt}"])
            for st in range(NT):
                pb = psTb[st % 2]
                pr.op("tensor", lambda E_, st=st, pb=pb, y=y_sl:
                      E_.transpose(pb[:], y[:, st*128:(st+1)*128], ident_b[:]),
                      reads=[f"corr{tt}", "ident_b"], writes=[f"ps{2+st%2}"])
                pr.op("scalar", lambda E_, tt=tt, st=st, pb=pb:
                      E_.copy(sT[:, st*1024 + tt*128: st*1024 + tt*128 + 128], pb[:]),
                      reads=[f"ps{2+st%2}"], writes=[f"yT{st}", "sTbuf"])

        ckpt(6)
        pr.barrier()
        for dt_ in range(NT):
            gb = gib[dt_ % 2]
            pr.dma("sync", gb[:], GIp[dt_].rearrange("p st j -> p (st j)"),
                   writes=[f"gib{dt_%2}"])
            pr.dma("sync", xfstr[:], xfp[:, dt_, :], writes=["xfstr"])
            for nch in range(2):
                pk = ps[nch]
                for st in range(NT):
                    pr.op("tensor", lambda E_, st=st, nch=nch, pk=pk, gb=gb:
                          E_.matmul(pk[:], gb[:, st*128:(st+1)*128],
                                    sT[:, st*1024 + nch*512: st*1024 + nch*512 + 512],
                                    start=(st == 0), stop=(st == NT - 1)),
                          reads=[f"gib{dt_%2}", f"yT{st}", "sTbuf"],
                          writes=[f"ps{nch}"])
                pr.op("vector", lambda E_, nch=nch, pk=pk: E_.tensor_tensor(
                    r0t[:, nch*512:(nch+1)*512], pk[:], xfstr[:, nch*512:(nch+1)*512],
                    ALU.add),
                      reads=[f"ps{nch}", "xfstr"], writes=["r0t"])
            pr.dma("gpsimd", r0d[dt_], r0t[:], reads=["r0t"], writes=[f"r0d{dt_}"])
            pr.op("vector", lambda E_: E_.tensor_copy(r0bf[:], r0t[:]),
                  reads=["r0t"], writes=["r0bf"])
            pr.op("scalar", lambda E_: E_.activation(sqbf[:], r0t[:], AF.Square),
                  reads=["r0t"], writes=["sqbf"])
            for nch in range(2):
                pr.op("tensor", lambda E_, nch=nch, dt_=dt_:
                      E_.matmul(psS1[0:1, nch*512:(nch+1)*512], ones_b[:],
                                r0bf[:, nch*512:(nch+1)*512],
                                start=(dt_ == 0), stop=(dt_ == NT - 1)),
                      reads=["r0bf", "ones_b"], writes=["psS1"])
                pr.op("tensor", lambda E_, nch=nch, dt_=dt_:
                      E_.matmul(psS2[0:1, nch*512:(nch+1)*512], ones_b[:],
                                sqbf[:, nch*512:(nch+1)*512],
                                start=(dt_ == 0), stop=(dt_ == NT - 1)),
                      reads=["sqbf", "ones_b"], writes=["psS2"])

        def ln_rows(tag):
            pr.op("vector", lambda E_: E_.tensor_scalar(
                mrow[:], psS1[:], 1.0 / 4096.0, None, op0=ALU.mult),
                  reads=["psS1"], writes=["mrow", "srowH"])
            pr.op("scalar", lambda E_: E_.activation(vrow[:], mrow[:], AF.Square),
                  reads=["mrow"], writes=["vrow", "srowH"])
            pr.op("vector", lambda E_: E_.scalar_tensor_tensor(
                vrow[:], psS2[:], 1.0 / 4096.0, vrow[:], op0=ALU.mult, op1=ALU.subtract),
                  reads=["psS2", "vrow"], writes=["vrow", "srowH"])
            pr.op("scalar", lambda E_: E_.activation(vrow[:], vrow[:], AF.Ln, bias=eps1[:]),
                  reads=["vrow", "eps1"], writes=["vrow", "srowH"])
            pr.op("scalar", lambda E_: E_.activation(rsrow[:], vrow[:], AF.Exp, scale=-0.5),
                  reads=["vrow"], writes=["rsrow", "srowH"])

        def ln_bcast():
            for nch in range(2):
                pr.op("tensor", lambda E_, nch=nch: E_.matmul(
                    ps[nch][:], ones_r[0:1, :], mrow[0:1, nch*512:(nch+1)*512],
                    start=True, stop=True),
                      reads=["mrow", "ones_r"], writes=[f"ps{nch}"])
                pr.op("tensor", lambda E_, nch=nch: E_.matmul(
                    ps[2+nch][:], ones_r[0:1, :], rsrow[0:1, nch*512:(nch+1)*512],
                    start=True, stop=True),
                      reads=["rsrow", "ones_r"], writes=[f"ps{2+nch}"])

        ln_rows("ln1")
        ln_bcast()
        ckpt(7)
        pr.barrier()
        for dt_ in range(NT):
            pr.dma("sync", r0t[:], r0d[dt_], reads=[f"r0d{dt_}"], writes=["r0t"])
            for nch in range(2):
                pr.op("vector", lambda E_, nch=nch: E_.tensor_tensor(
                    x1t[:, nch*512:(nch+1)*512], r0t[:, nch*512:(nch+1)*512],
                    ps[nch][:], ALU.subtract),
                      reads=["r0t", f"ps{nch}"], writes=["x1t"])
                pr.op("vector", lambda E_, nch=nch: E_.tensor_tensor(
                    x1t[:, nch*512:(nch+1)*512], x1t[:, nch*512:(nch+1)*512],
                    ps[2+nch][:], ALU.mult),
                      reads=[f"ps{2+nch}"], writes=["x1t"])
            pr.dma("gpsimd", x1d[dt_], x1t[:], reads=["x1t"], writes=[f"x1d{dt_}"])
            if debug:
                pr.dma("gpsimd", dbg_x1[dt_], x1t[:], reads=["x1t"], writes=[f"dbgx1{dt_}"])
            pr.op("vector", lambda E_, dt_=dt_: E_.tensor_copy(
                x1bf[:, dt_*1024:(dt_+1)*1024], x1t[:]),
                  reads=["x1t"], writes=[f"x1bf{dt_}"])

        ckpt(8)
        pr.barrier()
        pr.dma("gpsimd", a0sb[:], a0p.rearrange("p ft j -> p (ft j)"), writes=["a0sb"])
        pr.dma("gpsimd", b0sb[:], b0p.rearrange("r ot j -> r (ot j)"), writes=["b0sb"])
        pr.dma("gpsimd", wr0sb[:], wr0p.rearrange("p ft j -> p (ft j)"), writes=["wr0sb"])
        for tt in range(8):
            for ft in range(NT):
                pr.op("tensor", lambda E_, tt=tt, ft=ft: E_.matmul(
                    ps[0][:, 0:4], x1bf[:, ft*1024 + tt*128: ft*1024 + tt*128 + 128],
                    wr0sb[:, ft*4:(ft+1)*4], start=(ft == 0), stop=(ft == NT - 1)),
                      reads=[f"x1bf{ft}", "wr0sb"], writes=["ps0"])
            pr.op("scalar", lambda E_, tt=tt: E_.copy(logit[:, tt*4:(tt+1)*4],
                                                      ps[0][:, 0:4]),
                  reads=["ps0"], writes=["logit"])
        ckpt(9)
        for tt in range(8):
            lg = logit[:, tt*4:(tt+1)*4]
            pr.op("vector", lambda E_, lg=lg: E_.tensor_reduce(
                gmax[:], lg, AX.X, ALU.max), reads=["logit"], writes=["gmax"])
            pr.op("vector", lambda E_, lg=lg: E_.tensor_scalar(
                gwork[:], lg, gmax[:], -1e30, op0=ALU.is_ge, op1=ALU.mult),
                  reads=["logit", "gmax"], writes=["gwork"])
            pr.op("vector", lambda E_, lg=lg: E_.tensor_tensor(
                gwork[:], gwork[:], lg, ALU.add), reads=["logit"], writes=["gwork"])
            pr.op("vector", lambda E_: E_.tensor_reduce(
                gmax2[:], gwork[:], AX.X, ALU.max), reads=["gwork"], writes=["gmax2"])
            pr.op("vector", lambda E_: E_.tensor_scalar(
                negm1[:], gmax[:], -1.0, None, op0=ALU.mult),
                  reads=["gmax"], writes=["negm1"])
            pr.op("scalar", lambda E_, lg=lg, tt=tt: E_.activation(
                g4[:, tt*4:(tt+1)*4], lg, AF.Exp, bias=negm1[:]),
                  reads=["logit", "negm1"], writes=["g4"])
            pr.op("vector", lambda E_, lg=lg: E_.tensor_scalar(
                gwork[:], lg, gmax2[:], None, op0=ALU.is_ge),
                  reads=["logit", "gmax2"], writes=["gwork"])
            pr.op("vector", lambda E_, tt=tt: E_.tensor_tensor(
                g4[:, tt*4:(tt+1)*4], g4[:, tt*4:(tt+1)*4], gwork[:], ALU.mult),
                  reads=["gwork"], writes=["g4"])
            pr.op("vector", lambda E_, tt=tt: E_.tensor_reduce(
                gz[:], g4[:, tt*4:(tt+1)*4], AX.X, ALU.add),
                  reads=["g4"], writes=["gz"])
            pr.op("vector", lambda E_: E_.reciprocal(rz[:], gz[:]),
                  reads=["gz"], writes=["rz"])
            pr.op("vector", lambda E_, tt=tt: E_.tensor_scalar(
                g4[:, tt*4:(tt+1)*4], g4[:, tt*4:(tt+1)*4], rz[:], None, op0=ALU.mult),
                  reads=["rz"], writes=["g4"])
            pr.op("vector", lambda E_, tt=tt: E_.tensor_copy(
                g4b[:], g4[:, tt*4:(tt+1)*4]), reads=["g4"], writes=["g4b"])
            pr.op("tensor", lambda E_: E_.transpose(
                psTb[0][0:4, 0:128], g4b[:], ident_b[:]),
                  reads=["g4b", "ident_b"], writes=["ps2"])
            pr.op("scalar", lambda E_, tt=tt: E_.copy(
                gT[0:4, tt*128:(tt+1)*128], psTb[0][0:4, 0:128]),
                  reads=["ps2"], writes=["gT"])
        ckpt(10)
        for nch in range(2):
            pr.op("tensor", lambda E_, nch=nch: E_.matmul(
                ps[2][0:64, :], sel_sb[:], gT[:, nch*512:(nch+1)*512],
                start=True, stop=True),
                  reads=["sel_sb", "gT"], writes=["ps2"])
            for ft in range(NT):
                pr.op("tensor", lambda E_, ft=ft, nch=nch: E_.matmul(
                    ps[3][0:64, :], a0sb[:, ft*64:(ft+1)*64],
                    x1bf[:, ft*1024 + nch*512: ft*1024 + nch*512 + 512],
                    start=(ft == 0), stop=(ft == NT - 1)),
                      reads=["a0sb", f"x1bf{ft}"], writes=["ps3"])
            pr.op("scalar", lambda E_, nch=nch: E_.copy(
                mid_f[:, nch*512:(nch+1)*512], ps[3][0:64, :]),
                  reads=["ps3"], writes=["mid_f"])
            pr.op("vector", lambda E_, nch=nch: E_.tensor_tensor(
                midg[:, nch*512:(nch+1)*512], mid_f[:, nch*512:(nch+1)*512],
                ps[2][0:64, :], ALU.mult),
                  reads=["mid_f", "ps2"], writes=["midg"])

        ckpt(11)
        for ot in range(NT):
            pr.dma("sync", xfstr[:], x1d[ot], reads=[f"x1d{ot}"], writes=["xfstr"])
            for nch in range(2):
                pk = ps[nch]
                pr.op("tensor", lambda E_, ot=ot, nch=nch, pk=pk:
                      E_.matmul(pk[:], b0sb[0:64, ot*128:(ot+1)*128],
                                midg[0:64, nch*512:(nch+1)*512],
                                start=True, stop=True),
                      reads=["b0sb", "midg"], writes=[f"ps{nch}"])
                pr.op("vector", lambda E_, nch=nch, pk=pk: E_.tensor_tensor(
                    r0t[:, nch*512:(nch+1)*512], pk[:], xfstr[:, nch*512:(nch+1)*512],
                    ALU.add),
                      reads=[f"ps{nch}", "xfstr"], writes=["r0t"])
            pr.dma("gpsimd", r0d[ot], r0t[:], reads=["r0t"], writes=[f"r0d{ot}"])
            pr.op("vector", lambda E_: E_.tensor_copy(r0bf[:], r0t[:]),
                  reads=["r0t"], writes=["r0bf"])
            pr.op("scalar", lambda E_: E_.activation(sqbf[:], r0t[:], AF.Square),
                  reads=["r0t"], writes=["sqbf"])
            for nch in range(2):
                pr.op("tensor", lambda E_, nch=nch, ot=ot: E_.matmul(
                    psS1[0:1, nch*512:(nch+1)*512], ones_b[:],
                    r0bf[:, nch*512:(nch+1)*512], start=(ot == 0), stop=(ot == NT - 1)),
                      reads=["r0bf", "ones_b"], writes=["psS1"])
                pr.op("tensor", lambda E_, nch=nch, ot=ot: E_.matmul(
                    psS2[0:1, nch*512:(nch+1)*512], ones_b[:],
                    sqbf[:, nch*512:(nch+1)*512], start=(ot == 0), stop=(ot == NT - 1)),
                      reads=["sqbf", "ones_b"], writes=["psS2"])
        ckpt(12)
        ln_rows("ln2")
        ln_bcast()
        for dt_ in range(NT):
            pr.dma("sync", r0t[:], r0d[dt_], reads=[f"r0d{dt_}"], writes=["r0t"])
            for nch in range(2):
                pr.op("vector", lambda E_, nch=nch: E_.tensor_tensor(
                    x1t[:, nch*512:(nch+1)*512], r0t[:, nch*512:(nch+1)*512],
                    ps[nch][:], ALU.subtract),
                      reads=["r0t", f"ps{nch}"], writes=["x1t"])
                pr.op("vector", lambda E_, nch=nch: E_.tensor_tensor(
                    x1t[:, nch*512:(nch+1)*512], x1t[:, nch*512:(nch+1)*512],
                    ps[2+nch][:], ALU.mult),
                      reads=[f"ps{2+nch}"], writes=["x1t"])
            pr.dma("gpsimd", x2t[dt_], x1t[:], reads=["x1t"], writes=[f"x2t{dt_}"])

        pr.finalize(nc, block, sems)
    except _Done:
        pass
    _L1_CACHE[("l0", debug, lim)] = nc
    return nc


def build_inproj():
    if "inproj" in _L1_CACHE:
        return _L1_CACHE["inproj"]
    nc = bass.Bass("TRN2", target_bir_lowering=False)
    pr = Prog()
    def din(name, shape, dt=F32):
        return nc.dram_tensor(name, list(shape), dt, kind="ExternalInput").ap()
    def dout(name, shape, dt=F32):
        return nc.dram_tensor(name, list(shape), dt, kind="ExternalOutput").ap()
    x2t = din("x2t", (32, 128, 1024))
    wtip = din("wtip", (81, 128, 32, 128))
    gate = dout("gate", (32, 128, 1024))
    hbc = dout("hbc", (48, 128, 1024))
    dt_o = dout("dt_o", (64, 1024))
    from contextlib import ExitStack
    with ExitStack() as ctx:
        E = ctx.enter_context
        x2bf = E(nc.sbuf_tensor("x2bf", [128, 32 * 1024], BF16))
        wst = [E(nc.sbuf_tensor(f"wst{i}", [128, 32 * 128], BF16)) for i in range(2)]
        ev = [E(nc.sbuf_tensor(f"ev{i}", [128, 1024], F32)) for i in range(2)]
        ps = [E(nc.psum_tensor(f"ps{i}", [128, 512], F32)) for i in range(4)]
        sems = {e: E(nc.semaphore(f"sem_{e}")) for e in Prog.SEMS}
        block = E(nc.Block())
        for ft in range(NT):
            pr.dma("gpsimd", x2bf[:, ft*1024:(ft+1)*1024], x2t[ft], writes=[f"x2bf{ft}"])
        XR = [f"x2bf{ft}" for ft in range(NT)]
        for ct in range(CT_PROJ):
            w = wst[ct % 2]
            pr.dma("gpsimd", w[:], wtip[ct].rearrange("p ft j -> p (ft j)"),
                   writes=[f"wst{ct%2}"])
            ee = ev[ct % 2]
            for nch in range(2):
                pk = ps[(ct % 2) * 2 + nch]
                for ft in range(NT):
                    pr.op("tensor", lambda E_, ft=ft, nch=nch, pk=pk, w=w:
                          E_.matmul(pk[:], w[:, ft*128:(ft+1)*128],
                                    x2bf[:, ft*1024 + nch*512: ft*1024 + nch*512 + 512],
                                    start=(ft == 0), stop=(ft == NT - 1)),
                          reads=[f"wst{ct%2}"] + XR, writes=[f"ps{(ct%2)*2+nch}"])
                pr.op("scalar", lambda E_, nch=nch, pk=pk, ee=ee:
                      E_.copy(ee[:, nch*512:(nch+1)*512], pk[:]),
                      reads=[f"ps{(ct%2)*2+nch}"], writes=[f"ev{ct%2}"])
            if ct < 32:
                pr.dma("sync", gate[ct], ee[:], reads=[f"ev{ct%2}"], writes=[f"o{ct}"])
            elif ct < 80:
                pr.dma("sync", hbc[ct - 32], ee[:], reads=[f"ev{ct%2}"], writes=[f"o{ct}"])
            else:
                pr.dma("sync", dt_o, ee[0:64, :], reads=[f"ev{ct%2}"], writes=[f"o{ct}"])
        pr.finalize(nc, block, sems)
    _L1_CACHE["inproj"] = nc
    return nc


def build_conv():
    if "conv" in _L1_CACHE:
        return _L1_CACHE["conv"]
    nc = bass.Bass("TRN2", target_bir_lowering=False)
    pr = Prog()
    def din(name, shape, dt=F32):
        return nc.dram_tensor(name, list(shape), dt, kind="ExternalInput").ap()
    def dout(name, shape, dt=F32):
        return nc.dram_tensor(name, list(shape), dt, kind="ExternalOutput").ap()
    hbc = din("hbc", (48, 128, 1024))
    halo = din("halo", (48, 128, 3))
    dt_i = din("dt_i", (64, 1024))
    convw = din("convw", (48, 128, 4))
    convb = din("convb", (48, 128, 1))
    dtb = din("dtb", (64, 1))
    alog = din("alog", (64, 1))
    xbc = dout("xbc", (128, 8, 48, 128), BF16)
    dtp_o = dout("dtp_o", (64, 1024))
    scum_o = dout("scum_o", (64, 1024))
    from contextlib import ExitStack
    with ExitStack() as ctx:
        E = ctx.enter_context
        hb = [E(nc.sbuf_tensor(f"hb{i}", [128, 1027], F32)) for i in range(2)]
        cw = E(nc.sbuf_tensor("cw", [128, 4], F32))
        cb = E(nc.sbuf_tensor("cb", [128, 1], F32))
        acc = [E(nc.sbuf_tensor(f"acc{i}", [128, 1024], F32)) for i in range(2)]
        xb = [E(nc.sbuf_tensor(f"xb{i}", [128, 1024], BF16)) for i in range(2)]
        dts = E(nc.sbuf_tensor("dts", [64, 1024], F32))
        d1 = E(nc.sbuf_tensor("d1", [64, 1024], F32))
        zz = E(nc.sbuf_tensor("zz", [64, 1024], F32))
        l1p = E(nc.sbuf_tensor("l1p", [64, 1024], F32))
        ezz = E(nc.sbuf_tensor("ezz", [64, 1024], F32))
        rl = E(nc.sbuf_tensor("rl", [64, 1024], F32))
        dtp = E(nc.sbuf_tensor("dtp", [64, 1024], F32))
        aa = E(nc.sbuf_tensor("aa", [64, 1024], F32))
        scum = E(nc.sbuf_tensor("scum", [64, 1024], F32))
        dtbs = E(nc.sbuf_tensor("dtbs", [64, 1], F32))
        als = E(nc.sbuf_tensor("als", [64, 1], F32))
        nega = E(nc.sbuf_tensor("nega", [64, 1], F32))
        one64 = E(nc.sbuf_tensor("one64", [64, 1], F32))
        sems = {e: E(nc.semaphore(f"sem_{e}")) for e in Prog.SEMS}
        block = E(nc.Block())
        pr.op("vector", lambda E_: E_.memset(one64[:], 1.0), writes=["one64"])
        pr.dma("sync", dts[:], dt_i, writes=["dts"])
        pr.dma("sync", dtbs[:], dtb, writes=["dtbs"])
        pr.dma("sync", als[:], alog, writes=["als"])
        for ct in range(48):
            h = hb[ct % 2]; a = acc[ct % 2]; x = xb[ct % 2]
            pr.dma("sync", h[:, 0:3], halo[ct], writes=[f"hb{ct%2}"])
            pr.dma("sync", h[:, 3:1027], hbc[ct], writes=[f"hb{ct%2}"])
            pr.dma("sync", cw[:], convw[ct], writes=["cw"])
            pr.dma("sync", cb[:], convb[ct], writes=["cb"])
            pr.op("vector", lambda E_, h=h, a=a: E_.tensor_scalar(
                a[:], h[:, 0:1024], cw[:, 0:1], cb[:], op0=ALU.mult, op1=ALU.add),
                  reads=[f"hb{ct%2}", "cw", "cb"], writes=[f"acc{ct%2}"])
            for k in range(1, 4):
                pr.op("vector", lambda E_, h=h, a=a, k=k: E_.scalar_tensor_tensor(
                    a[:], h[:, k:1024+k], cw[:, k:k+1], a[:], op0=ALU.mult, op1=ALU.add),
                      reads=[f"hb{ct%2}", "cw", f"acc{ct%2}"], writes=[f"acc{ct%2}"])
            pr.op("scalar", lambda E_, a=a, x=x: E_.activation(x[:], a[:], AF.Silu),
                  reads=[f"acc{ct%2}"], writes=[f"xb{ct%2}"])
            pr.dma("gpsimd", xbc[:, :, ct, :],
                   x[:].rearrange("p (c t) -> p c t", c=8),
                   reads=[f"xb{ct%2}"], writes=[f"xo{ct}"])
        pr.op("vector", lambda E_: E_.tensor_scalar(d1[:], dts[:], dtbs[:], None,
                                                    op0=ALU.add),
              reads=["dts", "dtbs"], writes=["d1"])
        pr.op("scalar", lambda E_: E_.activation(zz[:], d1[:], AF.Abs),
              reads=["d1"], writes=["zz"])
        pr.op("scalar", lambda E_: E_.activation(ezz[:], zz[:], AF.Exp, scale=-1.0),
              reads=["zz"], writes=["ezz"])
        pr.op("scalar", lambda E_: E_.activation(l1p[:], ezz[:], AF.Ln, bias=one64[:]),
              reads=["ezz", "one64"], writes=["l1p"])
        pr.op("vector", lambda E_: E_.tensor_scalar(rl[:], d1[:], 0.0, None,
                                                    op0=ALU.max),
              reads=["d1"], writes=["rl"])
        pr.op("vector", lambda E_: E_.tensor_tensor(dtp[:], rl[:], l1p[:], ALU.add),
              reads=["l1p", "rl"], writes=["dtp"])
        pr.dma("gpsimd", dtp_o, dtp[:], reads=["dtp"], writes=["dtpo"])
        pr.op("scalar", lambda E_: E_.activation(nega[:], als[:], AF.Exp),
              reads=["als"], writes=["nega"])
        pr.op("vector", lambda E_: E_.tensor_scalar(nega[:], nega[:], -1.0, None,
                                                    op0=ALU.mult),
              reads=["nega"], writes=["nega"])
        pr.op("vector", lambda E_: E_.tensor_scalar(aa[:], dtp[:], nega[:], None,
                                                    op0=ALU.mult),
              reads=["dtp", "nega"], writes=["aa"])
        pr.op("vector", lambda E_: E_.tensor_tensor_scan(scum[:], aa[:], aa[:], 0.0,
                                                         op0=ALU.add, op1=ALU.bypass),
              reads=["aa"], writes=["scum"])
        pr.dma("gpsimd", scum_o, scum[:], reads=["scum"], writes=["scumo"])
        pr.finalize(nc, block, sems)
    _L1_CACHE["conv"] = nc
    return nc




def build_scan(with_init, debug=False):
    key = ("scan", with_init, debug)
    if key in _L1_CACHE:
        return _L1_CACHE[key]
    nc = bass.Bass("TRN2", target_bir_lowering=False)
    pr = Prog()
    def din(name, shape, dt=F32):
        return nc.dram_tensor(name, list(shape), dt, kind="ExternalInput").ap()
    def dout(name, shape, dt=F32):
        return nc.dram_tensor(name, list(shape), dt, kind="ExternalOutput").ap()
    def dint(name, shape, dt=F32):
        return nc.dram_tensor(name, list(shape), dt, kind="Internal").ap()

    xbc = din("xbc", (128, 8, 48, 128), BF16)
    dtp_i = din("dtp_i", (64, 1024))
    scum_i = din("scum_i", (64, 1024))
    maskc = din("maskc", (128, 128))
    identb = din("identb", (128, 128), BF16)
    identf = din("identf", (128, 128))
    onesrow = din("onesrow", (1, 128))
    onesrowb = din("onesrowb", (1, 128), BF16)
    onesb = din("onesb", (128, 1), BF16)
    drep = din("drep", (128, 32))
    if with_init:
        ST4 = din("ST4", (4, 128, 4096))
        scl4 = din("scl4", (4, 64, 1))
        flags4 = din("flags4", (128, 3))
        nwp = din("nwp", (128, 32))
        gate = din("gate", (32, 128, 1024))
        wtop = din("wtop", (32, 128, 32, 128))
        x2t = din("x2t", (32, 128, 1024))
        x3t = dout("x3t", (32, 128, 1024))
        if debug:
            dbg_att = dout("dbg_att", (32, 128, 1024))
        r0d = dint("r0d", (32, 128, 1024), F32)
    else:
        STout = dout("STout", (128, 4096))
    eD = dint("eD", (64, 128), F32)

    from contextlib import ExitStack
    with ExitStack() as ctx:
        E = ctx.enter_context
        scum = E(nc.sbuf_tensor("scum", [64, 1024], F32))
        dtp = E(nc.sbuf_tensor("dtp", [64, 1024], F32))
        ldt = E(nc.sbuf_tensor("ldt", [64, 1024], F32))
        ST = E(nc.sbuf_tensor("ST", [128, 4096], F32))
        STb = E(nc.sbuf_tensor("STb", [128, 4096], BF16))
        yn = E(nc.sbuf_tensor("yn", [128, 32 * 1024], BF16))
        ynf = yn[:].bitcast(F32)
        R4 = ynf[:, 0:4096]
        S4s = ynf[:, 4096:8192]
        xbcC0 = E(nc.sbuf_tensor("xbcC0", [128, 6144], BF16))
        xbcC = [xbcC0, xbcC0]
        arD = E(nc.sbuf_tensor("arD", [128, 8192], F32))
        yfin = arD[:, 0:4096]
        Dh = arD[:, 4096:8192]
        arB = E(nc.sbuf_tensor("arB", [128, 4096], BF16))
        arB2 = E(nc.sbuf_tensor("arB2", [128, 4096], BF16))
        xT = E(nc.sbuf_tensor("xT", [128, 4096], BF16))
        xwT = arB[:]
        scT = E(nc.sbuf_tensor("scT", [128, 64], F32))
        ldtT = E(nc.sbuf_tensor("ldtT", [128, 64], F32))
        scldT = E(nc.sbuf_tensor("scldT", [128, 64], F32))
        wmat = E(nc.sbuf_tensor("wmat", [64, 128], F32))
        wdt = E(nc.sbuf_tensor("wdt", [64, 128], F32))
        wdtTb = E(nc.sbuf_tensor("wdtTb", [128, 64], BF16))
        emat = E(nc.sbuf_tensor("emat", [64, 128], F32))
        srowH = E(nc.sbuf_tensor("srowH", [1, 4096], F32))
        erowb = E(nc.sbuf_tensor("erowb", [1, 4096], BF16))
        dtrow = E(nc.sbuf_tensor("dtrow", [1, 64], F32))
        dtbc = E(nc.sbuf_tensor("dtbc", [128, 64], F32))
        maskS = E(nc.sbuf_tensor("maskS", [128, 128], F32))
        idb = E(nc.sbuf_tensor("idb", [128, 128], BF16))
        idf = E(nc.sbuf_tensor("idf", [128, 128], F32))
        onr = E(nc.sbuf_tensor("onr", [1, 128], F32))
        onrb = E(nc.sbuf_tensor("onrb", [1, 128], BF16))
        onb = E(nc.sbuf_tensor("onb", [128, 1], BF16))
        drs = E(nc.sbuf_tensor("drs", [128, 32], F32))
        nsp = E(nc.sbuf_tensor("nsp", [64, 1], F32))
        sclE = E(nc.sbuf_tensor("sclE", [64, 1], F32))
        d4r = E(nc.sbuf_tensor("d4r", [1, 64], F32))
        d4bc = E(nc.sbuf_tensor("d4bc", [128, 64], F32))
        fl = E(nc.sbuf_tensor("fl", [128, 3], F32))
        ceb = [E(nc.sbuf_tensor(f"ceb{i}", [128, 128], BF16)) for i in range(2)]
        sqb = E(nc.sbuf_tensor("sqb", [128, 128], BF16))
        vrow = srowH[0:1, 1024:2048]
        rsr = srowH[0:1, 2048:3072]
        rsrb = srowH[0:1, :].bitcast(BF16)[0:1, 6144:7168]
        eps1 = E(nc.sbuf_tensor("eps1", [1, 1], F32))
        if with_init:
            nws = E(nc.sbuf_tensor("nws", [128, 32], F32))
            wostr = [arB[:], arB2[:]]
            xstr = arD[:, 0:1024]
            r0t = arD[:, 1024:2048]
            sgt = arD[:, 2048:3072]
            sgb = arD[:].bitcast(BF16)[:, 6144:7168]
            gtt = arD[:, 4096:5120]
            x1t = arD[:, 5120:6144]
            r0bf = STb[:, 0:1024]
            sq2 = STb[:, 1024:2048]
            mrow = srowH[0:1, 3072:4096]
            rsrow = rsr

        pk = [E(nc.psum_tensor(f"pk{i}", [128, 512], F32)) for i in range(6)]
        pkb = [pk[2][:].bitcast(BF16)[:, 0:128], pk[3][:].bitcast(BF16)[:, 0:128]]
        psV = E(nc.psum_tensor("psV", [1, 1024], F32))
        sems = {e: E(nc.semaphore(f"sem_{e}")) for e in Prog.SEMS}
        block = E(nc.Block())

        pr.op("vector", lambda E_: E_.memset(eps1[:], EPS), writes=["eps1"])
        pr.dma("sync", scum[:], scum_i, writes=["scum"])
        pr.dma("sync", dtp[:], dtp_i, writes=["dtp"])
        pr.dma("sync", maskS[:], maskc, writes=["maskS"])
        pr.dma("sync", idb[:], identb, writes=["idb"])
        pr.dma("sync", idf[:], identf, writes=["idf"])
        pr.dma("sync", onr[:], onesrow, writes=["onr"])
        pr.dma("sync", onrb[:], onesrowb, writes=["onrb"])
        pr.dma("sync", onb[:], onesb, writes=["onb"])
        pr.dma("sync", drs[:], drep, writes=["drs"])
        pr.op("scalar", lambda E_: E_.activation(ldt[:], dtp[:], AF.Ln),
              reads=["dtp"], writes=["ldt"])
        pr.op("gpsimd", lambda E_: E_.memset(ST[:], 0.0), writes=["ST"])
        if with_init:
            pr.dma("sync", nws[:], nwp, writes=["nws"])
            pr.op("gpsimd", lambda E_: E_.memset(R4[:], 0.0), writes=["R4"])
            pr.dma("sync", fl[:], flags4, writes=["fl"])
            for kq in range(3):
                pr.dma("sync", S4s[:], ST4[kq], writes=["S4s"])
                pr.dma("sync", sclE[:], scl4[kq], writes=["sclE"])
                if kq > 0:
                    pr.op("scalar", lambda E_: E_.activation(sclE[:], sclE[:], AF.Exp),
                          reads=["sclE"], writes=["sclE"])
                    pr.op("tensor", lambda E_: E_.transpose(
                        pk[2][0:1, 0:64], sclE[:], idf[0:64, 0:64]),
                          reads=["sclE", "idf"], writes=["pk2"])
                    pr.op("vector", lambda E_: E_.tensor_copy(d4r[0:1, :], pk[2][0:1, 0:64]),
                          reads=["pk2"], writes=["d4r"])
                    pr.op("tensor", lambda E_: E_.matmul(pk[3][:, 0:64], onr[0:1, :],
                                                         d4r[0:1, :], start=True, stop=True),
                          reads=["d4r", "onr"], writes=["pk3"])
                    pr.op("vector", lambda E_: E_.tensor_copy(d4bc[:], pk[3][:, 0:64]),
                          reads=["pk3"], writes=["d4bc"])
                    pr.op("vector", lambda E_: E_.tensor_tensor(
                        R4.rearrange("p (h q) -> p h q", q=64),
                        R4.rearrange("p (h q) -> p h q", q=64),
                        d4bc[:].unsqueeze(-1).to_broadcast([128, 64, 64]),
                        ALU.mult),
                          reads=["R4", "d4bc"], writes=["R4"])
                pr.op("vector", lambda E_: E_.tensor_tensor(R4, R4, S4s, ALU.add),
                      reads=["R4", "S4s"], writes=["R4"])
                pr.op("vector", lambda E_, kq=kq: E_.scalar_tensor_tensor(
                    ST[:], R4, fl[:, kq:kq+1], ST[:], op0=ALU.mult, op1=ALU.add),
                      reads=["R4", "fl", "ST"], writes=["ST"])
            pr.barrier()

        for c in range(NCH):
            xc = xbcC[0]
            XK = "xbcC0"
            pr.dma("sync", xc[:], xbc[:, c].rearrange("p ct t -> p (ct t)"),
                   writes=[XK])
            c0 = c * 128
            pr.op("vector", lambda E_: E_.tensor_copy(STb[:, 0:4096], ST[:]),
                  reads=["ST"], writes=["STb"])
            pr.op("tensor", lambda E_, c0=c0: E_.transpose(
                pk[2][:, 0:64], scum[:, c0:c0+128], idf[0:64, 0:64]),
                  reads=["scum", "idf"], writes=["pk2"])
            pr.op("vector", lambda E_: E_.tensor_copy(scT[:], pk[2][:, 0:64]),
                  reads=["pk2"], writes=["scT"])
            pr.op("tensor", lambda E_, c0=c0: E_.transpose(
                pk[3][:, 0:64], ldt[:, c0:c0+128], idf[0:64, 0:64]),
                  reads=["ldt", "idf"], writes=["pk3"])
            pr.op("vector", lambda E_: E_.tensor_copy(ldtT[:], pk[3][:, 0:64]),
                  reads=["pk3"], writes=["ldtT"])
            pr.op("vector", lambda E_: E_.tensor_tensor(scldT[:], scT[:], ldtT[:],
                                                        ALU.subtract),
                  reads=["scT", "ldtT"], writes=["scldT"])
            if c == 0:
                pr.op("scalar", lambda E_, c0=c0: E_.activation(
                    emat[:], scum[:, c0:c0+128], AF.Exp),
                      reads=["scum"], writes=["emat"])
            else:
                pr.op("vector", lambda E_, c0=c0: E_.tensor_scalar(
                    nsp[:], scum[:, c0-1:c0], -1.0, None, op0=ALU.mult),
                      reads=["scum"], writes=["nsp"])
                pr.op("scalar", lambda E_, c0=c0: E_.activation(
                    emat[:], scum[:, c0:c0+128], AF.Exp, bias=nsp[:]),
                      reads=["scum", "nsp"], writes=["emat"])
            pr.op("scalar", lambda E_, c0=c0: E_.activation(
                wmat[:], scum[:, c0:c0+128], AF.Exp, scale=-1.0,
                bias=scum[:, c0+127:c0+128]),
                  reads=["scum"], writes=["wmat"])
            pr.op("vector", lambda E_, c0=c0: E_.tensor_tensor(
                wdt[:], wmat[:], dtp[:, c0:c0+128], ALU.mult),
                  reads=["wmat", "dtp"], writes=["wdt"])
            pr.op("tensor", lambda E_: E_.transpose(pk[2][:, 0:64], wdt[:], idf[0:64, 0:64]),
                  reads=["wdt", "idf"], writes=["pk2"])
            pr.op("vector", lambda E_: E_.tensor_copy(wdtTb[:], pk[2][:, 0:64]),
                  reads=["pk2"], writes=["wdtTb"])
            pr.dma("gpsimd", eD, emat[:], reads=["emat"], writes=["eD"])
            pr.op("tensor", lambda E_: E_.transpose(pk[3][0:1, 0:64],
                                                    emat[:, 127:128], idf[0:64, 0:64]),
                  reads=["emat", "idf"], writes=["pk3"])
            pr.op("vector", lambda E_: E_.tensor_copy(dtrow[0:1, :], pk[3][0:1, 0:64]),
                  reads=["pk3"], writes=["dtrow"])
            pr.op("tensor", lambda E_: E_.matmul(pk[2][:, 0:64], onr[0:1, :],
                                                 dtrow[0:1, :], start=True, stop=True),
                  reads=["dtrow", "onr"], writes=["pk2"])
            pr.op("vector", lambda E_: E_.tensor_copy(dtbc[:], pk[2][:, 0:64]),
                  reads=["pk2"], writes=["dtbc"])
            for ft in range(NT):
                pb = pkb[ft % 2]
                pr.op("tensor", lambda E_, ft=ft, pb=pb, xc=xc: E_.transpose(
                    pb[:], xc[:, ft*128:(ft+1)*128], idb[:]),
                      reads=[XK, "idb"], writes=[f"pk{2+ft%2}"])
                pr.op("scalar", lambda E_, ft=ft, pb=pb: E_.copy(
                    xT[:, ft*128:(ft+1)*128], pb[:]),
                      reads=[f"pk{2+ft%2}"], writes=["xT"])
            for half in range(2):
                ho = half * 32
                pr.dma("sync", srowH[0:1, :].rearrange("o (h t) -> o h t", h=32),
                       scum_i[ho:ho+32, c0:c0+128].unsqueeze(0),
                       writes=["srowH"])
                pr.dma("gpsimd", erowb[0:1, :].rearrange("o (h t) -> o h t", h=32),
                       eD[ho:ho+32, :].unsqueeze(0),
                       reads=["eD"], writes=["erowb"])
                for hh in range(32):
                    pq = pk[2 + (hh % 2)]
                    pr.op("tensor", lambda E_, hh=hh, pq=pq: E_.matmul(
                        pq[:, 0:128], onr[0:1, :], srowH[0:1, hh*128:(hh+1)*128],
                        start=True, stop=True),
                          reads=["srowH", "onr"], writes=[f"pk{2+hh%2}"])
                    pr.op("vector", lambda E_, hh=hh, ho=ho, pq=pq:
                          E_.scalar_tensor_tensor(
                              Dh[:, hh*128:(hh+1)*128], pq[:, 0:128],
                              scldT[:, ho+hh:ho+hh+1], maskS[:],
                              op0=ALU.subtract, op1=ALU.add),
                          reads=[f"pk{2+hh%2}", "scldT", "maskS"], writes=["Dh"])
                pr.op("scalar", lambda E_: E_.activation(arB[:], Dh[:], AF.Exp),
                      reads=["Dh"], writes=["arB"])
                pA = pk[half]
                for gg in range(4):
                    g = half * 4 + gg
                    pr.op("tensor", lambda E_, g=g, gg=gg, pA=pA, xc=xc: E_.matmul(
                        pA[:, gg*128:(gg+1)*128],
                        xc[:, (32+g)*128:(33+g)*128], xc[:, (40+g)*128:(41+g)*128],
                        start=True, stop=True),
                          reads=[XK], writes=[f"pk{half}"])
                pr.op("vector", lambda E_, pA=pA: E_.tensor_tensor(
                    arB2[:].rearrange("p (gg j t) -> p gg j t", gg=4, j=8),
                    arB[:].rearrange("p (gg j t) -> p gg j t", gg=4, j=8),
                    pA[:].rearrange("p (gg t) -> p gg t", gg=4).unsqueeze(2).to_broadcast([128, 4, 8, 128]),
                    ALU.mult),
                      reads=["arB", f"pk{half}"], writes=["arB2"])
                for hh in range(32):
                    h = ho + hh
                    hb = hh % 2
                    pY = pk[4 + hb]
                    pe = pk[2 + hb]
                    g = h // 8
                    pr.op("tensor", lambda E_, hh=hh, pe=pe: E_.matmul(
                        pe[:, 0:128], onrb[0:1, :], erowb[0:1, hh*128:(hh+1)*128],
                        start=True, stop=True),
                          reads=["erowb", "onrb"], writes=[f"pk{2+hb}"])
                    ce = ceb[hb]
                    pr.op("vector", lambda E_, g=g, pe=pe, ce=ce, xc=xc: E_.tensor_tensor(
                        ce[:], xc[:, (40+g)*128:(41+g)*128], pe[:, 0:128], ALU.mult),
                          reads=[XK, f"pk{2+hb}"], writes=[f"ceb{hb}"])
                    pr.op("tensor", lambda E_, h=h, hh=hh, pY=pY: E_.matmul(
                        pY[0:64, 0:128], xT[:, h*64:(h+1)*64],
                        arB2[:, hh*128:(hh+1)*128], start=True, stop=False),
                          reads=["xT", "arB2"], writes=[f"pk{4+hb}"])
                    pr.op("tensor", lambda E_, h=h, pY=pY, ce=ce: E_.matmul(
                        pY[0:64, 0:128], STb[:, h*64:(h+1)*64], ce[:],
                        start=False, stop=True),
                          reads=["STb", f"ceb{hb}"], writes=[f"pk{4+hb}"])
                    ft = h // 2
                    rr0 = (h % 2) * 64
                    pr.op("vector", lambda E_, ft=ft, rr0=rr0, pY=pY, xc=xc:
                          E_.scalar_tensor_tensor(
                              yfin[rr0:rr0+64, ft*128:(ft+1)*128],
                              xc[rr0:rr0+64, ft*128:(ft+1)*128],
                              drs[rr0:rr0+64, ft:ft+1], pY[0:64, 0:128],
                              op0=ALU.mult, op1=ALU.add),
                          reads=[XK, "drs", f"pk{4+hb}"], writes=["yfin"])
            for ft in range(NT):
                pr.op("scalar", lambda E_, ft=ft: E_.activation(
                    sqb[:], yfin[:, ft*128:(ft+1)*128], AF.Square),
                      reads=["yfin"], writes=["sqb"])
                pr.op("tensor", lambda E_, ft=ft: E_.matmul(
                    psV[0:1, (ft//4)*128:(ft//4)*128+128], onb[:], sqb[:],
                    start=(ft % 4 == 0), stop=(ft % 4 == 3)),
                      reads=["sqb", "onb"], writes=["psV"])
            pr.op("scalar", lambda E_: E_.activation(vrow[:], psV[:], AF.Ln,
                                                     scale=1.0/512.0, bias=eps1[:]),
                  reads=["psV", "eps1"], writes=["vrow", "srowH"])
            pr.op("scalar", lambda E_: E_.activation(rsr[:], vrow[:], AF.Exp, scale=-0.5),
                  reads=["vrow"], writes=["rsr", "srowH"])
            pr.op("vector", lambda E_: E_.tensor_copy(rsrb[:], rsr[:]),
                  reads=["rsr"], writes=["rsrb", "srowH"])
            if with_init:
                for g in range(8):
                    pe = pk[2 + (g % 2)]
                    pr.op("tensor", lambda E_, g=g, pe=pe: E_.matmul(
                        pe[:, 0:128], onrb[0:1, :], rsrb[0:1, g*128:(g+1)*128],
                        start=True, stop=True),
                          reads=["rsrb", "onrb"], writes=[f"pk{2+g%2}"])
                    for ft in range(g*4, g*4+4):
                        pr.op("vector", lambda E_, ft=ft, g=g, pe=pe, c=c:
                              E_.scalar_tensor_tensor(
                                  yn[:, ft*1024 + c*128: ft*1024 + c*128 + 128],
                                  yfin[:, ft*128:(ft+1)*128], nws[:, ft:ft+1],
                                  pe[:, 0:128], op0=ALU.mult, op1=ALU.mult),
                              reads=["yfin", "nws", f"pk{2+g%2}"], writes=[f"yn{ft}"])
            pr.op("vector", lambda E_: E_.tensor_tensor(
                xwT.rearrange("p (h q) -> p h q", q=64),
                xT[:].rearrange("p (h q) -> p h q", q=64),
                wdtTb[:].unsqueeze(-1).to_broadcast([128, 64, 64]),
                ALU.mult),
                  reads=["xT", "wdtTb"], writes=["xwT", "arB"])
            for g in range(8):
                pb = pkb[g % 2]
                pr.op("tensor", lambda E_, g=g, pb=pb, xc=xc: E_.transpose(
                    pb[:], xc[:, (32+g)*128:(33+g)*128], idb[:]),
                      reads=[XK, "idb"], writes=[f"pk{2+g%2}"])
                bt = ceb[g % 2]
                pr.op("scalar", lambda E_, pb=pb, bt=bt: E_.copy(bt[:], pb[:]),
                      reads=[f"pk{2+g%2}"], writes=[f"ceb{g%2}"])
                pT_ = pk[4 + (g % 2)]
                for hh in range(8):
                    h = g * 8 + hh
                    pr.op("tensor", lambda E_, h=h, hh=hh, pT_=pT_, bt=bt: E_.matmul(
                        pT_[:, hh*64:(hh+1)*64], bt[:], xwT[:, h*64:(h+1)*64],
                        start=(hh == 0), stop=(hh == 7)),
                          reads=[f"ceb{g%2}", "xwT", "arB"], writes=[f"pk{4+g%2}"])
                pr.op("vector", lambda E_, g=g: E_.tensor_tensor(
                    ST[:, g*512:(g+1)*512].rearrange("p (hh q) -> p hh q", q=64),
                    ST[:, g*512:(g+1)*512].rearrange("p (hh q) -> p hh q", q=64),
                    dtbc[:, g*8:(g+1)*8].unsqueeze(-1).to_broadcast([128, 8, 64]),
                    ALU.mult),
                      reads=["ST", "dtbc"], writes=["ST"])
                pr.op("vector", lambda E_, g=g, pT_=pT_: E_.tensor_tensor(
                    ST[:, g*512:(g+1)*512], ST[:, g*512:(g+1)*512], pT_[:, 0:512],
                    ALU.add),
                      reads=["ST", f"pk{4+g%2}"], writes=["ST"])

        if not with_init:
            pr.dma("gpsimd", STout, ST[:], reads=["ST"], writes=["STo"])
        else:
            pr.barrier()
            YNK = [f"yn{ft}" for ft in range(NT)]
            for it in range(NT):
                pr.dma("sync", gtt[:], gate[it], writes=["gtt"])
                pr.op("scalar", lambda E_: E_.activation(sgb[:], gtt[:], AF.Silu),
                      reads=["gtt"], writes=["sgb"])
                pr.op("vector", lambda E_, it=it: E_.tensor_tensor(
                    yn[:, it*1024:(it+1)*1024], yn[:, it*1024:(it+1)*1024],
                    sgb[:], ALU.mult),
                      reads=["sgb", f"yn{it}"], writes=[f"yn{it}"])
            for dt_ in range(NT):
                w0 = wostr[dt_ % 2]
                WK = f"wostr{dt_%2}"
                pr.dma("gpsimd", w0, wtop[dt_].rearrange("p it j -> p (it j)"),
                       writes=[WK])
                pr.dma("sync", xstr, x2t[dt_], writes=["xstr"])
                for nch in range(2):
                    pq = pk[(dt_ % 2) * 2 + nch]
                    PK = f"pk{(dt_%2)*2+nch}"
                    for it in range(NT):
                        pr.op("tensor", lambda E_, it=it, nch=nch, pq=pq, w0=w0:
                              E_.matmul(pq[:], w0[:, it*128:(it+1)*128],
                                        yn[:, it*1024 + nch*512: it*1024 + nch*512 + 512],
                                        start=(it == 0), stop=(it == NT - 1)),
                              reads=[WK] + YNK, writes=[PK])
                    pr.op("vector", lambda E_, nch=nch, pq=pq: E_.tensor_tensor(
                        r0t[:, nch*512:(nch+1)*512], pq[:],
                        xstr[:, nch*512:(nch+1)*512], ALU.add),
                          reads=[PK, "xstr"], writes=["r0t"])
                pr.dma("gpsimd", r0d[dt_], r0t, reads=["r0t"], writes=[f"r0d{dt_}"])
                if debug:
                    pr.dma("gpsimd", dbg_att[dt_], r0t, reads=["r0t"],
                           writes=[f"dba{dt_}"])
                pr.op("vector", lambda E_: E_.tensor_copy(r0bf, r0t),
                      reads=["r0t"], writes=["r0bf"])
                pr.op("scalar", lambda E_: E_.activation(sq2, r0t, AF.Square),
                      reads=["r0t"], writes=["sq2"])
                for nch in range(2):
                    pr.op("tensor", lambda E_, nch=nch, dt_=dt_: E_.matmul(
                        psV[0:1, nch*512:(nch+1)*512], onb[:],
                        r0bf[:, nch*512:(nch+1)*512],
                        start=(dt_ == 0), stop=(dt_ == NT - 1)),
                          reads=["r0bf", "onb"], writes=["psV"])
                    pq2 = pk[4 + nch]
                    pr.op("tensor", lambda E_, nch=nch, dt_=dt_, pq2=pq2: E_.matmul(
                        pq2[0:1, :], onb[:], sq2[:, nch*512:(nch+1)*512],
                        start=(dt_ == 0), stop=(dt_ == NT - 1)),
                          reads=["sq2", "onb"], writes=[f"pk{4+nch}"])
            pr.op("vector", lambda E_: E_.tensor_scalar(
                mrow[:], psV[:], 1.0 / 4096.0, None, op0=ALU.mult),
                  reads=["psV"], writes=["mrow", "srowH"])
            pr.op("scalar", lambda E_: E_.activation(vrow[:], mrow[:], AF.Square),
                  reads=["mrow"], writes=["vrow", "srowH"])
            for nch in range(2):
                pq2 = pk[4 + nch]
                pr.op("vector", lambda E_, nch=nch, pq2=pq2: E_.scalar_tensor_tensor(
                    vrow[:, nch*512:(nch+1)*512], pq2[0:1, :], 1.0 / 4096.0,
                    vrow[:, nch*512:(nch+1)*512], op0=ALU.mult, op1=ALU.subtract),
                      reads=[f"pk{4+nch}", "vrow"], writes=["vrow", "srowH"])
            pr.op("scalar", lambda E_: E_.activation(vrow[:], vrow[:], AF.Ln,
                                                     bias=eps1[:]),
                  reads=["vrow", "eps1"], writes=["vrow", "srowH"])
            pr.op("scalar", lambda E_: E_.activation(rsrow[:], vrow[:], AF.Exp,
                                                     scale=-0.5),
                  reads=["vrow"], writes=["rsrow", "srowH"])
            for nch in range(2):
                pr.op("tensor", lambda E_, nch=nch: E_.matmul(
                    pk[nch][:], onr[0:1, :], mrow[0:1, nch*512:(nch+1)*512],
                    start=True, stop=True),
                      reads=["mrow", "onr"], writes=[f"pk{nch}"])
                pr.op("tensor", lambda E_, nch=nch: E_.matmul(
                    pk[2+nch][:], onr[0:1, :], rsrow[0:1, nch*512:(nch+1)*512],
                    start=True, stop=True),
                      reads=["rsrow", "onr"], writes=[f"pk{2+nch}"])
            for dt_ in range(NT):
                pr.dma("sync", r0t, r0d[dt_], reads=[f"r0d{dt_}"], writes=["r0t"])
                for nch in range(2):
                    pr.op("vector", lambda E_, nch=nch: E_.tensor_tensor(
                        x1t[:, nch*512:(nch+1)*512], r0t[:, nch*512:(nch+1)*512],
                        pk[nch][:], ALU.subtract),
                          reads=["r0t", f"pk{nch}"], writes=["x1t"])
                    pr.op("vector", lambda E_, nch=nch: E_.tensor_tensor(
                        x1t[:, nch*512:(nch+1)*512], x1t[:, nch*512:(nch+1)*512],
                        pk[2+nch][:], ALU.mult),
                          reads=[f"pk{2+nch}"], writes=["x1t"])
                pr.dma("gpsimd", x3t[dt_], x1t, reads=["x1t"], writes=[f"x3o{dt_}"])
        pr.finalize(nc, block, sems)
    _L1_CACHE[key] = nc
    return nc



def build_ffn():
    if "ffn" in _L1_CACHE:
        return _L1_CACHE["ffn"]
    nc = bass.Bass("TRN2", target_bir_lowering=False)
    pr = Prog()
    def din(name, shape, dt=F32):
        return nc.dram_tensor(name, list(shape), dt, kind="ExternalInput").ap()
    def dout(name, shape, dt=F32):
        return nc.dram_tensor(name, list(shape), dt, kind="ExternalOutput").ap()
    def dint(name, shape, dt=F32):
        return nc.dram_tensor(name, list(shape), dt, kind="Internal").ap()
    x3t = din("x3t", (32, 128, 1024))
    wtbp = din("wtbp", (32, 128, 32, 128))
    bbp = din("bbp", (128, 32))
    a1p = din("a1p", (128, 32, 64))
    b1p = din("b1p", (32, 128, 8, 128))
    r1p = din("r1p", (128, 32, 16))
    selp = din("selp", (16, 8, 128), BF16)
    identb = din("identb", (128, 128), BF16)
    onesrow = din("onesrow", (1, 128))
    onesb = din("onesb", (128, 1), BF16)
    x4t = dout("x4t", (32, 128, 1024))
    r0d = dint("r0d", (32, 128, 1024), F32)
    from contextlib import ExitStack
    with ExitStack() as ctx:
        E = ctx.enter_context
        x3bf = E(nc.sbuf_tensor("x3bf", [128, 32 * 1024], BF16))
        a1sb = E(nc.sbuf_tensor("a1sb", [128, 32 * 64], BF16))
        r1sb = E(nc.sbuf_tensor("r1sb", [128, 32 * 16], BF16))
        selsb = E(nc.sbuf_tensor("selsb", [16, 8 * 128], BF16))
        wstr = [E(nc.sbuf_tensor(f"wstr{i}", [128, 32 * 128], BF16)) for i in range(2)]
        bstr = [E(nc.sbuf_tensor(f"bstr{i}", [128, 8 * 128], BF16)) for i in range(2)]
        bbs = E(nc.sbuf_tensor("bbs", [128, 32], F32))
        idb = E(nc.sbuf_tensor("idb", [128, 128], BF16))
        onr = E(nc.sbuf_tensor("onr", [1, 128], F32))
        onb = E(nc.sbuf_tensor("onb", [128, 1], BF16))
        lg16 = E(nc.sbuf_tensor("lg16", [128, 8 * 16], F32))
        prob = E(nc.sbuf_tensor("prob", [128, 16], F32))
        gw = E(nc.sbuf_tensor("gw", [128, 16], F32))
        gm = E(nc.sbuf_tensor("gm", [128, 1], F32))
        gm2 = E(nc.sbuf_tensor("gm2", [128, 1], F32))
        gz = E(nc.sbuf_tensor("gz", [128, 1], F32))
        ngm = E(nc.sbuf_tensor("ngm", [128, 1], F32))
        rz = E(nc.sbuf_tensor("rz", [128, 1], F32))
        cmb = E(nc.sbuf_tensor("cmb", [128, 16], F32))
        cmbb = E(nc.sbuf_tensor("cmbb", [128, 16], BF16))
        cmbT = E(nc.sbuf_tensor("cmbT", [16, 1024], BF16))
        midT = E(nc.sbuf_tensor("midT", [64, 1024], F32))
        midg = E(nc.sbuf_tensor("midg", [128, 8 * 1024], BF16))
        xstr = E(nc.sbuf_tensor("xstr", [128, 1024], F32))
        r0t = E(nc.sbuf_tensor("r0t", [128, 1024], F32))
        r0bf = E(nc.sbuf_tensor("r0bf", [128, 1024], BF16))
        sq2 = E(nc.sbuf_tensor("sq2", [128, 1024], BF16))
        x1t = E(nc.sbuf_tensor("x1t", [128, 1024], F32))
        mrow = E(nc.sbuf_tensor("mrow", [1, 1024], F32))
        vrow = E(nc.sbuf_tensor("vrow", [1, 1024], F32))
        rsrow = E(nc.sbuf_tensor("rsrow", [1, 1024], F32))
        eps1 = E(nc.sbuf_tensor("eps1", [1, 1], F32))
        pk = [E(nc.psum_tensor(f"pk{i}", [128, 512], F32)) for i in range(6)]
        pkb = [pk[2][:].bitcast(BF16)[:, 0:128], pk[3][:].bitcast(BF16)[:, 0:128]]
        psV = E(nc.psum_tensor("psV", [1, 1024], F32))
        sems = {e: E(nc.semaphore(f"sem_{e}")) for e in Prog.SEMS}
        block = E(nc.Block())

        pr.op("vector", lambda E_: E_.memset(eps1[:], EPS), writes=["eps1"])
        for ft in range(NT):
            pr.dma("gpsimd", x3bf[:, ft*1024:(ft+1)*1024], x3t[ft],
                   writes=[f"x3bf{ft}"])
        XK = [f"x3bf{ft}" for ft in range(NT)]
        pr.dma("gpsimd", a1sb[:], a1p.rearrange("p ft j -> p (ft j)"), writes=["a1sb"])
        pr.dma("gpsimd", r1sb[:], r1p.rearrange("p ft j -> p (ft j)"), writes=["r1sb"])
        pr.dma("sync", selsb[:], selp.rearrange("e pr j -> e (pr j)"), writes=["selsb"])
        pr.dma("sync", idb[:], identb, writes=["idb"])
        pr.dma("sync", onr[:], onesrow, writes=["onr"])
        pr.dma("sync", onb[:], onesb, writes=["onb"])
        pr.dma("sync", bbs[:], bbp, writes=["bbs"])
        for tt in range(8):
            for ft in range(NT):
                pr.op("tensor", lambda E_, tt=tt, ft=ft: E_.matmul(
                    pk[0][:, 0:16], x3bf[:, ft*1024 + tt*128: ft*1024 + tt*128 + 128],
                    r1sb[:, ft*16:(ft+1)*16], start=(ft == 0), stop=(ft == NT - 1)),
                      reads=[f"x3bf{ft}", "r1sb"], writes=["pk0"])
            pr.op("scalar", lambda E_, tt=tt: E_.copy(lg16[:, tt*16:(tt+1)*16],
                                                      pk[0][:, 0:16]),
                  reads=["pk0"], writes=["lg16"])
        for tt in range(8):
            lg = lg16[:, tt*16:(tt+1)*16]
            pr.op("vector", lambda E_, lg=lg: E_.tensor_reduce(gm[:], lg, AX.X, ALU.max),
                  reads=["lg16"], writes=["gm"])
            pr.op("vector", lambda E_: E_.tensor_scalar(ngm[:], gm[:], -1.0, None,
                                                        op0=ALU.mult),
                  reads=["gm"], writes=["ngm"])
            pr.op("scalar", lambda E_, lg=lg: E_.activation(prob[:], lg, AF.Exp,
                                                            bias=ngm[:],
                                                            accum_out=gz[:]),
                  reads=["lg16", "ngm"], writes=["prob", "gz"])
            pr.op("vector", lambda E_: E_.reciprocal(rz[:], gz[:]),
                  reads=["gz"], writes=["rz"])
            pr.op("vector", lambda E_: E_.tensor_scalar(prob[:], prob[:], rz[:], None,
                                                        op0=ALU.mult),
                  reads=["prob", "rz"], writes=["prob"])
            pr.op("vector", lambda E_: E_.tensor_reduce(gm[:], prob[:], AX.X, ALU.max),
                  reads=["prob"], writes=["gm"])
            pr.op("vector", lambda E_: E_.tensor_scalar(gw[:], prob[:], gm[:], -1e30,
                                                        op0=ALU.is_ge, op1=ALU.mult),
                  reads=["prob", "gm"], writes=["gw"])
            pr.op("vector", lambda E_: E_.tensor_tensor(gw[:], gw[:], prob[:], ALU.add),
                  reads=["prob", "gw"], writes=["gw"])
            pr.op("vector", lambda E_: E_.tensor_reduce(gm2[:], gw[:], AX.X, ALU.max),
                  reads=["gw"], writes=["gm2"])
            pr.op("vector", lambda E_: E_.tensor_tensor(gz[:], gm[:], gm2[:], ALU.add),
                  reads=["gm", "gm2"], writes=["gz"])
            pr.op("vector", lambda E_: E_.tensor_scalar(gz[:], gz[:], 1e-6, None,
                                                        op0=ALU.add),
                  reads=["gz"], writes=["gz"])
            pr.op("vector", lambda E_: E_.reciprocal(rz[:], gz[:]),
                  reads=["gz"], writes=["rz"])
            pr.op("vector", lambda E_: E_.tensor_scalar(cmb[:], prob[:], gm2[:], None,
                                                        op0=ALU.is_ge),
                  reads=["prob", "gm2"], writes=["cmb"])
            pr.op("vector", lambda E_: E_.tensor_tensor(cmb[:], cmb[:], prob[:],
                                                        ALU.mult),
                  reads=["prob", "cmb"], writes=["cmb"])
            pr.op("vector", lambda E_: E_.tensor_scalar(cmb[:], cmb[:], rz[:], 0.5,
                                                        op0=ALU.mult, op1=ALU.mult),
                  reads=["cmb", "rz"], writes=["cmb"])
            pr.op("vector", lambda E_: E_.tensor_copy(cmbb[:], cmb[:]),
                  reads=["cmb"], writes=["cmbb"])
            pr.op("tensor", lambda E_: E_.transpose(
                pk[2][:].bitcast(BF16)[0:16, 0:128], cmbb[:], idb[:]),
                  reads=["cmbb", "idb"], writes=["pk2"])
            pr.op("scalar", lambda E_, tt=tt: E_.copy(cmbT[0:16, tt*128:(tt+1)*128],
                                                      pk[2][:].bitcast(BF16)[0:16, 0:128]),
                  reads=["pk2"], writes=["cmbT"])
        for nch in range(2):
            for ft in range(NT):
                pr.op("tensor", lambda E_, ft=ft, nch=nch: E_.matmul(
                    pk[1][0:64, :], a1sb[:, ft*64:(ft+1)*64],
                    x3bf[:, ft*1024 + nch*512: ft*1024 + nch*512 + 512],
                    start=(ft == 0), stop=(ft == NT - 1)),
                      reads=[f"x3bf{ft}", "a1sb"], writes=["pk1"])
            pr.op("scalar", lambda E_, nch=nch: E_.copy(midT[:, nch*512:(nch+1)*512],
                                                        pk[1][0:64, :]),
                  reads=["pk1"], writes=["midT"])
        for pr_ in range(8):
            for nch in range(2):
                pr.op("tensor", lambda E_, pr_=pr_, nch=nch: E_.matmul(
                    pk[4][:], selsb[:, pr_*128:(pr_+1)*128],
                    cmbT[:, nch*512:(nch+1)*512], start=True, stop=True),
                      reads=["selsb", "cmbT"], writes=["pk4"])
                for half in range(2):
                    pr.op("vector", lambda E_, pr_=pr_, nch=nch, half=half:
                          E_.tensor_tensor(
                              midg[half*64:(half+1)*64,
                                   pr_*1024 + nch*512: pr_*1024 + nch*512 + 512],
                              midT[:, nch*512:(nch+1)*512],
                              pk[4][half*64:(half+1)*64, :], ALU.mult),
                          reads=["midT", "pk4"], writes=[f"midg{pr_}"])
        for ot in range(NT):
            w0 = wstr[ot % 2]; b0 = bstr[ot % 2]
            WK, BK = f"wstr{ot%2}", f"bstr{ot%2}"
            pr.dma("gpsimd", w0[:], wtbp[ot].rearrange("p it j -> p (it j)"), writes=[WK])
            pr.dma("gpsimd", b0[:], b1p[ot].rearrange("p pr j -> p (pr j)"), writes=[BK])
            pr.dma("sync", xstr[:], x3t[ot], writes=["xstr"])
            for nch in range(2):
                pq = pk[(ot % 2) * 2 + nch]
                PK = f"pk{(ot%2)*2+nch}"
                for it in range(NT):
                    pr.op("tensor", lambda E_, it=it, nch=nch, pq=pq, w0=w0: E_.matmul(
                        pq[:], w0[:, it*128:(it+1)*128],
                        x3bf[:, it*1024 + nch*512: it*1024 + nch*512 + 512],
                        start=(it == 0), stop=False),
                          reads=[WK, f"x3bf{it}"], writes=[PK])
                for pr_ in range(8):
                    pr.op("tensor", lambda E_, pr_=pr_, nch=nch, pq=pq, b0=b0: E_.matmul(
                        pq[:], b0[:, pr_*128:(pr_+1)*128],
                        midg[:, pr_*1024 + nch*512: pr_*1024 + nch*512 + 512],
                        start=False, stop=(pr_ == 7)),
                          reads=[BK, f"midg{pr_}"], writes=[PK])
                pr.op("vector", lambda E_, nch=nch, pq=pq, ot=ot:
                      E_.scalar_tensor_tensor(
                          r0t[:, nch*512:(nch+1)*512], pq[:], bbs[:, ot:ot+1],
                          xstr[:, nch*512:(nch+1)*512], op0=ALU.add, op1=ALU.add),
                      reads=[PK, "bbs", "xstr"], writes=["r0t"])
            pr.dma("gpsimd", r0d[ot], r0t[:], reads=["r0t"], writes=[f"r0d{ot}"])
            pr.op("vector", lambda E_: E_.tensor_copy(r0bf[:], r0t[:]),
                  reads=["r0t"], writes=["r0bf"])
            pr.op("scalar", lambda E_: E_.activation(sq2[:], r0t[:], AF.Square),
                  reads=["r0t"], writes=["sq2"])
            for nch in range(2):
                pr.op("tensor", lambda E_, nch=nch, ot=ot: E_.matmul(
                    psV[0:1, nch*512:(nch+1)*512], onb[:],
                    r0bf[:, nch*512:(nch+1)*512], start=(ot == 0), stop=(ot == NT - 1)),
                      reads=["r0bf", "onb"], writes=["psV"])
                pq2 = pk[4 + nch]
                pr.op("tensor", lambda E_, nch=nch, ot=ot, pq2=pq2: E_.matmul(
                    pq2[0:1, :], onb[:], sq2[:, nch*512:(nch+1)*512],
                    start=(ot == 0), stop=(ot == NT - 1)),
                      reads=["sq2", "onb"], writes=[f"pk{4+nch}"])
        pr.op("vector", lambda E_: E_.tensor_scalar(
            mrow[:], psV[:], 1.0 / 4096.0, None, op0=ALU.mult),
              reads=["psV"], writes=["mrow"])
        pr.op("scalar", lambda E_: E_.activation(vrow[:], mrow[:], AF.Square),
              reads=["mrow"], writes=["vrow"])
        for nch in range(2):
            pq2 = pk[4 + nch]
            pr.op("vector", lambda E_, nch=nch, pq2=pq2: E_.scalar_tensor_tensor(
                vrow[:, nch*512:(nch+1)*512], pq2[0:1, :], 1.0 / 4096.0,
                vrow[:, nch*512:(nch+1)*512], op0=ALU.mult, op1=ALU.subtract),
                  reads=[f"pk{4+nch}", "vrow"], writes=["vrow"])
        pr.op("scalar", lambda E_: E_.activation(vrow[:], vrow[:], AF.Ln, bias=eps1[:]),
              reads=["vrow", "eps1"], writes=["vrow"])
        pr.op("scalar", lambda E_: E_.activation(rsrow[:], vrow[:], AF.Exp, scale=-0.5),
              reads=["vrow"], writes=["rsrow"])
        for nch in range(2):
            pr.op("tensor", lambda E_, nch=nch: E_.matmul(
                pk[nch][:], onr[0:1, :], mrow[0:1, nch*512:(nch+1)*512],
                start=True, stop=True),
                  reads=["mrow", "onr"], writes=[f"pk{nch}"])
            pr.op("tensor", lambda E_, nch=nch: E_.matmul(
                pk[2+nch][:], onr[0:1, :], rsrow[0:1, nch*512:(nch+1)*512],
                start=True, stop=True),
                  reads=["rsrow", "onr"], writes=[f"pk{2+nch}"])
        for dt_ in range(NT):
            pr.dma("sync", r0t[:], r0d[dt_], reads=[f"r0d{dt_}"], writes=["r0t"])
            for nch in range(2):
                pr.op("vector", lambda E_, nch=nch: E_.tensor_tensor(
                    x1t[:, nch*512:(nch+1)*512], r0t[:, nch*512:(nch+1)*512],
                    pk[nch][:], ALU.subtract),
                      reads=["r0t", f"pk{nch}"], writes=["x1t"])
                pr.op("vector", lambda E_, nch=nch: E_.tensor_tensor(
                    x1t[:, nch*512:(nch+1)*512], x1t[:, nch*512:(nch+1)*512],
                    pk[2+nch][:], ALU.mult),
                      reads=[f"pk{2+nch}"], writes=["x1t"])
            pr.dma("gpsimd", x4t[dt_], x1t[:], reads=["x1t"], writes=[f"x4o{dt_}"])
        pr.finalize(nc, block, sems)
    _L1_CACHE["ffn"] = nc
    return nc


_CONST_L0 = ["Fp", "Gcp", "GIp", "identb", "zerosb", "onesb", "onesrow", "selE",
             "a0p", "b0p", "wr0p"]


def kernel(**inputs):
    pp = _prep(inputs)
    cores = list(range(NCORE))

    nc = build_l0(debug=False)
    maps = []
    for c in cores:
        m = {k: np.asarray(pp[k]) for k in _CONST_L0}
        m["xfp"] = pp["xfp"][c]
        maps.append(m)
    r0 = run_bass_kernel_spmd(nc, maps, core_ids=cores).results
    x2t = [r0[c]["x2t"] for c in cores]

    nc = build_inproj()
    maps = [dict(x2t=x2t[c], wtip=pp["wtip"]) for c in cores]
    r2 = run_bass_kernel_spmd(nc, maps, core_ids=cores).results

    nc = build_conv()
    maps = []
    for c in cores:
        if c % 4 == 0:
            halo = np.zeros((48, 128, 3), np.float32)
        else:
            halo = np.ascontiguousarray(r2[c - 1]["hbc"][:, :, 1021:1024])
        maps.append(dict(hbc=r2[c]["hbc"], halo=halo, dt_i=r2[c]["dt_o"],
                         convw=pp["convw"], convb=pp["convb"], dtb=pp["dtb"],
                         alog=pp["alog"]))
    r3 = run_bass_kernel_spmd(nc, maps, core_ids=cores).results

    nc = build_scan(with_init=False)
    base_m = dict(maskc=pp["maskc"], identb=pp["identb"], identf=pp["identf"],
                  onesrow=pp["onesrow"], onesrowb=pp["onesrowb"],
                  onesb=pp["onesb"], drep=pp["drep"])
    maps = [dict(base_m, xbc=r3[c]["xbc"], dtp_i=r3[c]["dtp_o"],
                 scum_i=r3[c]["scum_o"]) for c in cores]
    r4a = run_bass_kernel_spmd(nc, maps, core_ids=cores).results

    nc = build_scan(with_init=True)
    maps = []
    for c in cores:
        g0 = 4 * (c // 4)
        ST4 = np.stack([r4a[g0 + k]["STout"] for k in range(4)])
        scl4 = np.stack([r3[g0 + k]["scum_o"][:, 1023:1024] for k in range(4)])
        fl = np.zeros((128, 3), np.float32)
        if c % 4 > 0:
            fl[:, c % 4 - 1] = 1.0
        maps.append(dict(base_m, xbc=r3[c]["xbc"], dtp_i=r3[c]["dtp_o"],
                         scum_i=r3[c]["scum_o"], ST4=ST4, scl4=scl4, flags4=fl,
                         nwp=pp["nwp"], gate=r2[c]["gate"], wtop=pp["wtop"],
                         x2t=x2t[c]))
    r4b = run_bass_kernel_spmd(nc, maps, core_ids=cores).results

    nc = build_ffn()
    maps = [dict(x3t=r4b[c]["x3t"], wtbp=pp["wtbp"], bbp=pp["bbp"],
                 a1p=pp["a1p"], b1p=pp["b1p"], r1p=pp["r1p"], selp=pp["selp"],
                 identb=pp["identb"], onesrow=pp["onesrow"], onesb=pp["onesb"])
            for c in cores]
    r5 = run_bass_kernel_spmd(nc, maps, core_ids=cores).results

    out = np.empty((2, 4096, 4096), np.float32)
    for c in cores:
        b, q = c // 4, c % 4
        out[b, q*1024:(q+1)*1024, :] = np.asarray(
            r5[c]["x4t"]).reshape(4096, 1024).T
    return out


# version 07
# speedup vs baseline: 1.1524x; 1.1524x over previous
import math

import numpy as np
import ml_dtypes
import concourse.bass as bass
import concourse.mybir as mybir
from concourse.bass_utils import run_bass_kernel_spmd

F32 = mybir.dt.float32
BF16 = mybir.dt.bfloat16
AF = mybir.ActivationFunctionType
ALU = mybir.AluOpType
AX = mybir.AxisListType

NCORE = 8
T = 1024
D = 4096
NT = 32
TT = 8
H, P, NS, G, KC = 64, 64, 128, 8, 4
I_ = H * P
GN = G * NS
CONV_DIM = I_ + 2 * GN
PROJ = I_ + CONV_DIM + H
CT_PROJ = 81
Q = 128
NCH = T // Q
EPS = 1e-5

bf16 = ml_dtypes.bfloat16


class Prog:
    ENGINES = ("sync", "gpsimd", "tensor", "vector", "scalar")
    SELF_ORDERED = {"tensor": True, "vector": False, "scalar": False,
                    "gpsimd": False, "sync": False}
    SEMS = ("sync", "gpsimd", "tensor", "vector", "scalar", "syncd", "gpsimdd")

    def __init__(self):
        self.items = {e: [] for e in self.ENGINES}
        self.count = {s: 0 for s in self.SEMS}
        self.last_w = {}
        self.readers = {}
        self.pending = {}
        self.known = {e: {s: 0 for s in self.SEMS} for e in self.ENGINES}

    def op(self, eng, fn, reads=(), writes=(), dma=False):
        sem = eng + "d" if dma else eng
        inc = 16 if dma else 1
        need = {}
        def _need(e2, s2, v2):
            if e2 == eng and s2 == eng and self.SELF_ORDERED[eng]:
                return
            need[s2] = max(need.get(s2, 0), v2)
        for r in reads:
            ev = self.last_w.get(r)
            if ev is not None:
                _need(*ev)
        for w in writes:
            ev = self.last_w.get(w)
            if ev is not None:
                _need(*ev)
            for (e2, s2), v2 in self.readers.get(w, {}).items():
                _need(e2, s2, v2)
        for s2, v2 in self.pending.pop(eng, {}).items():
            need[s2] = max(need.get(s2, 0), v2)
        if dma:
            need[sem] = max(need.get(sem, 0), self.count[sem])
        kn = self.known[eng]
        waits = {s2: v2 for s2, v2 in need.items() if v2 > kn[s2]}
        kn.update(waits)
        self.count[sem] += inc
        val = self.count[sem]
        self.items[eng].append((waits, fn, sem, inc))
        for w in writes:
            self.last_w[w] = (eng, sem, val)
            self.readers[w] = {}
        for r in reads:
            d = self.readers.setdefault(r, {})
            k2 = (eng, sem)
            d[k2] = max(d.get(k2, 0), val)
        return (eng, sem, val)

    def barrier(self):
        snapshot = dict(self.count)
        for e in self.ENGINES:
            kn = self.known[e]
            for s in self.SEMS:
                if snapshot[s] > kn[s]:
                    self.pending.setdefault(e, {})[s] = max(
                        self.pending.get(e, {}).get(s, 0), snapshot[s])

    def dma(self, eng, out, in_, reads=(), writes=(), **kw):
        return self.op(eng, lambda E, o=out, i=in_, kw=kw: E.dma_start(out=o, in_=i, **kw),
                       reads=reads, writes=writes, dma=True)

    def finalize(self, nc, block, sems):
        counts = dict(self.count)
        def emit(name):
            def inner(eng):
                for waits, fn, sem, inc in self.items[name]:
                    for s2, v2 in sorted(waits.items()):
                        eng.wait_ge(sems[s2], v2)
                    inst = fn(eng)
                    inst.then_inc(sems[sem], inc)
                if name == "vector":
                    for s2 in self.SEMS:
                        if s2 != "vector" and counts[s2] > 0:
                            eng.wait_ge(sems[s2], counts[s2])
            return inner
        if self.items["sync"]:
            block.sync(emit("sync"))
        block.vector(emit("vector"))
        if self.items["scalar"]:
            block.scalar(emit("scalar"))
        if self.items["tensor"]:
            block.tensor(emit("tensor"))
        if self.items["gpsimd"]:
            block.gpsimd(emit("gpsimd"))


_PREP = {}

def _dft_tables():
    N = 4096
    f = np.arange(N, dtype=np.float64)
    k = np.arange(2049, dtype=np.float64)
    ang = np.outer(f, k) * (2 * np.pi / N)
    Fp = np.empty((N, N), np.float64)
    Fp[:, :2049] = np.cos(ang)
    Fp[:, 2049:] = -np.sin(ang[:, 1:2048])
    ck = np.full(2049, 2.0 / N); ck[0] = 1.0 / N; ck[2048] = 1.0 / N
    d = np.arange(N, dtype=np.float64)
    ang2 = np.outer(k, d) * (2 * np.pi / N)
    Gc = np.zeros((2176, N), np.float64)
    Gc[:2049] = ck[:, None] * np.cos(ang2)
    GI = np.empty((N, N), np.float64)
    GI[:2049] = ck[:, None] * np.cos(ang2)
    GI[2049:] = -(ck[1:2048, None]) * np.sin(ang2[1:2048])
    return Fp, Gc, GI

def _tile_kpm(W):
    Kin, Mout = W.shape
    ct = (Mout + 127) // 128
    nt = Kin // 128
    Wp = np.zeros((Kin, ct * 128), W.dtype)
    Wp[:, :Mout] = W
    return np.ascontiguousarray(
        Wp.reshape(nt, 128, ct, 128).transpose(2, 1, 0, 3))

def _prep(inputs):
    key = id(inputs.get("L0_attn_x"))
    if _PREP.get("key") == key:
        return _PREP
    _PREP.clear()
    _PREP["key"] = key
    f32 = np.float32

    Fp, Gc, GI = _dft_tables()
    _PREP["Fp"] = np.ascontiguousarray(
        Fp.reshape(32, 128, 8, 512).transpose(2, 1, 0, 3)).astype(bf16)
    _PREP["Gcp"] = np.ascontiguousarray(
        Gc.reshape(17, 128, 8, 512).transpose(2, 1, 0, 3)).astype(bf16)
    _PREP["GIp"] = np.ascontiguousarray(
        GI.reshape(32, 128, 32, 128).transpose(2, 1, 0, 3)).astype(bf16)
    _PREP["identb"] = np.eye(128, dtype=bf16)
    _PREP["identf"] = np.eye(128, dtype=f32)
    _PREP["zerosb"] = np.zeros((128, 128), bf16)
    _PREP["onesb"] = np.ones((128, 1), bf16)
    _PREP["onesrow"] = np.ones((1, 128), f32)
    _PREP["onesrowb"] = np.ones((1, 128), bf16)
    mk = np.zeros((128, 128), f32)
    tt_, tau_ = np.meshgrid(np.arange(128), np.arange(128))
    mk[tau_ > tt_] = -1e9
    _PREP["maskc"] = mk
    selp = np.zeros((16, 8, 128), bf16)
    for pr_ in range(8):
        selp[2*pr_, pr_, 0:64] = bf16(1.0)
        selp[2*pr_+1, pr_, 64:128] = bf16(1.0)
    _PREP["selp"] = selp
    selE = np.zeros((4, 64), bf16)
    for e in range(4):
        selE[e, 16*e:16*e+16] = bf16(4.0)
    _PREP["selE"] = selE

    A0 = inputs["L0_ffn_A"].astype(f32)
    a0t = np.ascontiguousarray(A0.transpose(2, 0, 1).reshape(4096, 64))
    _PREP["a0p"] = np.ascontiguousarray(
        a0t.reshape(32, 128, 64).transpose(1, 0, 2))
    B0 = inputs["L0_ffn_B"].astype(f32)
    b0p = np.zeros((64, 32, 128), f32)
    for pair in range(2):
        for ep in range(2):
            e = 2 * pair + ep
            blk = B0[e].reshape(32, 128, 16).transpose(0, 2, 1)
            b0p[32*pair+16*ep:32*pair+16*ep+16] = blk.transpose(1, 0, 2)
    _PREP["b0p"] = np.ascontiguousarray(b0p)
    Wr0 = inputs["L0_ffn_Wr"].astype(f32)
    _PREP["wr0p"] = np.ascontiguousarray(
        Wr0.T.reshape(32, 128, 4).transpose(1, 0, 2))

    _PREP["wtip"] = _tile_kpm(np.ascontiguousarray(
        inputs["L1_attn_in_proj_w"].astype(f32).T))
    _PREP["wtop"] = _tile_kpm(np.ascontiguousarray(
        inputs["L1_attn_out_proj_w"].astype(f32).T))
    _PREP["convw"] = inputs["L1_attn_conv_weight"].astype(f32).reshape(48, 128, 4)
    _PREP["convb"] = inputs["L1_attn_conv_bias"].astype(f32).reshape(48, 128, 1)
    _PREP["dtb"] = inputs["L1_attn_dt_bias"].astype(f32).reshape(64, 1)
    _PREP["alog"] = inputs["L1_attn_A_log"].astype(f32).reshape(64, 1)
    _PREP["drep"] = np.ascontiguousarray(
        np.repeat(inputs["L1_attn_D"].astype(f32), 64).reshape(32, 128).T)
    _PREP["nwp"] = np.ascontiguousarray(
        inputs["L1_attn_norm_weight"].astype(f32).reshape(32, 128).T)

    _PREP["wtbp"] = _tile_kpm(np.ascontiguousarray(
        inputs["L1_ffn_base_w"].astype(f32).T))
    _PREP["bbp"] = np.ascontiguousarray(
        inputs["L1_ffn_base_b"].astype(f32).reshape(32, 128).T)
    A1 = inputs["L1_ffn_A"].astype(f32)
    _PREP["a1p"] = np.ascontiguousarray(A1.T.reshape(32, 128, 64).transpose(1, 0, 2))
    B1 = inputs["L1_ffn_B"].astype(f32)
    b1p = np.zeros((8, 32, 128, 128), f32)
    for pair in range(8):
        for ep in range(2):
            e = 2 * pair + ep
            blk = B1[e].reshape(32, 128, 64).transpose(0, 2, 1)
            b1p[pair, :, 64*ep:64*ep+64, :] = blk
    _PREP["b1p"] = np.ascontiguousarray(b1p.transpose(1, 2, 0, 3))
    _PREP["r1p"] = np.ascontiguousarray(
        inputs["L1_ffn_router_w"].astype(f32).T.reshape(32, 128, 16).transpose(1, 0, 2))

    x = inputs["L0_attn_x"].astype(f32)
    xfp = []
    for c in range(NCORE):
        b, q = c // 4, c % 4
        xs = np.ascontiguousarray(x[b, q*1024:(q+1)*1024, :].T)
        xfp.append(np.ascontiguousarray(
            xs.reshape(32, 128, 1024).transpose(1, 0, 2)))
    _PREP["xfp"] = xfp
    return _PREP


_L1_CACHE = {}

def emit_l0(nc, pr, io, debug=False):
    def dint(name, shape, dt):
        return nc.dram_tensor(name, list(shape), dt, kind="Internal").ap()
    xfp = io["xfp"]; Fp = io["Fp"]; Gcp = io["Gcp"]; GIp = io["GIp"]
    identb = io["identb"]; zerosb = io["zerosb"]; onesb = io["onesb"]
    onesrow = io["onesrow"]; selE = io["selE"]
    a0p = io["a0p"]; b0p = io["b0p"]; wr0p = io["wr0p"]
    x2t = io["x2t"]
    if debug:
        dbg_x1 = io["dbg_x1"]; dbg_corr = io["dbg_corr"]; dbg_s = io["dbg_s"]

    Xd = dint("l0_Xd", (8, 128, 4096), BF16)
    Sd = dint("l0_Sd", (8, 128, 4096), BF16)
    r0d = dint("l0_r0d", (32, 128, 1024), F32)
    x1d = dint("l0_x1d", (32, 128, 1024), F32)

    from contextlib import ExitStack
    if True:
      with ExitStack() as ctx:
        E = ctx.enter_context
        ar1 = E(nc.sbuf_tensor("ar1", [128, 32 * 1024], BF16))
        ar2 = E(nc.sbuf_tensor("ar2", [128, 32 * 1024], BF16))
        ar3f = E(nc.sbuf_tensor("ar3f", [128, 9280], F32))
        ar3b = ar3f[:].bitcast(BF16)
        xbf = ar1[:]
        corr = ar1[:]
        pT = ar2[:, 0:17408]
        sT = ar2[:]
        x1bf = ar2[:]
        fbuf = [ar3b[:, 0:16384], ar3b[:, 0:16384]]
        gcb = [ar3b[:, 0:8704], ar3b[:, 0:8704]]
        gib = [ar3b[:, 0:4096], ar3b[:, 4096:8192]]
        pfa = ar3f[:, 0:2049]
        pfb = ar3f[:, 2049:4096]
        xt_B1 = ar3b[:, 8192:12288]
        pbf = ar3b[:, 12288:14464]
        e_f = ar3f[:, 0:4096]
        mask_f = ar3f[:, 4096:8192]
        xt_D = ar3b[:, 0:4096]
        st_ = ar3b[:, 4096:8192]
        pfaD = ar3f[:, 4096:6145]
        pfbD = ar3f[:, 6145:8192]
        a0sb = ar3b[:, 0:2048]
        b0sb = ar3b[0:64, 2048:6144]
        wr0sb = ar3b[:, 6144:6272]
        gT = ar3b[0:4, 6272:7296]
        mid_f = ar3f[0:64, 3648:4672]
        midg = ar3b[0:64, 9344:11392]
        evb = [E(nc.sbuf_tensor(f"evb{i}", [128, 512], BF16)) for i in range(4)]
        m8b = E(nc.sbuf_tensor("m8b", [128, 8], BF16))
        m8f = E(nc.sbuf_tensor("m8f", [128, 8], F32))
        e8f = E(nc.sbuf_tensor("e8f", [128, 8], F32))
        negm1 = E(nc.sbuf_tensor("negm1", [128, 1], F32))
        theta = E(nc.sbuf_tensor("theta", [128, 1], F32))
        zrow = E(nc.sbuf_tensor("zrow", [128, 1], F32))
        rz = E(nc.sbuf_tensor("rz", [128, 1], F32))
        ones_b = E(nc.sbuf_tensor("ones_b", [128, 1], BF16))
        ones_r = E(nc.sbuf_tensor("ones_r", [1, 128], F32))
        ident_b = E(nc.sbuf_tensor("ident_b", [128, 128], BF16))
        xfstr = E(nc.sbuf_tensor("xfstr", [128, 1024], F32))
        r0bf = E(nc.sbuf_tensor("r0bf", [128, 1024], BF16))
        sqbf = E(nc.sbuf_tensor("sqbf", [128, 1024], BF16))
        r0t = E(nc.sbuf_tensor("r0t", [128, 1024], F32))
        x1t = E(nc.sbuf_tensor("x1t", [128, 1024], F32))
        mrow = E(nc.sbuf_tensor("mrow", [1, 1024], F32))
        vrow = E(nc.sbuf_tensor("vrow", [1, 1024], F32))
        rsrow = E(nc.sbuf_tensor("rsrow", [1, 1024], F32))
        sel_sb = E(nc.sbuf_tensor("sel_sb", [4, 64], BF16))
        logit = E(nc.sbuf_tensor("logit", [128, 8 * 4], F32))
        gwork = E(nc.sbuf_tensor("gwork", [128, 4], F32))
        gmax = E(nc.sbuf_tensor("gmax", [128, 1], F32))
        gmax2 = E(nc.sbuf_tensor("gmax2", [128, 1], F32))
        gz = E(nc.sbuf_tensor("gz", [128, 1], F32))
        g4 = E(nc.sbuf_tensor("g4", [128, 8 * 4], F32))
        g4b = E(nc.sbuf_tensor("g4b", [128, 4], BF16))
        eps1 = E(nc.sbuf_tensor("eps1", [1, 1], F32))

        ps = [E(nc.psum_tensor(f"ps{i}", [128, 512], F32)) for i in range(4)]
        psTb = [ps[2][:].bitcast(BF16)[:, 0:128], ps[3][:].bitcast(BF16)[:, 0:128]]
        psS1 = E(nc.psum_tensor("psS1", [1, 1024], F32))
        psS2 = E(nc.psum_tensor("psS2", [1, 1024], F32))

        def ckpt(k):
            pass

        pr.op("vector", lambda E_: E_.memset(eps1[:], EPS), writes=["eps1"])
        pr.dma("sync", ones_b[:], onesb, writes=["ones_b"])
        pr.dma("sync", ones_r[:], onesrow, writes=["ones_r"])
        pr.dma("sync", ident_b[:], identb, writes=["ident_b"])
        pr.dma("sync", sel_sb[:], selE, writes=["sel_sb"])
        pr.dma("gpsimd", xbf[:], xfp.rearrange("p ft t -> p (ft t)"), writes=["xbf"])

        for kc in range(8):
            fb = fbuf[kc % 2]
            pr.dma("sync", fb[:], Fp[kc].rearrange("p ft k -> p (ft k)"),
                   reads=[], writes=["fbuf"])
            for half in range(2):
                for tt4 in range(4):
                    tt = half * 4 + tt4
                    pk = ps[tt4]
                    for ft in range(NT):
                        pr.op("tensor",
                              lambda E_, tt=tt, ft=ft, pk=pk, fb=fb, kc=kc:
                              E_.matmul(pk[:], xbf[:, ft*1024 + tt*128: ft*1024 + tt*128 + 128],
                                        fb[:, ft*512: ft*512 + 512],
                                        start=(ft == 0), stop=(ft == NT - 1)),
                              reads=["xbf", "fbuf"], writes=[f"ps{tt4}"])
                    ev = evb[tt4]
                    pr.op("scalar", lambda E_, pk=pk, ev=ev: E_.copy(ev[:], pk[:]),
                          reads=[f"ps{tt4}"], writes=[f"evb{tt4}"])
                    pr.dma("gpsimd", Xd[tt][:, kc*512:(kc+1)*512], ev[:],
                           reads=[f"evb{tt4}"], writes=[f"Xd{tt}{kc}"])

        ckpt(1)
        pr.barrier()
        pr.dma("sync", pbf[:, 2048:2176], zerosb[:, 0:128], writes=["pbf_pad"])
        for tt in range(8):
            pr.dma("sync", xt_B1[:], Xd[tt],
                   reads=[f"Xd{tt}{k}" for k in range(8)], writes=["xt_"])
            pr.op("scalar", lambda E_: E_.activation(pfa[:], xt_B1[:, 0:2049], AF.Square),
                  reads=["xt_"], writes=["pfa"])
            pr.op("scalar", lambda E_: E_.activation(pfb[:], xt_B1[:, 2049:4096], AF.Square),
                  reads=["xt_"], writes=["pfb"])
            pr.op("vector", lambda E_: E_.tensor_copy(pbf[:, 0:1], pfa[:, 0:1]),
                  reads=["pfa"], writes=["pbf_c0"])
            pr.op("vector", lambda E_: E_.tensor_copy(pbf[:, 2048:2049], pfa[:, 2048:2049]),
                  reads=["pfa"], writes=["pbf_c2048"])
            pr.op("vector", lambda E_: E_.tensor_tensor(pbf[:, 1:2048], pfa[:, 1:2048],
                                                        pfb[:], ALU.add),
                  reads=["pfa", "pfb"], writes=["pbf"])
            for kt in range(17):
                pb = psTb[kt % 2]
                pr.op("tensor", lambda E_, kt=kt, pb=pb:
                      E_.transpose(pb[:], pbf[:, kt*128:(kt+1)*128], ident_b[:]),
                      reads=["pbf", "pbf_c0", "pbf_c2048", "pbf_pad", "ident_b"], writes=[f"ps{2+kt%2}"])
                pr.op("scalar", lambda E_, tt=tt, kt=kt, pb=pb:
                      E_.copy(pT[:, tt*2176 + kt*128: tt*2176 + kt*128 + 128], pb[:]),
                      reads=[f"ps{2+kt%2}"], writes=[f"pT{tt}"])

        ckpt(2)
        pr.barrier()
        for dc in range(8):
            gb = gcb[dc % 2]
            pr.dma("sync", gb[:], Gcp[dc].rearrange("p kt k -> p (kt k)"),
                   writes=["gcb"])
            for half in range(2):
                for tt4 in range(4):
                    tt = half * 4 + tt4
                    pk = ps[tt4]
                    for kt in range(17):
                        pr.op("tensor", lambda E_, tt=tt, kt=kt, pk=pk, gb=gb:
                              E_.matmul(pk[:], pT[:, tt*2176 + kt*128: tt*2176 + kt*128 + 128],
                                        gb[:, kt*512:(kt+1)*512],
                                        start=(kt == 0), stop=(kt == 16)),
                              reads=[f"pT{tt}", "gcb"], writes=[f"ps{tt4}"])
                    pr.op("scalar", lambda E_, tt=tt, dc=dc, pk=pk:
                          E_.copy(corr[:, tt*4096 + dc*512: tt*4096 + dc*512 + 512], pk[:]),
                          reads=[f"ps{tt4}"], writes=[f"corr{tt}"])

        ckpt(3)
        pr.barrier()
        for tt in range(8):
            c_sl = corr[:, tt*4096:(tt+1)*4096]
            if debug:
                pr.dma("gpsimd", dbg_corr[tt], c_sl, reads=[f"corr{tt}"],
                       writes=[f"dbgc{tt}"])
            pr.op("vector", lambda E_, c_sl=c_sl: E_.max(out=m8b[:], in_=c_sl),
                  reads=[f"corr{tt}"], writes=["m8b"])
            pr.op("vector", lambda E_: E_.tensor_copy(m8f[:], m8b[:]),
                  reads=["m8b"], writes=["m8f"])
            pr.op("vector", lambda E_: E_.tensor_scalar(negm1[:], m8f[:, 0:1], -1.0, None,
                                                        op0=ALU.mult),
                  reads=["m8f"], writes=["negm1"])
            pr.op("vector", lambda E_: E_.tensor_copy(theta[:], m8f[:, 7:8]),
                  reads=["m8f"], writes=["theta"])
            pr.op("scalar", lambda E_: E_.activation(e8f[:], m8f[:], AF.Exp,
                                                     bias=negm1[:], accum_out=zrow[:]),
                  reads=["m8f", "negm1"], writes=["e8f", "zrow"])
            pr.op("vector", lambda E_: E_.reciprocal(rz[:], zrow[:]),
                  reads=["zrow"], writes=["rz"])
            pr.op("scalar", lambda E_, c_sl=c_sl: E_.activation(e_f[:], c_sl, AF.Exp,
                                                                bias=negm1[:]),
                  reads=[f"corr{tt}", "negm1"], writes=["e_f"])
            pr.op("vector", lambda E_, c_sl=c_sl: E_.tensor_scalar(
                mask_f[:], c_sl, theta[:], None, op0=ALU.is_ge),
                  reads=[f"corr{tt}", "theta"], writes=["mask_f"])
            pr.op("vector", lambda E_, c_sl=c_sl: E_.scalar_tensor_tensor(
                c_sl, e_f[:], rz[:], mask_f[:], op0=ALU.mult, op1=ALU.mult),
                  reads=["e_f", "rz", "mask_f"], writes=[f"corr{tt}"])
            if debug:
                pr.dma("gpsimd", dbg_s[tt], c_sl, reads=[f"corr{tt}"],
                       writes=[f"dbgs{tt}"])
            for ft in range(NT):
                pb = psTb[ft % 2]
                pr.op("tensor", lambda E_, ft=ft, pb=pb, c_sl=c_sl:
                      E_.transpose(pb[:], c_sl[:, ft*128:(ft+1)*128], ident_b[:]),
                      reads=[f"corr{tt}", "ident_b"], writes=[f"ps{2+ft%2}"])
                pr.op("scalar", lambda E_, tt=tt, ft=ft, pb=pb:
                      E_.copy(sT[:, tt*4096 + ft*128: tt*4096 + ft*128 + 128], pb[:]),
                      reads=[f"ps{2+ft%2}"], writes=[f"sT{tt}", "sTbuf"])

        ckpt(4)
        pr.barrier()
        for kc in range(8):
            fb = fbuf[kc % 2]
            pr.dma("sync", fb[:], Fp[kc].rearrange("p ft k -> p (ft k)"),
                   writes=["fbuf"])
            for half in range(2):
                for tt4 in range(4):
                    tt = half * 4 + tt4
                    pk = ps[tt4]
                    for ft in range(NT):
                        pr.op("tensor", lambda E_, tt=tt, ft=ft, pk=pk, fb=fb:
                              E_.matmul(pk[:], sT[:, tt*4096 + ft*128: tt*4096 + ft*128 + 128],
                                        fb[:, ft*512: ft*512 + 512],
                                        start=(ft == 0), stop=(ft == NT - 1)),
                              reads=[f"sT{tt}", "sTbuf", "fbuf"], writes=[f"ps{tt4}"])
                    ev = evb[tt4]
                    pr.op("scalar", lambda E_, pk=pk, ev=ev: E_.copy(ev[:], pk[:]),
                          reads=[f"ps{tt4}"], writes=[f"evb{tt4}"])
                    pr.dma("gpsimd", Sd[tt][:, kc*512:(kc+1)*512], ev[:],
                           reads=[f"evb{tt4}"], writes=[f"Sd{tt}{kc}"])

        ckpt(5)
        pr.barrier()
        for tt in range(8):
            y_sl = corr[:, tt*4096:(tt+1)*4096]
            pr.dma("sync", xt_D[:], Xd[tt],
                   reads=[f"Xd{tt}{k}" for k in range(8)], writes=["xt_"])
            pr.dma("sync", st_[:], Sd[tt],
                   reads=[f"Sd{tt}{k}" for k in range(8)], writes=["st_"])
            pr.op("vector", lambda E_, y=y_sl: E_.tensor_tensor(
                y[:, 0:2049], xt_D[:, 0:2049], st_[:, 0:2049], ALU.mult),
                  reads=["xt_", "st_"], writes=[f"corr{tt}"])
            pr.op("vector", lambda E_: E_.tensor_tensor(
                pfbD[:], xt_D[:, 2049:4096], st_[:, 2049:4096], ALU.mult),
                  reads=["xt_", "st_"], writes=["pfb"])
            pr.op("vector", lambda E_, y=y_sl: E_.tensor_tensor(
                y[:, 1:2048], y[:, 1:2048], pfbD[:], ALU.add),
                  reads=["pfb"], writes=[f"corr{tt}"])
            pr.op("vector", lambda E_: E_.tensor_tensor(
                pfaD[:, 0:2047], xt_D[:, 2049:4096], st_[:, 1:2048], ALU.mult),
                  reads=["xt_", "st_"], writes=["pfa"])
            pr.op("vector", lambda E_: E_.tensor_tensor(
                pfbD[:], xt_D[:, 1:2048], st_[:, 2049:4096], ALU.mult),
                  reads=["xt_", "st_"], writes=["pfb"])
            pr.op("vector", lambda E_, y=y_sl: E_.tensor_tensor(
                y[:, 2049:4096], pfaD[:, 0:2047], pfbD[:], ALU.subtract),
                  reads=["pfa", "pfb"], writes=[f"corr{tt}"])
            for st in range(NT):
                pb = psTb[st % 2]
                pr.op("tensor", lambda E_, st=st, pb=pb, y=y_sl:
                      E_.transpose(pb[:], y[:, st*128:(st+1)*128], ident_b[:]),
                      reads=[f"corr{tt}", "ident_b"], writes=[f"ps{2+st%2}"])
                pr.op("scalar", lambda E_, tt=tt, st=st, pb=pb:
                      E_.copy(sT[:, st*1024 + tt*128: st*1024 + tt*128 + 128], pb[:]),
                      reads=[f"ps{2+st%2}"], writes=[f"yT{st}", "sTbuf"])

        ckpt(6)
        pr.barrier()
        for dt_ in range(NT):
            gb = gib[dt_ % 2]
            pr.dma("sync", gb[:], GIp[dt_].rearrange("p st j -> p (st j)"),
                   writes=[f"gib{dt_%2}"])
            pr.dma("sync", xfstr[:], xfp[:, dt_, :], writes=["xfstr"])
            for nch in range(2):
                pk = ps[nch]
                for st in range(NT):
                    pr.op("tensor", lambda E_, st=st, nch=nch, pk=pk, gb=gb:
                          E_.matmul(pk[:], gb[:, st*128:(st+1)*128],
                                    sT[:, st*1024 + nch*512: st*1024 + nch*512 + 512],
                                    start=(st == 0), stop=(st == NT - 1)),
                          reads=[f"gib{dt_%2}", f"yT{st}", "sTbuf"],
                          writes=[f"ps{nch}"])
                pr.op("vector", lambda E_, nch=nch, pk=pk: E_.tensor_tensor(
                    r0t[:, nch*512:(nch+1)*512], pk[:], xfstr[:, nch*512:(nch+1)*512],
                    ALU.add),
                      reads=[f"ps{nch}", "xfstr"], writes=["r0t"])
            pr.dma("gpsimd", r0d[dt_], r0t[:], reads=["r0t"], writes=[f"r0d{dt_}"])
            pr.op("vector", lambda E_: E_.tensor_copy(r0bf[:], r0t[:]),
                  reads=["r0t"], writes=["r0bf"])
            pr.op("scalar", lambda E_: E_.activation(sqbf[:], r0t[:], AF.Square),
                  reads=["r0t"], writes=["sqbf"])
            for nch in range(2):
                pr.op("tensor", lambda E_, nch=nch, dt_=dt_:
                      E_.matmul(psS1[0:1, nch*512:(nch+1)*512], ones_b[:],
                                r0bf[:, nch*512:(nch+1)*512],
                                start=(dt_ == 0), stop=(dt_ == NT - 1)),
                      reads=["r0bf", "ones_b"], writes=["psS1"])
                pr.op("tensor", lambda E_, nch=nch, dt_=dt_:
                      E_.matmul(psS2[0:1, nch*512:(nch+1)*512], ones_b[:],
                                sqbf[:, nch*512:(nch+1)*512],
                                start=(dt_ == 0), stop=(dt_ == NT - 1)),
                      reads=["sqbf", "ones_b"], writes=["psS2"])

        def ln_rows(tag):
            pr.op("vector", lambda E_: E_.tensor_scalar(
                mrow[:], psS1[:], 1.0 / 4096.0, None, op0=ALU.mult),
                  reads=["psS1"], writes=["mrow", "srowH"])
            pr.op("scalar", lambda E_: E_.activation(vrow[:], mrow[:], AF.Square),
                  reads=["mrow"], writes=["vrow", "srowH"])
            pr.op("vector", lambda E_: E_.scalar_tensor_tensor(
                vrow[:], psS2[:], 1.0 / 4096.0, vrow[:], op0=ALU.mult, op1=ALU.subtract),
                  reads=["psS2", "vrow"], writes=["vrow", "srowH"])
            pr.op("scalar", lambda E_: E_.activation(vrow[:], vrow[:], AF.Ln, bias=eps1[:]),
                  reads=["vrow", "eps1"], writes=["vrow", "srowH"])
            pr.op("scalar", lambda E_: E_.activation(rsrow[:], vrow[:], AF.Exp, scale=-0.5),
                  reads=["vrow"], writes=["rsrow", "srowH"])

        def ln_bcast():
            for nch in range(2):
                pr.op("tensor", lambda E_, nch=nch: E_.matmul(
                    ps[nch][:], ones_r[0:1, :], mrow[0:1, nch*512:(nch+1)*512],
                    start=True, stop=True),
                      reads=["mrow", "ones_r"], writes=[f"ps{nch}"])
                pr.op("tensor", lambda E_, nch=nch: E_.matmul(
                    ps[2+nch][:], ones_r[0:1, :], rsrow[0:1, nch*512:(nch+1)*512],
                    start=True, stop=True),
                      reads=["rsrow", "ones_r"], writes=[f"ps{2+nch}"])

        ln_rows("ln1")
        ln_bcast()
        ckpt(7)
        pr.barrier()
        for dt_ in range(NT):
            pr.dma("sync", r0t[:], r0d[dt_], reads=[f"r0d{dt_}"], writes=["r0t"])
            for nch in range(2):
                pr.op("vector", lambda E_, nch=nch: E_.tensor_tensor(
                    x1t[:, nch*512:(nch+1)*512], r0t[:, nch*512:(nch+1)*512],
                    ps[nch][:], ALU.subtract),
                      reads=["r0t", f"ps{nch}"], writes=["x1t"])
                pr.op("vector", lambda E_, nch=nch: E_.tensor_tensor(
                    x1t[:, nch*512:(nch+1)*512], x1t[:, nch*512:(nch+1)*512],
                    ps[2+nch][:], ALU.mult),
                      reads=[f"ps{2+nch}"], writes=["x1t"])
            pr.dma("gpsimd", x1d[dt_], x1t[:], reads=["x1t"], writes=[f"x1d{dt_}"])
            if debug:
                pr.dma("gpsimd", dbg_x1[dt_], x1t[:], reads=["x1t"], writes=[f"dbgx1{dt_}"])
            pr.op("vector", lambda E_, dt_=dt_: E_.tensor_copy(
                x1bf[:, dt_*1024:(dt_+1)*1024], x1t[:]),
                  reads=["x1t"], writes=[f"x1bf{dt_}"])

        ckpt(8)
        pr.barrier()
        pr.dma("gpsimd", a0sb[:], a0p.rearrange("p ft j -> p (ft j)"), writes=["a0sb"])
        pr.dma("gpsimd", b0sb[:], b0p.rearrange("r ot j -> r (ot j)"), writes=["b0sb"])
        pr.dma("gpsimd", wr0sb[:], wr0p.rearrange("p ft j -> p (ft j)"), writes=["wr0sb"])
        for tt in range(8):
            for ft in range(NT):
                pr.op("tensor", lambda E_, tt=tt, ft=ft: E_.matmul(
                    ps[0][:, 0:4], x1bf[:, ft*1024 + tt*128: ft*1024 + tt*128 + 128],
                    wr0sb[:, ft*4:(ft+1)*4], start=(ft == 0), stop=(ft == NT - 1)),
                      reads=[f"x1bf{ft}", "wr0sb"], writes=["ps0"])
            pr.op("scalar", lambda E_, tt=tt: E_.copy(logit[:, tt*4:(tt+1)*4],
                                                      ps[0][:, 0:4]),
                  reads=["ps0"], writes=["logit"])
        ckpt(9)
        for tt in range(8):
            lg = logit[:, tt*4:(tt+1)*4]
            pr.op("vector", lambda E_, lg=lg: E_.tensor_reduce(
                gmax[:], lg, AX.X, ALU.max), reads=["logit"], writes=["gmax"])
            pr.op("vector", lambda E_, lg=lg: E_.tensor_scalar(
                gwork[:], lg, gmax[:], -1e30, op0=ALU.is_ge, op1=ALU.mult),
                  reads=["logit", "gmax"], writes=["gwork"])
            pr.op("vector", lambda E_, lg=lg: E_.tensor_tensor(
                gwork[:], gwork[:], lg, ALU.add), reads=["logit"], writes=["gwork"])
            pr.op("vector", lambda E_: E_.tensor_reduce(
                gmax2[:], gwork[:], AX.X, ALU.max), reads=["gwork"], writes=["gmax2"])
            pr.op("vector", lambda E_: E_.tensor_scalar(
                negm1[:], gmax[:], -1.0, None, op0=ALU.mult),
                  reads=["gmax"], writes=["negm1"])
            pr.op("scalar", lambda E_, lg=lg, tt=tt: E_.activation(
                g4[:, tt*4:(tt+1)*4], lg, AF.Exp, bias=negm1[:]),
                  reads=["logit", "negm1"], writes=["g4"])
            pr.op("vector", lambda E_, lg=lg: E_.tensor_scalar(
                gwork[:], lg, gmax2[:], None, op0=ALU.is_ge),
                  reads=["logit", "gmax2"], writes=["gwork"])
            pr.op("vector", lambda E_, tt=tt: E_.tensor_tensor(
                g4[:, tt*4:(tt+1)*4], g4[:, tt*4:(tt+1)*4], gwork[:], ALU.mult),
                  reads=["gwork"], writes=["g4"])
            pr.op("vector", lambda E_, tt=tt: E_.tensor_reduce(
                gz[:], g4[:, tt*4:(tt+1)*4], AX.X, ALU.add),
                  reads=["g4"], writes=["gz"])
            pr.op("vector", lambda E_: E_.reciprocal(rz[:], gz[:]),
                  reads=["gz"], writes=["rz"])
            pr.op("vector", lambda E_, tt=tt: E_.tensor_scalar(
                g4[:, tt*4:(tt+1)*4], g4[:, tt*4:(tt+1)*4], rz[:], None, op0=ALU.mult),
                  reads=["rz"], writes=["g4"])
            pr.op("vector", lambda E_, tt=tt: E_.tensor_copy(
                g4b[:], g4[:, tt*4:(tt+1)*4]), reads=["g4"], writes=["g4b"])
            pr.op("tensor", lambda E_: E_.transpose(
                psTb[0][0:4, 0:128], g4b[:], ident_b[:]),
                  reads=["g4b", "ident_b"], writes=["ps2"])
            pr.op("scalar", lambda E_, tt=tt: E_.copy(
                gT[0:4, tt*128:(tt+1)*128], psTb[0][0:4, 0:128]),
                  reads=["ps2"], writes=["gT"])
        ckpt(10)
        for nch in range(2):
            pr.op("tensor", lambda E_, nch=nch: E_.matmul(
                ps[2][0:64, :], sel_sb[:], gT[:, nch*512:(nch+1)*512],
                start=True, stop=True),
                  reads=["sel_sb", "gT"], writes=["ps2"])
            for ft in range(NT):
                pr.op("tensor", lambda E_, ft=ft, nch=nch: E_.matmul(
                    ps[3][0:64, :], a0sb[:, ft*64:(ft+1)*64],
                    x1bf[:, ft*1024 + nch*512: ft*1024 + nch*512 + 512],
                    start=(ft == 0), stop=(ft == NT - 1)),
                      reads=["a0sb", f"x1bf{ft}"], writes=["ps3"])
            pr.op("scalar", lambda E_, nch=nch: E_.copy(
                mid_f[:, nch*512:(nch+1)*512], ps[3][0:64, :]),
                  reads=["ps3"], writes=["mid_f"])
            pr.op("vector", lambda E_, nch=nch: E_.tensor_tensor(
                midg[:, nch*512:(nch+1)*512], mid_f[:, nch*512:(nch+1)*512],
                ps[2][0:64, :], ALU.mult),
                  reads=["mid_f", "ps2"], writes=["midg"])

        ckpt(11)
        for ot in range(NT):
            pr.dma("sync", xfstr[:], x1d[ot], reads=[f"x1d{ot}"], writes=["xfstr"])
            for nch in range(2):
                pk = ps[nch]
                pr.op("tensor", lambda E_, ot=ot, nch=nch, pk=pk:
                      E_.matmul(pk[:], b0sb[0:64, ot*128:(ot+1)*128],
                                midg[0:64, nch*512:(nch+1)*512],
                                start=True, stop=True),
                      reads=["b0sb", "midg"], writes=[f"ps{nch}"])
                pr.op("vector", lambda E_, nch=nch, pk=pk: E_.tensor_tensor(
                    r0t[:, nch*512:(nch+1)*512], pk[:], xfstr[:, nch*512:(nch+1)*512],
                    ALU.add),
                      reads=[f"ps{nch}", "xfstr"], writes=["r0t"])
            pr.dma("gpsimd", r0d[ot], r0t[:], reads=["r0t"], writes=[f"r0d{ot}"])
            pr.op("vector", lambda E_: E_.tensor_copy(r0bf[:], r0t[:]),
                  reads=["r0t"], writes=["r0bf"])
            pr.op("scalar", lambda E_: E_.activation(sqbf[:], r0t[:], AF.Square),
                  reads=["r0t"], writes=["sqbf"])
            for nch in range(2):
                pr.op("tensor", lambda E_, nch=nch, ot=ot: E_.matmul(
                    psS1[0:1, nch*512:(nch+1)*512], ones_b[:],
                    r0bf[:, nch*512:(nch+1)*512], start=(ot == 0), stop=(ot == NT - 1)),
                      reads=["r0bf", "ones_b"], writes=["psS1"])
                pr.op("tensor", lambda E_, nch=nch, ot=ot: E_.matmul(
                    psS2[0:1, nch*512:(nch+1)*512], ones_b[:],
                    sqbf[:, nch*512:(nch+1)*512], start=(ot == 0), stop=(ot == NT - 1)),
                      reads=["sqbf", "ones_b"], writes=["psS2"])
        ckpt(12)
        ln_rows("ln2")
        ln_bcast()
        for dt_ in range(NT):
            pr.dma("sync", r0t[:], r0d[dt_], reads=[f"r0d{dt_}"], writes=["r0t"])
            for nch in range(2):
                pr.op("vector", lambda E_, nch=nch: E_.tensor_tensor(
                    x1t[:, nch*512:(nch+1)*512], r0t[:, nch*512:(nch+1)*512],
                    ps[nch][:], ALU.subtract),
                      reads=["r0t", f"ps{nch}"], writes=["x1t"])
                pr.op("vector", lambda E_, nch=nch: E_.tensor_tensor(
                    x1t[:, nch*512:(nch+1)*512], x1t[:, nch*512:(nch+1)*512],
                    ps[2+nch][:], ALU.mult),
                      reads=[f"ps{2+nch}"], writes=["x1t"])
            pr.dma("gpsimd", x2t[dt_], x1t[:], reads=["x1t"], writes=[f"x2t{dt_}"])

    return


def emit_inproj(nc, pr, io, halo_sb=None):
    x2t = io["x2t"]; wtip = io["wtip"]
    gate = io["gate"]; hbc = io["hbc"]; dt_o = io["dt_o"]
    halo_out = io.get("halo_out")
    from contextlib import ExitStack
    with ExitStack() as ctx:
        E = ctx.enter_context
        if halo_sb is not None:
            x2h = io["x2h"]
            x2hbf = E(nc.sbuf_tensor("x2hbf", [128, 96], BF16))
            psH = E(nc.psum_tensor("psH", [128, 512], F32))
            pr.dma("gpsimd", x2hbf[:], x2h.rearrange("p ft j -> p (ft j)"),
                   writes=["x2hbf"])
        x2bf = E(nc.sbuf_tensor("x2bf", [128, 32 * 1024], BF16))
        wst = [E(nc.sbuf_tensor(f"wst{i}", [128, 32 * 128], BF16)) for i in range(2)]
        ev = [E(nc.sbuf_tensor(f"ev{i}", [128, 1024], F32)) for i in range(2)]
        ps = [E(nc.psum_tensor(f"ps{i}", [128, 512], F32)) for i in range(4)]
        for ft in range(NT):
            pr.dma("gpsimd", x2bf[:, ft*1024:(ft+1)*1024], x2t[ft], writes=[f"x2bf{ft}"])
        XR = [f"x2bf{ft}" for ft in range(NT)]
        for ct in range(CT_PROJ):
            w = wst[ct % 2]
            pr.dma("gpsimd", w[:], wtip[ct].rearrange("p ft j -> p (ft j)"),
                   writes=[f"wst{ct%2}"])
            ee = ev[ct % 2]
            for nch in range(2):
                pk = ps[(ct % 2) * 2 + nch]
                for ft in range(NT):
                    pr.op("tensor", lambda E_, ft=ft, nch=nch, pk=pk, w=w:
                          E_.matmul(pk[:], w[:, ft*128:(ft+1)*128],
                                    x2bf[:, ft*1024 + nch*512: ft*1024 + nch*512 + 512],
                                    start=(ft == 0), stop=(ft == NT - 1)),
                          reads=[f"wst{ct%2}"] + XR, writes=[f"ps{(ct%2)*2+nch}"])
                pr.op("scalar", lambda E_, nch=nch, pk=pk, ee=ee:
                      E_.copy(ee[:, nch*512:(nch+1)*512], pk[:]),
                      reads=[f"ps{(ct%2)*2+nch}"], writes=[f"ev{ct%2}"])
            if ct < 32:
                pr.dma("sync", gate[ct], ee[:], reads=[f"ev{ct%2}"], writes=[f"o{ct}"])
            elif ct < 80:
                pr.dma("sync", hbc[ct - 32], ee[:], reads=[f"ev{ct%2}"], writes=[f"o{ct}"])
                if halo_out is not None:
                    j = ct - 32
                    pr.dma("sync", halo_out[:, j*3:(j+1)*3], ee[:, 1021:1024],
                           reads=[f"ev{ct%2}"], writes=[f"ho{ct}"])
                if halo_sb is not None:
                    j = ct - 32
                    for ft in range(NT):
                        pr.op("tensor", lambda E_, ft=ft, w=w, ct=ct: E_.matmul(
                            psH[:, 0:3], w[:, ft*128:(ft+1)*128],
                            x2hbf[:, ft*3:(ft+1)*3],
                            start=(ft == 0), stop=(ft == NT - 1)),
                              reads=[f"wst{ct%2}", "x2hbf"], writes=["psH"])
                    pr.op("scalar", lambda E_, j=j: E_.copy(
                        halo_sb[:, j*3:(j+1)*3], psH[:, 0:3]),
                          reads=["psH"], writes=["halo_sel"])
            else:
                pr.dma("sync", dt_o, ee[0:64, :], reads=[f"ev{ct%2}"], writes=[f"o{ct}"])
    return


def emit_conv(nc, pr, io, halo_fn):
    hbc = io["hbc"]; dt_i = io["dt_i"]
    convw = io["convw"]; convb = io["convb"]; dtb = io["dtb"]; alog = io["alog"]
    xbc = io["xbc"]; dtp_o = io["dtp_o"]; scum_o = io["scum_o"]
    from contextlib import ExitStack
    with ExitStack() as ctx:
        E = ctx.enter_context
        halo_sb, HKEYS = halo_fn(E, pr)
        hb = [E(nc.sbuf_tensor(f"hb{i}", [128, 1027], F32)) for i in range(2)]
        cw = E(nc.sbuf_tensor("cw", [128, 4], F32))
        cb = E(nc.sbuf_tensor("cb", [128, 1], F32))
        acc = [E(nc.sbuf_tensor(f"acc{i}", [128, 1024], F32)) for i in range(2)]
        xb = [E(nc.sbuf_tensor(f"xb{i}", [128, 1024], BF16)) for i in range(2)]
        dts = E(nc.sbuf_tensor("dts", [64, 1024], F32))
        d1 = E(nc.sbuf_tensor("d1", [64, 1024], F32))
        zz = E(nc.sbuf_tensor("zz", [64, 1024], F32))
        l1p = E(nc.sbuf_tensor("l1p", [64, 1024], F32))
        ezz = E(nc.sbuf_tensor("ezz", [64, 1024], F32))
        rl = E(nc.sbuf_tensor("rl", [64, 1024], F32))
        dtp = E(nc.sbuf_tensor("dtp", [64, 1024], F32))
        aa = E(nc.sbuf_tensor("aa", [64, 1024], F32))
        scum = E(nc.sbuf_tensor("scum", [64, 1024], F32))
        dtbs = E(nc.sbuf_tensor("dtbs", [64, 1], F32))
        als = E(nc.sbuf_tensor("als", [64, 1], F32))
        nega = E(nc.sbuf_tensor("nega", [64, 1], F32))
        one64 = E(nc.sbuf_tensor("one64", [64, 1], F32))
        pr.op("vector", lambda E_: E_.memset(one64[:], 1.0), writes=["one64"])
        pr.dma("sync", dts[:], dt_i, writes=["dts"])
        pr.dma("sync", dtbs[:], dtb, writes=["dtbs"])
        pr.dma("sync", als[:], alog, writes=["als"])
        for ct in range(48):
            h = hb[ct % 2]; a = acc[ct % 2]; x = xb[ct % 2]
            pr.op("vector", lambda E_, h=h, ct=ct: E_.tensor_copy(
                h[:, 0:3], halo_sb[:, ct*3:(ct+1)*3]),
                  reads=HKEYS, writes=[f"hb{ct%2}"])
            pr.dma("sync", h[:, 3:1027], hbc[ct], writes=[f"hb{ct%2}"])
            pr.dma("sync", cw[:], convw[ct], writes=["cw"])
            pr.dma("sync", cb[:], convb[ct], writes=["cb"])
            pr.op("vector", lambda E_, h=h, a=a: E_.tensor_scalar(
                a[:], h[:, 0:1024], cw[:, 0:1], cb[:], op0=ALU.mult, op1=ALU.add),
                  reads=[f"hb{ct%2}", "cw", "cb"], writes=[f"acc{ct%2}"])
            for k in range(1, 4):
                pr.op("vector", lambda E_, h=h, a=a, k=k: E_.scalar_tensor_tensor(
                    a[:], h[:, k:1024+k], cw[:, k:k+1], a[:], op0=ALU.mult, op1=ALU.add),
                      reads=[f"hb{ct%2}", "cw", f"acc{ct%2}"], writes=[f"acc{ct%2}"])
            pr.op("scalar", lambda E_, a=a, x=x: E_.activation(x[:], a[:], AF.Silu),
                  reads=[f"acc{ct%2}"], writes=[f"xb{ct%2}"])
            pr.dma("gpsimd", xbc[:, :, ct, :],
                   x[:].rearrange("p (c t) -> p c t", c=8),
                   reads=[f"xb{ct%2}"], writes=[f"xo{ct}"])
        pr.op("vector", lambda E_: E_.tensor_scalar(d1[:], dts[:], dtbs[:], None,
                                                    op0=ALU.add),
              reads=["dts", "dtbs"], writes=["d1"])
        pr.op("scalar", lambda E_: E_.activation(zz[:], d1[:], AF.Abs),
              reads=["d1"], writes=["zz"])
        pr.op("scalar", lambda E_: E_.activation(ezz[:], zz[:], AF.Exp, scale=-1.0),
              reads=["zz"], writes=["ezz"])
        pr.op("scalar", lambda E_: E_.activation(l1p[:], ezz[:], AF.Ln, bias=one64[:]),
              reads=["ezz", "one64"], writes=["l1p"])
        pr.op("vector", lambda E_: E_.tensor_scalar(rl[:], d1[:], 0.0, None,
                                                    op0=ALU.max),
              reads=["d1"], writes=["rl"])
        pr.op("vector", lambda E_: E_.tensor_tensor(dtp[:], rl[:], l1p[:], ALU.add),
              reads=["l1p", "rl"], writes=["dtp"])
        pr.dma("gpsimd", dtp_o, dtp[:], reads=["dtp"], writes=["dtpo"])
        pr.op("scalar", lambda E_: E_.activation(nega[:], als[:], AF.Exp),
              reads=["als"], writes=["nega"])
        pr.op("vector", lambda E_: E_.tensor_scalar(nega[:], nega[:], -1.0, None,
                                                    op0=ALU.mult),
              reads=["nega"], writes=["nega"])
        pr.op("vector", lambda E_: E_.tensor_scalar(aa[:], dtp[:], nega[:], None,
                                                    op0=ALU.mult),
              reads=["dtp", "nega"], writes=["aa"])
        pr.op("vector", lambda E_: E_.tensor_tensor_scan(scum[:], aa[:], aa[:], 0.0,
                                                         op0=ALU.add, op1=ALU.bypass),
              reads=["aa"], writes=["scum"])
        pr.dma("gpsimd", scum_o, scum[:], reads=["scum"], writes=["scumo"])
    return




def emit_scan(nc, pr, io, with_init, debug=False, tag="", y_work=True):
    def dint(name, shape, dt=F32):
        return nc.dram_tensor(name, list(shape), dt, kind="Internal").ap()
    xbc = io["xbc"]; dtp_i = io["dtp_i"]; scum_i = io["scum_i"]
    maskc = io["maskc"]; identb = io["identb"]; identf = io["identf"]
    onesrow = io["onesrow"]; onesrowb = io["onesrowb"]; onesb = io["onesb"]
    drep = io["drep"]
    if with_init:
        st_slot = io["st_slot"]
        flags4 = io["flags4"]
        nwp = io["nwp"]; gate = io["gate"]; wtop = io["wtop"]; x2t = io["x2t"]
        x3t = io["x3t"]
        if debug:
            dbg_att = io["dbg_att"]
        r0d = dint(tag + "s_r0d", (32, 128, 1024), F32)
    else:
        STout = io["STout"]
    eD = dint(tag + "s_eD", (64, 128), F32)
    eDb = dint(tag + "s_eDb", (64, 128), BF16)

    from contextlib import ExitStack
    with ExitStack() as ctx:
        E = ctx.enter_context
        scum = E(nc.sbuf_tensor("scum", [64, 1024], F32))
        dtp = E(nc.sbuf_tensor("dtp", [64, 1024], F32))
        ldt = E(nc.sbuf_tensor("ldt", [64, 1024], F32))
        ST = E(nc.sbuf_tensor("ST", [128, 4096], F32))
        STb = E(nc.sbuf_tensor("STb", [128, 4096], BF16))
        yn = E(nc.sbuf_tensor("yn", [128, 32 * 1024], BF16))
        ynf = yn[:].bitcast(F32)
        R4 = ynf[:, 0:4096]
        S4s = ynf[:, 4096:8192]
        xbcC0 = E(nc.sbuf_tensor("xbcC0", [128, 6144], BF16))
        xbcC = [xbcC0, xbcC0]
        arD = E(nc.sbuf_tensor("arD", [128, 8192], F32))
        yfin = arD[:, 0:4096]
        Dh = arD[:, 4096:8192]
        arB = E(nc.sbuf_tensor("arB", [128, 4096], BF16))
        arB2 = E(nc.sbuf_tensor("arB2", [128, 4096], BF16))
        xT = E(nc.sbuf_tensor("xT", [128, 4096], BF16))
        xwT = arB[:]
        scT = E(nc.sbuf_tensor("scT", [128, 64], F32))
        ldtT = E(nc.sbuf_tensor("ldtT", [128, 64], F32))
        scldT = E(nc.sbuf_tensor("scldT", [128, 64], F32))
        wmat = E(nc.sbuf_tensor("wmat", [64, 128], F32))
        wdt = E(nc.sbuf_tensor("wdt", [64, 128], F32))
        wdtTb = E(nc.sbuf_tensor("wdtTb", [128, 64], BF16))
        emat = E(nc.sbuf_tensor("emat", [64, 128], F32))
        srowH = E(nc.sbuf_tensor("srowH", [1, 4096], F32))
        Ceh = E(nc.sbuf_tensor("Ceh", [128, 4096], BF16))
        drsb = E(nc.sbuf_tensor("drsb", [128, 32], BF16))
        dtrow = E(nc.sbuf_tensor("dtrow", [1, 64], F32))
        dtbc = E(nc.sbuf_tensor("dtbc", [128, 64], F32))
        maskS = E(nc.sbuf_tensor("maskS", [128, 128], F32))
        idb = E(nc.sbuf_tensor("idb", [128, 128], BF16))
        idf = E(nc.sbuf_tensor("idf", [128, 128], F32))
        onr = E(nc.sbuf_tensor("onr", [1, 128], F32))
        onrb = E(nc.sbuf_tensor("onrb", [1, 128], BF16))
        onb = E(nc.sbuf_tensor("onb", [128, 1], BF16))
        drs = E(nc.sbuf_tensor("drs", [128, 32], F32))
        nsp = E(nc.sbuf_tensor("nsp", [64, 1], F32))
        sclE = E(nc.sbuf_tensor("sclE", [64, 1], F32))
        d4r = E(nc.sbuf_tensor("d4r", [1, 64], F32))
        d4bc = E(nc.sbuf_tensor("d4bc", [128, 64], F32))
        fl = E(nc.sbuf_tensor("fl", [128, 3], F32))
        ceb = [E(nc.sbuf_tensor(f"ceb{i}", [128, 128], BF16)) for i in range(2)]
        sqb = E(nc.sbuf_tensor("sqb", [128, 128], BF16))
        vrow = srowH[0:1, 1024:2048]
        rsr = srowH[0:1, 2048:3072]
        rsrb = srowH[0:1, :].bitcast(BF16)[0:1, 6144:7168]
        eps1 = E(nc.sbuf_tensor("eps1", [1, 1], F32))
        if with_init:
            nws = E(nc.sbuf_tensor("nws", [128, 32], F32))
            wostr = [arB[:], arB2[:]]
            xstr = arD[:, 0:1024]
            r0t = arD[:, 1024:2048]
            sgt = arD[:, 2048:3072]
            sgb = arD[:].bitcast(BF16)[:, 6144:7168]
            gtt = arD[:, 4096:5120]
            x1t = arD[:, 5120:6144]
            r0bf = STb[:, 0:1024]
            sq2 = STb[:, 1024:2048]
            mrow = srowH[0:1, 3072:4096]
            rsrow = rsr

        pk = [E(nc.psum_tensor(f"pk{i}", [128, 512], F32)) for i in range(6)]
        pkb = [pk[2][:].bitcast(BF16)[:, 0:128], pk[3][:].bitcast(BF16)[:, 0:128]]
        psV = E(nc.psum_tensor("psV", [1, 1024], F32))

        pr.op("vector", lambda E_: E_.memset(eps1[:], EPS), writes=["eps1"])
        pr.dma("sync", scum[:], scum_i, writes=["scum"])
        pr.dma("sync", dtp[:], dtp_i, writes=["dtp"])
        pr.dma("sync", maskS[:], maskc, writes=["maskS"])
        pr.dma("sync", idb[:], identb, writes=["idb"])
        pr.dma("sync", idf[:], identf, writes=["idf"])
        pr.dma("sync", onr[:], onesrow, writes=["onr"])
        pr.dma("sync", onrb[:], onesrowb, writes=["onrb"])
        pr.dma("sync", onb[:], onesb, writes=["onb"])
        pr.dma("sync", drs[:], drep, writes=["drs"])
        pr.op("scalar", lambda E_: E_.activation(ldt[:], dtp[:], AF.Ln),
              reads=["dtp"], writes=["ldt"])
        pr.op("vector", lambda E_: E_.tensor_copy(drsb[:], drs[:]),
              reads=["drs"], writes=["drsb"])
        pr.op("gpsimd", lambda E_: E_.memset(ST[:], 0.0), writes=["ST"])
        if with_init:
            pr.dma("sync", nws[:], nwp, writes=["nws"])
            pr.op("gpsimd", lambda E_: E_.memset(R4[:], 0.0), writes=["R4"])
            pr.dma("sync", fl[:], flags4, writes=["fl"])
            for kq in range(3):
                st_ap, scl_ap = st_slot(kq)
                pr.dma("sync", S4s[:], st_ap, writes=["S4s"])
                pr.dma("sync", sclE[:], scl_ap, writes=["sclE"])
                if kq > 0:
                    pr.op("scalar", lambda E_: E_.activation(sclE[:], sclE[:], AF.Exp),
                          reads=["sclE"], writes=["sclE"])
                    pr.op("tensor", lambda E_: E_.transpose(
                        pk[2][0:1, 0:64], sclE[:], idf[0:64, 0:64]),
                          reads=["sclE", "idf"], writes=["pk2"])
                    pr.op("vector", lambda E_: E_.tensor_copy(d4r[0:1, :], pk[2][0:1, 0:64]),
                          reads=["pk2"], writes=["d4r"])
                    pr.op("tensor", lambda E_: E_.matmul(pk[3][:, 0:64], onr[0:1, :],
                                                         d4r[0:1, :], start=True, stop=True),
                          reads=["d4r", "onr"], writes=["pk3"])
                    pr.op("vector", lambda E_: E_.tensor_copy(d4bc[:], pk[3][:, 0:64]),
                          reads=["pk3"], writes=["d4bc"])
                    pr.op("vector", lambda E_: E_.tensor_tensor(
                        R4.rearrange("p (h q) -> p h q", q=64),
                        R4.rearrange("p (h q) -> p h q", q=64),
                        d4bc[:].unsqueeze(-1).to_broadcast([128, 64, 64]),
                        ALU.mult),
                          reads=["R4", "d4bc"], writes=["R4"])
                pr.op("vector", lambda E_: E_.tensor_tensor(R4, R4, S4s, ALU.add),
                      reads=["R4", "S4s"], writes=["R4"])
                pr.op("vector", lambda E_, kq=kq: E_.scalar_tensor_tensor(
                    ST[:], R4, fl[:, kq:kq+1], ST[:], op0=ALU.mult, op1=ALU.add),
                      reads=["R4", "fl", "ST"], writes=["ST"])
            pr.barrier()

        for c in range(NCH):
            xc = xbcC[0]
            XK = "xbcC0"
            pr.dma("sync", xc[:], xbc[:, c].rearrange("p ct t -> p (ct t)"),
                   writes=[XK])
            c0 = c * 128
            pr.op("vector", lambda E_: E_.tensor_copy(STb[:, 0:4096], ST[:]),
                  reads=["ST"], writes=["STb"])
            pr.op("tensor", lambda E_, c0=c0: E_.transpose(
                pk[2][:, 0:64], scum[:, c0:c0+128], idf[0:64, 0:64]),
                  reads=["scum", "idf"], writes=["pk2"])
            pr.op("vector", lambda E_: E_.tensor_copy(scT[:], pk[2][:, 0:64]),
                  reads=["pk2"], writes=["scT"])
            pr.op("tensor", lambda E_, c0=c0: E_.transpose(
                pk[3][:, 0:64], ldt[:, c0:c0+128], idf[0:64, 0:64]),
                  reads=["ldt", "idf"], writes=["pk3"])
            pr.op("vector", lambda E_: E_.tensor_copy(ldtT[:], pk[3][:, 0:64]),
                  reads=["pk3"], writes=["ldtT"])
            pr.op("vector", lambda E_: E_.tensor_tensor(scldT[:], scT[:], ldtT[:],
                                                        ALU.subtract),
                  reads=["scT", "ldtT"], writes=["scldT"])
            if c == 0:
                pr.op("scalar", lambda E_, c0=c0: E_.activation(
                    emat[:], scum[:, c0:c0+128], AF.Exp),
                      reads=["scum"], writes=["emat"])
            else:
                pr.op("vector", lambda E_, c0=c0: E_.tensor_scalar(
                    nsp[:], scum[:, c0-1:c0], -1.0, None, op0=ALU.mult),
                      reads=["scum"], writes=["nsp"])
                pr.op("scalar", lambda E_, c0=c0: E_.activation(
                    emat[:], scum[:, c0:c0+128], AF.Exp, bias=nsp[:]),
                      reads=["scum", "nsp"], writes=["emat"])
            pr.op("scalar", lambda E_, c0=c0: E_.activation(
                wmat[:], scum[:, c0:c0+128], AF.Exp, scale=-1.0,
                bias=scum[:, c0+127:c0+128]),
                  reads=["scum"], writes=["wmat"])
            pr.op("vector", lambda E_, c0=c0: E_.tensor_tensor(
                wdt[:], wmat[:], dtp[:, c0:c0+128], ALU.mult),
                  reads=["wmat", "dtp"], writes=["wdt"])
            pr.op("tensor", lambda E_: E_.transpose(pk[2][:, 0:64], wdt[:], idf[0:64, 0:64]),
                  reads=["wdt", "idf"], writes=["pk2"])
            pr.op("vector", lambda E_: E_.tensor_copy(wdtTb[:], pk[2][:, 0:64]),
                  reads=["pk2"], writes=["wdtTb"])
            pr.dma("gpsimd", eDb, emat[:], reads=["emat"], writes=["eDb"])
            pr.op("tensor", lambda E_: E_.transpose(pk[3][0:1, 0:64],
                                                    emat[:, 127:128], idf[0:64, 0:64]),
                  reads=["emat", "idf"], writes=["pk3"])
            pr.op("vector", lambda E_: E_.tensor_copy(dtrow[0:1, :], pk[3][0:1, 0:64]),
                  reads=["pk3"], writes=["dtrow"])
            pr.op("tensor", lambda E_: E_.matmul(pk[2][:, 0:64], onr[0:1, :],
                                                 dtrow[0:1, :], start=True, stop=True),
                  reads=["dtrow", "onr"], writes=["pk2"])
            pr.op("vector", lambda E_: E_.tensor_copy(dtbc[:], pk[2][:, 0:64]),
                  reads=["pk2"], writes=["dtbc"])
            for ft in range(NT):
                pb = pkb[ft % 2]
                pr.op("tensor", lambda E_, ft=ft, pb=pb, xc=xc: E_.transpose(
                    pb[:], xc[:, ft*128:(ft+1)*128], idb[:]),
                      reads=[XK, "idb"], writes=[f"pk{2+ft%2}"])
                pr.op("scalar", lambda E_, ft=ft, pb=pb: E_.copy(
                    xT[:, ft*128:(ft+1)*128], pb[:]),
                      reads=[f"pk{2+ft%2}"], writes=["xT"])
            if y_work:
                pr.op("vector", lambda E_, xc=xc: E_.tensor_tensor(
                    yfin.rearrange("p (ft t) -> p ft t", ft=32),
                    xc[:, 0:4096].rearrange("p (ft t) -> p ft t", ft=32),
                    drsb[:].unsqueeze(-1).to_broadcast([128, 32, 128]), ALU.mult),
                      reads=[XK, "drsb"], writes=["yfin"])
            for half in range(2 if y_work else 0):
                ho = half * 32
                pr.dma("sync", arB2[:].rearrange("p (h t) -> p h t", h=32),
                       eDb[ho:ho+32, :].partition_broadcast(128),
                       reads=["eDb"], writes=["arB2"])
                pr.op("vector", lambda E_, xc=xc, half=half: E_.tensor_tensor(
                    Ceh[:].rearrange("p (g j t) -> p g j t", g=4, j=8),
                    xc[:, (40+half*4)*128:(44+half*4)*128].rearrange(
                        "p (g t) -> p g t", g=4).unsqueeze(2).to_broadcast(
                        [128, 4, 8, 128]),
                    arB2[:].rearrange("p (g j t) -> p g j t", g=4, j=8), ALU.mult),
                      reads=[XK, "arB2"], writes=["Ceh"])
                pr.dma("sync", Dh.rearrange("p (h t) -> p h t", h=32),
                       scum_i[ho:ho+32, c0:c0+128].partition_broadcast(128),
                       writes=["Dh"])
                pr.op("vector", lambda E_, ho=ho: E_.tensor_tensor(
                    Dh.rearrange("p (h t) -> p h t", h=32),
                    Dh.rearrange("p (h t) -> p h t", h=32),
                    scldT[:, ho:ho+32].unsqueeze(-1).to_broadcast([128, 32, 128]),
                    ALU.subtract),
                      reads=["Dh", "scldT"], writes=["Dh"])
                pr.op("vector", lambda E_: E_.tensor_tensor(
                    Dh.rearrange("p (h t) -> p h t", h=32),
                    Dh.rearrange("p (h t) -> p h t", h=32),
                    maskS[:].unsqueeze(1).to_broadcast([128, 32, 128]), ALU.add),
                      reads=["Dh", "maskS"], writes=["Dh"])
                pr.op("scalar", lambda E_: E_.activation(arB[:], Dh[:], AF.Exp),
                      reads=["Dh"], writes=["arB"])
                pA = pk[half]
                for gg in range(4):
                    g = half * 4 + gg
                    pr.op("tensor", lambda E_, g=g, gg=gg, pA=pA, xc=xc: E_.matmul(
                        pA[:, gg*128:(gg+1)*128],
                        xc[:, (32+g)*128:(33+g)*128], xc[:, (40+g)*128:(41+g)*128],
                        start=True, stop=True),
                          reads=[XK], writes=[f"pk{half}"])
                pr.op("vector", lambda E_, pA=pA: E_.tensor_tensor(
                    arB2[:].rearrange("p (gg j t) -> p gg j t", gg=4, j=8),
                    arB[:].rearrange("p (gg j t) -> p gg j t", gg=4, j=8),
                    pA[:].rearrange("p (gg t) -> p gg t", gg=4).unsqueeze(2).to_broadcast([128, 4, 8, 128]),
                    ALU.mult),
                      reads=["arB", f"pk{half}"], writes=["arB2"])
                for hh in range(32):
                    h = ho + hh
                    hb = hh % 2
                    pY = pk[4 + hb]
                    pr.op("tensor", lambda E_, h=h, hh=hh, pY=pY: E_.matmul(
                        pY[0:64, 0:128], xT[:, h*64:(h+1)*64],
                        arB2[:, hh*128:(hh+1)*128], start=True, stop=False),
                          reads=["xT", "arB2"], writes=[f"pk{4+hb}"])
                    pr.op("tensor", lambda E_, h=h, hh=hh, pY=pY: E_.matmul(
                        pY[0:64, 0:128], STb[:, h*64:(h+1)*64],
                        Ceh[:, hh*128:(hh+1)*128], start=False, stop=True),
                          reads=["STb", "Ceh"], writes=[f"pk{4+hb}"])
                    ft = h // 2
                    rr0 = (h % 2) * 64
                    pr.op("vector", lambda E_, ft=ft, rr0=rr0, pY=pY:
                          E_.tensor_tensor(
                              yfin[rr0:rr0+64, ft*128:(ft+1)*128],
                              yfin[rr0:rr0+64, ft*128:(ft+1)*128],
                              pY[0:64, 0:128], ALU.add),
                          reads=[f"pk{4+hb}", "yfin"], writes=["yfin"])
            for ft in range(NT if y_work else 0):
                pr.op("scalar", lambda E_, ft=ft: E_.activation(
                    sqb[:], yfin[:, ft*128:(ft+1)*128], AF.Square),
                      reads=["yfin"], writes=["sqb"])
                pr.op("tensor", lambda E_, ft=ft: E_.matmul(
                    psV[0:1, (ft//4)*128:(ft//4)*128+128], onb[:], sqb[:],
                    start=(ft % 4 == 0), stop=(ft % 4 == 3)),
                      reads=["sqb", "onb"], writes=["psV"])
            if y_work:
                pr.op("scalar", lambda E_: E_.activation(vrow[:], psV[:], AF.Ln,
                                                         scale=1.0/512.0, bias=eps1[:]),
                      reads=["psV", "eps1"], writes=["vrow", "srowH"])
                pr.op("scalar", lambda E_: E_.activation(rsr[:], vrow[:], AF.Exp,
                                                         scale=-0.5),
                      reads=["vrow"], writes=["rsr", "srowH"])
                pr.op("vector", lambda E_: E_.tensor_copy(rsrb[:], rsr[:]),
                      reads=["rsr"], writes=["rsrb", "srowH"])
            if with_init:
                for g in range(8):
                    pe = pk[2 + (g % 2)]
                    pr.op("tensor", lambda E_, g=g, pe=pe: E_.matmul(
                        pe[:, 0:128], onrb[0:1, :], rsrb[0:1, g*128:(g+1)*128],
                        start=True, stop=True),
                          reads=["rsrb", "onrb"], writes=[f"pk{2+g%2}"])
                    for ft in range(g*4, g*4+4):
                        pr.op("vector", lambda E_, ft=ft, g=g, pe=pe, c=c:
                              E_.scalar_tensor_tensor(
                                  yn[:, ft*1024 + c*128: ft*1024 + c*128 + 128],
                                  yfin[:, ft*128:(ft+1)*128], nws[:, ft:ft+1],
                                  pe[:, 0:128], op0=ALU.mult, op1=ALU.mult),
                              reads=["yfin", "nws", f"pk{2+g%2}"], writes=[f"yn{ft}"])
            pr.op("vector", lambda E_: E_.tensor_tensor(
                xwT.rearrange("p (h q) -> p h q", q=64),
                xT[:].rearrange("p (h q) -> p h q", q=64),
                wdtTb[:].unsqueeze(-1).to_broadcast([128, 64, 64]),
                ALU.mult),
                  reads=["xT", "wdtTb"], writes=["xwT", "arB"])
            for g in range(8):
                pb = pkb[g % 2]
                pr.op("tensor", lambda E_, g=g, pb=pb, xc=xc: E_.transpose(
                    pb[:], xc[:, (32+g)*128:(33+g)*128], idb[:]),
                      reads=[XK, "idb"], writes=[f"pk{2+g%2}"])
                bt = ceb[g % 2]
                pr.op("scalar", lambda E_, pb=pb, bt=bt: E_.copy(bt[:], pb[:]),
                      reads=[f"pk{2+g%2}"], writes=[f"ceb{g%2}"])
                pT_ = pk[4 + (g % 2)]
                for hh in range(8):
                    h = g * 8 + hh
                    pr.op("tensor", lambda E_, h=h, hh=hh, pT_=pT_, bt=bt: E_.matmul(
                        pT_[:, hh*64:(hh+1)*64], bt[:], xwT[:, h*64:(h+1)*64],
                        start=(hh == 0), stop=(hh == 7)),
                          reads=[f"ceb{g%2}", "xwT", "arB"], writes=[f"pk{4+g%2}"])
                pr.op("vector", lambda E_, g=g: E_.tensor_tensor(
                    ST[:, g*512:(g+1)*512].rearrange("p (hh q) -> p hh q", q=64),
                    ST[:, g*512:(g+1)*512].rearrange("p (hh q) -> p hh q", q=64),
                    dtbc[:, g*8:(g+1)*8].unsqueeze(-1).to_broadcast([128, 8, 64]),
                    ALU.mult),
                      reads=["ST", "dtbc"], writes=["ST"])
                pr.op("vector", lambda E_, g=g, pT_=pT_: E_.tensor_tensor(
                    ST[:, g*512:(g+1)*512], ST[:, g*512:(g+1)*512], pT_[:, 0:512],
                    ALU.add),
                      reads=["ST", f"pk{4+g%2}"], writes=["ST"])

        if not with_init:
            pr.dma("gpsimd", STout, ST[:], reads=["ST"], writes=["STo"])
        else:
            pr.barrier()
            YNK = [f"yn{ft}" for ft in range(NT)]
            for it in range(NT):
                pr.dma("sync", gtt[:], gate[it], writes=["gtt"])
                pr.op("scalar", lambda E_: E_.activation(sgb[:], gtt[:], AF.Silu),
                      reads=["gtt"], writes=["sgb"])
                pr.op("vector", lambda E_, it=it: E_.tensor_tensor(
                    yn[:, it*1024:(it+1)*1024], yn[:, it*1024:(it+1)*1024],
                    sgb[:], ALU.mult),
                      reads=["sgb", f"yn{it}"], writes=[f"yn{it}"])
            for dt_ in range(NT):
                w0 = wostr[dt_ % 2]
                WK = f"wostr{dt_%2}"
                pr.dma("gpsimd", w0, wtop[dt_].rearrange("p it j -> p (it j)"),
                       writes=[WK])
                pr.dma("sync", xstr, x2t[dt_], writes=["xstr"])
                for nch in range(2):
                    pq = pk[(dt_ % 2) * 2 + nch]
                    PK = f"pk{(dt_%2)*2+nch}"
                    for it in range(NT):
                        pr.op("tensor", lambda E_, it=it, nch=nch, pq=pq, w0=w0:
                              E_.matmul(pq[:], w0[:, it*128:(it+1)*128],
                                        yn[:, it*1024 + nch*512: it*1024 + nch*512 + 512],
                                        start=(it == 0), stop=(it == NT - 1)),
                              reads=[WK] + YNK, writes=[PK])
                    pr.op("vector", lambda E_, nch=nch, pq=pq: E_.tensor_tensor(
                        r0t[:, nch*512:(nch+1)*512], pq[:],
                        xstr[:, nch*512:(nch+1)*512], ALU.add),
                          reads=[PK, "xstr"], writes=["r0t"])
                pr.dma("gpsimd", r0d[dt_], r0t, reads=["r0t"], writes=[f"r0d{dt_}"])
                if debug:
                    pr.dma("gpsimd", dbg_att[dt_], r0t, reads=["r0t"],
                           writes=[f"dba{dt_}"])
                pr.op("vector", lambda E_: E_.tensor_copy(r0bf, r0t),
                      reads=["r0t"], writes=["r0bf"])
                pr.op("scalar", lambda E_: E_.activation(sq2, r0t, AF.Square),
                      reads=["r0t"], writes=["sq2"])
                for nch in range(2):
                    pr.op("tensor", lambda E_, nch=nch, dt_=dt_: E_.matmul(
                        psV[0:1, nch*512:(nch+1)*512], onb[:],
                        r0bf[:, nch*512:(nch+1)*512],
                        start=(dt_ == 0), stop=(dt_ == NT - 1)),
                          reads=["r0bf", "onb"], writes=["psV"])
                    pq2 = pk[4 + nch]
                    pr.op("tensor", lambda E_, nch=nch, dt_=dt_, pq2=pq2: E_.matmul(
                        pq2[0:1, :], onb[:], sq2[:, nch*512:(nch+1)*512],
                        start=(dt_ == 0), stop=(dt_ == NT - 1)),
                          reads=["sq2", "onb"], writes=[f"pk{4+nch}"])
            pr.op("vector", lambda E_: E_.tensor_scalar(
                mrow[:], psV[:], 1.0 / 4096.0, None, op0=ALU.mult),
                  reads=["psV"], writes=["mrow", "srowH"])
            pr.op("scalar", lambda E_: E_.activation(vrow[:], mrow[:], AF.Square),
                  reads=["mrow"], writes=["vrow", "srowH"])
            for nch in range(2):
                pq2 = pk[4 + nch]
                pr.op("vector", lambda E_, nch=nch, pq2=pq2: E_.scalar_tensor_tensor(
                    vrow[:, nch*512:(nch+1)*512], pq2[0:1, :], 1.0 / 4096.0,
                    vrow[:, nch*512:(nch+1)*512], op0=ALU.mult, op1=ALU.subtract),
                      reads=[f"pk{4+nch}", "vrow"], writes=["vrow", "srowH"])
            pr.op("scalar", lambda E_: E_.activation(vrow[:], vrow[:], AF.Ln,
                                                     bias=eps1[:]),
                  reads=["vrow", "eps1"], writes=["vrow", "srowH"])
            pr.op("scalar", lambda E_: E_.activation(rsrow[:], vrow[:], AF.Exp,
                                                     scale=-0.5),
                  reads=["vrow"], writes=["rsrow", "srowH"])
            for nch in range(2):
                pr.op("tensor", lambda E_, nch=nch: E_.matmul(
                    pk[nch][:], onr[0:1, :], mrow[0:1, nch*512:(nch+1)*512],
                    start=True, stop=True),
                      reads=["mrow", "onr"], writes=[f"pk{nch}"])
                pr.op("tensor", lambda E_, nch=nch: E_.matmul(
                    pk[2+nch][:], onr[0:1, :], rsrow[0:1, nch*512:(nch+1)*512],
                    start=True, stop=True),
                      reads=["rsrow", "onr"], writes=[f"pk{2+nch}"])
            for dt_ in range(NT):
                pr.dma("sync", r0t, r0d[dt_], reads=[f"r0d{dt_}"], writes=["r0t"])
                for nch in range(2):
                    pr.op("vector", lambda E_, nch=nch: E_.tensor_tensor(
                        x1t[:, nch*512:(nch+1)*512], r0t[:, nch*512:(nch+1)*512],
                        pk[nch][:], ALU.subtract),
                          reads=["r0t", f"pk{nch}"], writes=["x1t"])
                    pr.op("vector", lambda E_, nch=nch: E_.tensor_tensor(
                        x1t[:, nch*512:(nch+1)*512], x1t[:, nch*512:(nch+1)*512],
                        pk[2+nch][:], ALU.mult),
                          reads=[f"pk{2+nch}"], writes=["x1t"])
                pr.dma("gpsimd", x3t[dt_], x1t, reads=["x1t"], writes=[f"x3o{dt_}"])
    return



def emit_ffn(nc, pr, io):
    def dint(name, shape, dt=F32):
        return nc.dram_tensor(name, list(shape), dt, kind="Internal").ap()
    x3t = io["x3t"]; wtbp = io["wtbp"]; bbp = io["bbp"]; a1p = io["a1p"]
    b1p = io["b1p"]; r1p = io["r1p"]; selp = io["selp"]
    identb = io["identb"]; onesrow = io["onesrow"]; onesb = io["onesb"]
    x4t = io["x4t"]
    r0d = dint("f_r0d", (32, 128, 1024), F32)
    from contextlib import ExitStack
    with ExitStack() as ctx:
        E = ctx.enter_context
        x3bf = E(nc.sbuf_tensor("x3bf", [128, 32 * 1024], BF16))
        a1sb = E(nc.sbuf_tensor("a1sb", [128, 32 * 64], BF16))
        r1sb = E(nc.sbuf_tensor("r1sb", [128, 32 * 16], BF16))
        selsb = E(nc.sbuf_tensor("selsb", [16, 8 * 128], BF16))
        wstr = [E(nc.sbuf_tensor(f"wstr{i}", [128, 32 * 128], BF16)) for i in range(2)]
        bstr = [E(nc.sbuf_tensor(f"bstr{i}", [128, 8 * 128], BF16)) for i in range(2)]
        bbs = E(nc.sbuf_tensor("bbs", [128, 32], F32))
        idb = E(nc.sbuf_tensor("idb", [128, 128], BF16))
        onr = E(nc.sbuf_tensor("onr", [1, 128], F32))
        onb = E(nc.sbuf_tensor("onb", [128, 1], BF16))
        lg16 = E(nc.sbuf_tensor("lg16", [128, 8 * 16], F32))
        prob = E(nc.sbuf_tensor("prob", [128, 16], F32))
        gw = E(nc.sbuf_tensor("gw", [128, 16], F32))
        gm = E(nc.sbuf_tensor("gm", [128, 1], F32))
        gm2 = E(nc.sbuf_tensor("gm2", [128, 1], F32))
        gz = E(nc.sbuf_tensor("gz", [128, 1], F32))
        ngm = E(nc.sbuf_tensor("ngm", [128, 1], F32))
        rz = E(nc.sbuf_tensor("rz", [128, 1], F32))
        cmb = E(nc.sbuf_tensor("cmb", [128, 16], F32))
        cmbb = E(nc.sbuf_tensor("cmbb", [128, 16], BF16))
        cmbT = E(nc.sbuf_tensor("cmbT", [16, 1024], BF16))
        midT = E(nc.sbuf_tensor("midT", [64, 1024], F32))
        midg = E(nc.sbuf_tensor("midg", [128, 8 * 1024], BF16))
        xstr = E(nc.sbuf_tensor("xstr", [128, 1024], F32))
        r0t = E(nc.sbuf_tensor("r0t", [128, 1024], F32))
        r0bf = E(nc.sbuf_tensor("r0bf", [128, 1024], BF16))
        sq2 = E(nc.sbuf_tensor("sq2", [128, 1024], BF16))
        x1t = E(nc.sbuf_tensor("x1t", [128, 1024], F32))
        mrow = E(nc.sbuf_tensor("mrow", [1, 1024], F32))
        vrow = E(nc.sbuf_tensor("vrow", [1, 1024], F32))
        rsrow = E(nc.sbuf_tensor("rsrow", [1, 1024], F32))
        eps1 = E(nc.sbuf_tensor("eps1", [1, 1], F32))
        pk = [E(nc.psum_tensor(f"pk{i}", [128, 512], F32)) for i in range(6)]
        pkb = [pk[2][:].bitcast(BF16)[:, 0:128], pk[3][:].bitcast(BF16)[:, 0:128]]
        psV = E(nc.psum_tensor("psV", [1, 1024], F32))

        pr.op("vector", lambda E_: E_.memset(eps1[:], EPS), writes=["eps1"])
        for ft in range(NT):
            pr.dma("gpsimd", x3bf[:, ft*1024:(ft+1)*1024], x3t[ft],
                   writes=[f"x3bf{ft}"])
        XK = [f"x3bf{ft}" for ft in range(NT)]
        pr.dma("gpsimd", a1sb[:], a1p.rearrange("p ft j -> p (ft j)"), writes=["a1sb"])
        pr.dma("gpsimd", r1sb[:], r1p.rearrange("p ft j -> p (ft j)"), writes=["r1sb"])
        pr.dma("sync", selsb[:], selp.rearrange("e pr j -> e (pr j)"), writes=["selsb"])
        pr.dma("sync", idb[:], identb, writes=["idb"])
        pr.dma("sync", onr[:], onesrow, writes=["onr"])
        pr.dma("sync", onb[:], onesb, writes=["onb"])
        pr.dma("sync", bbs[:], bbp, writes=["bbs"])
        for tt in range(8):
            for ft in range(NT):
                pr.op("tensor", lambda E_, tt=tt, ft=ft: E_.matmul(
                    pk[0][:, 0:16], x3bf[:, ft*1024 + tt*128: ft*1024 + tt*128 + 128],
                    r1sb[:, ft*16:(ft+1)*16], start=(ft == 0), stop=(ft == NT - 1)),
                      reads=[f"x3bf{ft}", "r1sb"], writes=["pk0"])
            pr.op("scalar", lambda E_, tt=tt: E_.copy(lg16[:, tt*16:(tt+1)*16],
                                                      pk[0][:, 0:16]),
                  reads=["pk0"], writes=["lg16"])
        for tt in range(8):
            lg = lg16[:, tt*16:(tt+1)*16]
            pr.op("vector", lambda E_, lg=lg: E_.tensor_reduce(gm[:], lg, AX.X, ALU.max),
                  reads=["lg16"], writes=["gm"])
            pr.op("vector", lambda E_: E_.tensor_scalar(ngm[:], gm[:], -1.0, None,
                                                        op0=ALU.mult),
                  reads=["gm"], writes=["ngm"])
            pr.op("scalar", lambda E_, lg=lg: E_.activation(prob[:], lg, AF.Exp,
                                                            bias=ngm[:],
                                                            accum_out=gz[:]),
                  reads=["lg16", "ngm"], writes=["prob", "gz"])
            pr.op("vector", lambda E_: E_.reciprocal(rz[:], gz[:]),
                  reads=["gz"], writes=["rz"])
            pr.op("vector", lambda E_: E_.tensor_scalar(prob[:], prob[:], rz[:], None,
                                                        op0=ALU.mult),
                  reads=["prob", "rz"], writes=["prob"])
            pr.op("vector", lambda E_: E_.tensor_reduce(gm[:], prob[:], AX.X, ALU.max),
                  reads=["prob"], writes=["gm"])
            pr.op("vector", lambda E_: E_.tensor_scalar(gw[:], prob[:], gm[:], -1e30,
                                                        op0=ALU.is_ge, op1=ALU.mult),
                  reads=["prob", "gm"], writes=["gw"])
            pr.op("vector", lambda E_: E_.tensor_tensor(gw[:], gw[:], prob[:], ALU.add),
                  reads=["prob", "gw"], writes=["gw"])
            pr.op("vector", lambda E_: E_.tensor_reduce(gm2[:], gw[:], AX.X, ALU.max),
                  reads=["gw"], writes=["gm2"])
            pr.op("vector", lambda E_: E_.tensor_tensor(gz[:], gm[:], gm2[:], ALU.add),
                  reads=["gm", "gm2"], writes=["gz"])
            pr.op("vector", lambda E_: E_.tensor_scalar(gz[:], gz[:], 1e-6, None,
                                                        op0=ALU.add),
                  reads=["gz"], writes=["gz"])
            pr.op("vector", lambda E_: E_.reciprocal(rz[:], gz[:]),
                  reads=["gz"], writes=["rz"])
            pr.op("vector", lambda E_: E_.tensor_scalar(cmb[:], prob[:], gm2[:], None,
                                                        op0=ALU.is_ge),
                  reads=["prob", "gm2"], writes=["cmb"])
            pr.op("vector", lambda E_: E_.tensor_tensor(cmb[:], cmb[:], prob[:],
                                                        ALU.mult),
                  reads=["prob", "cmb"], writes=["cmb"])
            pr.op("vector", lambda E_: E_.tensor_scalar(cmb[:], cmb[:], rz[:], 0.5,
                                                        op0=ALU.mult, op1=ALU.mult),
                  reads=["cmb", "rz"], writes=["cmb"])
            pr.op("vector", lambda E_: E_.tensor_copy(cmbb[:], cmb[:]),
                  reads=["cmb"], writes=["cmbb"])
            pr.op("tensor", lambda E_: E_.transpose(
                pk[2][:].bitcast(BF16)[0:16, 0:128], cmbb[:], idb[:]),
                  reads=["cmbb", "idb"], writes=["pk2"])
            pr.op("scalar", lambda E_, tt=tt: E_.copy(cmbT[0:16, tt*128:(tt+1)*128],
                                                      pk[2][:].bitcast(BF16)[0:16, 0:128]),
                  reads=["pk2"], writes=["cmbT"])
        for nch in range(2):
            for ft in range(NT):
                pr.op("tensor", lambda E_, ft=ft, nch=nch: E_.matmul(
                    pk[1][0:64, :], a1sb[:, ft*64:(ft+1)*64],
                    x3bf[:, ft*1024 + nch*512: ft*1024 + nch*512 + 512],
                    start=(ft == 0), stop=(ft == NT - 1)),
                      reads=[f"x3bf{ft}", "a1sb"], writes=["pk1"])
            pr.op("scalar", lambda E_, nch=nch: E_.copy(midT[:, nch*512:(nch+1)*512],
                                                        pk[1][0:64, :]),
                  reads=["pk1"], writes=["midT"])
        for pr_ in range(8):
            for nch in range(2):
                pr.op("tensor", lambda E_, pr_=pr_, nch=nch: E_.matmul(
                    pk[4][:], selsb[:, pr_*128:(pr_+1)*128],
                    cmbT[:, nch*512:(nch+1)*512], start=True, stop=True),
                      reads=["selsb", "cmbT"], writes=["pk4"])
                for half in range(2):
                    pr.op("vector", lambda E_, pr_=pr_, nch=nch, half=half:
                          E_.tensor_tensor(
                              midg[half*64:(half+1)*64,
                                   pr_*1024 + nch*512: pr_*1024 + nch*512 + 512],
                              midT[:, nch*512:(nch+1)*512],
                              pk[4][half*64:(half+1)*64, :], ALU.mult),
                          reads=["midT", "pk4"], writes=[f"midg{pr_}"])
        for ot in range(NT):
            w0 = wstr[ot % 2]; b0 = bstr[ot % 2]
            WK, BK = f"wstr{ot%2}", f"bstr{ot%2}"
            pr.dma("gpsimd", w0[:], wtbp[ot].rearrange("p it j -> p (it j)"), writes=[WK])
            pr.dma("gpsimd", b0[:], b1p[ot].rearrange("p pr j -> p (pr j)"), writes=[BK])
            pr.dma("sync", xstr[:], x3t[ot], writes=["xstr"])
            for nch in range(2):
                pq = pk[(ot % 2) * 2 + nch]
                PK = f"pk{(ot%2)*2+nch}"
                for it in range(NT):
                    pr.op("tensor", lambda E_, it=it, nch=nch, pq=pq, w0=w0: E_.matmul(
                        pq[:], w0[:, it*128:(it+1)*128],
                        x3bf[:, it*1024 + nch*512: it*1024 + nch*512 + 512],
                        start=(it == 0), stop=False),
                          reads=[WK, f"x3bf{it}"], writes=[PK])
                for pr_ in range(8):
                    pr.op("tensor", lambda E_, pr_=pr_, nch=nch, pq=pq, b0=b0: E_.matmul(
                        pq[:], b0[:, pr_*128:(pr_+1)*128],
                        midg[:, pr_*1024 + nch*512: pr_*1024 + nch*512 + 512],
                        start=False, stop=(pr_ == 7)),
                          reads=[BK, f"midg{pr_}"], writes=[PK])
                pr.op("vector", lambda E_, nch=nch, pq=pq, ot=ot:
                      E_.scalar_tensor_tensor(
                          r0t[:, nch*512:(nch+1)*512], pq[:], bbs[:, ot:ot+1],
                          xstr[:, nch*512:(nch+1)*512], op0=ALU.add, op1=ALU.add),
                      reads=[PK, "bbs", "xstr"], writes=["r0t"])
            pr.dma("gpsimd", r0d[ot], r0t[:], reads=["r0t"], writes=[f"r0d{ot}"])
            pr.op("vector", lambda E_: E_.tensor_copy(r0bf[:], r0t[:]),
                  reads=["r0t"], writes=["r0bf"])
            pr.op("scalar", lambda E_: E_.activation(sq2[:], r0t[:], AF.Square),
                  reads=["r0t"], writes=["sq2"])
            for nch in range(2):
                pr.op("tensor", lambda E_, nch=nch, ot=ot: E_.matmul(
                    psV[0:1, nch*512:(nch+1)*512], onb[:],
                    r0bf[:, nch*512:(nch+1)*512], start=(ot == 0), stop=(ot == NT - 1)),
                      reads=["r0bf", "onb"], writes=["psV"])
                pq2 = pk[4 + nch]
                pr.op("tensor", lambda E_, nch=nch, ot=ot, pq2=pq2: E_.matmul(
                    pq2[0:1, :], onb[:], sq2[:, nch*512:(nch+1)*512],
                    start=(ot == 0), stop=(ot == NT - 1)),
                      reads=["sq2", "onb"], writes=[f"pk{4+nch}"])
        pr.op("vector", lambda E_: E_.tensor_scalar(
            mrow[:], psV[:], 1.0 / 4096.0, None, op0=ALU.mult),
              reads=["psV"], writes=["mrow"])
        pr.op("scalar", lambda E_: E_.activation(vrow[:], mrow[:], AF.Square),
              reads=["mrow"], writes=["vrow"])
        for nch in range(2):
            pq2 = pk[4 + nch]
            pr.op("vector", lambda E_, nch=nch, pq2=pq2: E_.scalar_tensor_tensor(
                vrow[:, nch*512:(nch+1)*512], pq2[0:1, :], 1.0 / 4096.0,
                vrow[:, nch*512:(nch+1)*512], op0=ALU.mult, op1=ALU.subtract),
                  reads=[f"pk{4+nch}", "vrow"], writes=["vrow"])
        pr.op("scalar", lambda E_: E_.activation(vrow[:], vrow[:], AF.Ln, bias=eps1[:]),
              reads=["vrow", "eps1"], writes=["vrow"])
        pr.op("scalar", lambda E_: E_.activation(rsrow[:], vrow[:], AF.Exp, scale=-0.5),
              reads=["vrow"], writes=["rsrow"])
        for nch in range(2):
            pr.op("tensor", lambda E_, nch=nch: E_.matmul(
                pk[nch][:], onr[0:1, :], mrow[0:1, nch*512:(nch+1)*512],
                start=True, stop=True),
                  reads=["mrow", "onr"], writes=[f"pk{nch}"])
            pr.op("tensor", lambda E_, nch=nch: E_.matmul(
                pk[2+nch][:], onr[0:1, :], rsrow[0:1, nch*512:(nch+1)*512],
                start=True, stop=True),
                  reads=["rsrow", "onr"], writes=[f"pk{2+nch}"])
        for dt_ in range(NT):
            pr.dma("sync", r0t[:], r0d[dt_], reads=[f"r0d{dt_}"], writes=["r0t"])
            for nch in range(2):
                pr.op("vector", lambda E_, nch=nch: E_.tensor_tensor(
                    x1t[:, nch*512:(nch+1)*512], r0t[:, nch*512:(nch+1)*512],
                    pk[nch][:], ALU.subtract),
                      reads=["r0t", f"pk{nch}"], writes=["x1t"])
                pr.op("vector", lambda E_, nch=nch: E_.tensor_tensor(
                    x1t[:, nch*512:(nch+1)*512], x1t[:, nch*512:(nch+1)*512],
                    pk[2+nch][:], ALU.mult),
                      reads=[f"pk{2+nch}"], writes=["x1t"])
            pr.dma("gpsimd", x4t[dt_], x1t[:], reads=["x1t"], writes=[f"x4o{dt_}"])
    return


RG = [[0, 1, 2, 3], [4, 5, 6, 7]]


class _NCProxy:
    def __init__(self, nc, prefix):
        self._nc = nc
        self._p = prefix
    def sbuf_tensor(self, name, *a, **k):
        return self._nc.sbuf_tensor(self._p + name, *a, **k)
    def psum_tensor(self, name, *a, **k):
        return self._nc.psum_tensor(self._p + name, *a, **k)
    def __getattr__(self, x):
        return getattr(self._nc, x)


def build_fused(debug=False):
    key = ("fused", debug)
    if key in _L1_CACHE:
        return _L1_CACHE[key]
    nc = bass.Bass("TRN2", target_bir_lowering=False)
    pr = Prog()
    def din(name, shape, dt=F32):
        return nc.dram_tensor(name, list(shape), dt, kind="ExternalInput").ap()
    def dout(name, shape, dt=F32):
        return nc.dram_tensor(name, list(shape), dt, kind="ExternalOutput").ap()
    def dint(name, shape, dt=F32):
        return nc.dram_tensor(name, list(shape), dt, kind="Internal").ap()

    io = {}
    for name, shape, dt in [
        ("xfp", (128, 32, 1024), F32), ("Fp", (8, 128, 32, 512), BF16),
        ("Gcp", (8, 128, 17, 512), BF16), ("GIp", (32, 128, 32, 128), BF16),
        ("identb", (128, 128), BF16), ("zerosb", (128, 128), BF16),
        ("onesb", (128, 1), BF16), ("onesrow", (1, 128), F32),
        ("selE", (4, 64), BF16), ("a0p", (128, 32, 64), F32),
        ("b0p", (64, 32, 128), F32), ("wr0p", (128, 32, 4), F32),
        ("wtip", (81, 128, 32, 128), F32), ("convw", (48, 128, 4), F32),
        ("convb", (48, 128, 1), F32), ("dtb", (64, 1), F32),
        ("alog", (64, 1), F32), ("maskc", (128, 128), F32),
        ("identf", (128, 128), F32), ("onesrowb", (1, 128), BF16),
        ("drep", (128, 32), F32), ("nwp", (128, 32), F32),
        ("wtop", (32, 128, 32, 128), F32), ("flags4", (128, 3), F32),
        ("halosel", (128, 4), F32), ("wtbp", (32, 128, 32, 128), F32),
        ("bbp", (128, 32), F32), ("a1p", (128, 32, 64), F32),
        ("b1p", (32, 128, 8, 128), F32), ("r1p", (128, 32, 16), F32),
        ("selp", (16, 8, 128), BF16),
    ]:
        io[name] = din(name, shape, dt)
    io["x4t"] = dout("x4t", (32, 128, 1024))
    io["x2t"] = dint("x2t_d", (32, 128, 1024))
    io["gate"] = dint("gate_d", (32, 128, 1024))
    io["hbc"] = dint("hbc_d", (48, 128, 1024))
    io["dt_o"] = io["dt_i"] = dint("dt_d", (64, 1024))
    io["xbc"] = dint("xbc_d", (128, 8, 48, 128), BF16)
    io["dtp_i"] = io["dtp_o"] = dint("dtp_d", (64, 1024))
    io["scum_i"] = io["scum_o"] = dint("scum_d", (64, 1024))
    io["x3t"] = dint("x3t_d", (32, 128, 1024))
    io["halo_out"] = dint("halo_in_d", (128, 144))
    halo_ag = dint("halo_ag_d", (512, 144))
    io["STout"] = dint("stloc_d", (128, 4096))
    scll = dint("scll_d", (64, 1))
    st_ag = dint("st_ag_d", (512, 4096))
    scl_ag = dint("scl_ag_d", (256, 1))
    io["st_slot"] = lambda k: (st_ag[k*128:(k+1)*128, :], scl_ag[k*64:(k+1)*64, :])
    if debug:
        io["dbg_x1"] = dout("dbg_x1", (32, 128, 1024))
        io["dbg_corr"] = dout("dbg_corr", (8, 128, 4096))
        io["dbg_s"] = dout("dbg_s", (8, 128, 4096))
        io["dbg_att"] = dout("dbg_att", (32, 128, 1024))

    from contextlib import ExitStack
    with ExitStack() as octx:
        sems = {e: octx.enter_context(nc.semaphore(f"sem_{e}")) for e in Prog.SEMS}
        block = octx.enter_context(nc.Block())

        emit_l0(_NCProxy(nc, "A"), pr, io, debug=debug)
        pr.barrier()
        emit_inproj(_NCProxy(nc, "B"), pr, io)
        pr.barrier()
        HOK = [f"ho{ct}" for ct in range(32, 80)]
        pr.op("gpsimd", lambda E_: E_.collective_compute(
            "AllGather", ALU.bypass, replica_groups=RG,
            ins=[io["halo_out"]], outs=[halo_ag]),
              reads=HOK, writes=["halo_ag"], dma=True)
        pr.barrier()

        def halo_fn(E, pr):
            hsl = E(nc.sbuf_tensor("Xhsl", [128, 576], F32))
            hout = E(nc.sbuf_tensor("Xhout", [128, 144], F32))
            hfl = E(nc.sbuf_tensor("Xhfl", [128, 4], F32))
            pr.dma("sync", hfl[:], io["halosel"], writes=["hfl"])
            for k in range(4):
                pr.dma("sync", hsl[:, k*144:(k+1)*144], halo_ag[k*128:(k+1)*128, :],
                       reads=["halo_ag"], writes=["hsl"])
            pr.op("vector", lambda E_: E_.memset(hout[:], 0.0), writes=["halo_sel"])
            for k in range(4):
                pr.op("vector", lambda E_, k=k: E_.scalar_tensor_tensor(
                    hout[:], hsl[:, k*144:(k+1)*144], hfl[:, k:k+1], hout[:],
                    op0=ALU.mult, op1=ALU.add),
                      reads=["hsl", "hfl", "halo_sel"], writes=["halo_sel"])
            return hout[:], ["halo_sel"]

        emit_conv(_NCProxy(nc, "C"), pr, io, halo_fn)
        pr.barrier()
        emit_scan(_NCProxy(nc, "D"), pr, io, with_init=False, tag="a_", y_work=False)
        pr.dma("gpsimd", scll.unsqueeze(-1), io["scum_o"][:, 1023:1024].unsqueeze(-1),
               reads=["scumo"], writes=["scll"], allow_slow_non_contiguous=True)
        pr.barrier()
        pr.op("gpsimd", lambda E_: E_.collective_compute(
            "AllGather", ALU.bypass, replica_groups=RG,
            ins=[io["STout"]], outs=[st_ag]),
              reads=["STo"], writes=["st_ag"], dma=True)
        pr.op("gpsimd", lambda E_: E_.collective_compute(
            "AllGather", ALU.bypass, replica_groups=RG,
            ins=[scll], outs=[scl_ag]),
              reads=["scll"], writes=["st_ag"], dma=True)
        pr.barrier()
        emit_scan(_NCProxy(nc, "E"), pr, io, with_init=True, debug=debug, tag="b_")
        pr.barrier()
        emit_ffn(_NCProxy(nc, "F"), pr, io)
        pr.finalize(nc, block, sems)
    _L1_CACHE[key] = nc
    return nc


def _declare(nc, io, names, dout_names=()):
    SHAPES = {
        "xfp": ((128, 32, 1024), F32), "Fp": ((8, 128, 32, 512), BF16),
        "Gcp": ((8, 128, 17, 512), BF16), "GIp": ((32, 128, 32, 128), BF16),
        "identb": ((128, 128), BF16), "zerosb": ((128, 128), BF16),
        "onesb": ((128, 1), BF16), "onesrow": ((1, 128), F32),
        "selE": ((4, 64), BF16), "a0p": ((128, 32, 64), F32),
        "b0p": ((64, 32, 128), F32), "wr0p": ((128, 32, 4), F32),
        "wtip": ((81, 128, 32, 128), F32), "convw": ((48, 128, 4), F32),
        "convb": ((48, 128, 1), F32), "dtb": ((64, 1), F32),
        "alog": ((64, 1), F32), "maskc": ((128, 128), F32),
        "identf": ((128, 128), F32), "onesrowb": ((1, 128), BF16),
        "drep": ((128, 32), F32), "nwp": ((128, 32), F32),
        "wtop": ((32, 128, 32, 128), F32), "flags4": ((128, 3), F32),
        "wtbp": ((32, 128, 32, 128), F32), "bbp": ((128, 32), F32),
        "a1p": ((128, 32, 64), F32), "b1p": ((32, 128, 8, 128), F32),
        "r1p": ((128, 32, 16), F32), "selp": ((16, 8, 128), BF16),
        "x2t": ((32, 128, 1024), F32), "x2h": ((128, 32, 3), F32),
        "gate": ((32, 128, 1024), F32), "hbc": ((48, 128, 1024), F32),
        "dt_o": ((64, 1024), F32), "xbc": ((128, 8, 48, 128), BF16),
        "dtp_o": ((64, 1024), F32), "scum_o": ((64, 1024), F32),
        "STout": ((128, 4096), F32), "ST4": ((4, 128, 4096), F32),
        "scl4": ((4, 64, 1), F32), "x3t": ((32, 128, 1024), F32),
        "x4t": ((32, 128, 1024), F32),
    }
    for n in names:
        sh, dt = SHAPES[n]
        io[n] = nc.dram_tensor(n, list(sh), dt, kind="ExternalInput").ap()
    for n in dout_names:
        sh, dt = SHAPES[n]
        io[n] = nc.dram_tensor(n, list(sh), dt, kind="ExternalOutput").ap()


def _wrap(build_fn, key):
    if key in _L1_CACHE:
        return _L1_CACHE[key]
    nc = bass.Bass("TRN2", target_bir_lowering=False)
    pr = Prog()
    from contextlib import ExitStack
    with ExitStack() as octx:
        sems = {e: octx.enter_context(nc.semaphore(f"sem_{e}")) for e in Prog.SEMS}
        block = octx.enter_context(nc.Block())
        build_fn(nc, pr)
        pr.finalize(nc, block, sems)
    _L1_CACHE[key] = nc
    return nc


def build_a():
    def body(nc, pr):
        io = {}
        _declare(nc, io, ["xfp", "Fp", "Gcp", "GIp", "identb", "zerosb", "onesb",
                          "onesrow", "selE", "a0p", "b0p", "wr0p"], ["x2t"])
        emit_l0(_NCProxy(nc, "A"), pr, io)
    return _wrap(body, "A")


def build_b():
    def body(nc, pr):
        io = {}
        _declare(nc, io, ["x2t", "x2h", "wtip", "convw", "convb", "dtb", "alog",
                          "identb", "identf", "onesrow", "onesrowb", "onesb",
                          "maskc", "drep"],
                 ["gate", "hbc", "dt_o", "xbc", "dtp_o", "scum_o", "STout"])
        io["dt_i"] = io["dt_o"]
        io["dtp_i"] = io["dtp_o"]
        io["scum_i"] = io["scum_o"]
        from contextlib import ExitStack
        with ExitStack() as hstack:
            halo_sb = hstack.enter_context(nc.sbuf_tensor("haloSB", [128, 144], F32))
            emit_inproj(_NCProxy(nc, "B"), pr, io, halo_sb=halo_sb[:])
            pr.barrier()
            emit_conv(_NCProxy(nc, "C"), pr, io,
                      lambda E, pr_: (halo_sb[:], ["halo_sel"]))
            pr.barrier()
            emit_scan(_NCProxy(nc, "D"), pr, io, with_init=False, tag="a_",
                      y_work=False)
    return _wrap(body, "B")


def build_c(debug=False):
    def body(nc, pr):
        io = {}
        _declare(nc, io, ["xbc", "dtp_o", "scum_o", "gate", "x2t", "ST4", "scl4",
                          "flags4", "maskc", "identb", "identf", "onesrow",
                          "onesrowb", "onesb", "drep", "nwp", "wtop", "wtbp",
                          "bbp", "a1p", "b1p", "r1p", "selp"], ["x4t"])
        io["dtp_i"] = io["dtp_o"]
        io["scum_i"] = io["scum_o"]
        io["x3t"] = nc.dram_tensor("x3t_d", [32, 128, 1024], F32, kind="Internal").ap()
        io["st_slot"] = lambda k: (io["ST4"][k], io["scl4"][k])
        emit_scan(_NCProxy(nc, "E"), pr, io, with_init=True, tag="b_")
        pr.barrier()
        emit_ffn(_NCProxy(nc, "F"), pr, io)
    return _wrap(body, "C")


_CONST_ALL = ["Fp", "Gcp", "GIp", "identb", "zerosb", "onesb", "onesrow", "selE",
              "a0p", "b0p", "wr0p", "wtip", "convw", "convb", "dtb", "alog",
              "maskc", "identf", "onesrowb", "drep", "nwp", "wtop", "wtbp",
              "bbp", "a1p", "b1p", "r1p", "selp"]


def kernel(**inputs):
    pp = _prep(inputs)
    cores = list(range(NCORE))

    nc = build_a()
    maps = []
    for c in cores:
        m = {k: np.asarray(pp[k]) for k in ["Fp", "Gcp", "GIp", "identb", "zerosb",
                                            "onesb", "onesrow", "selE", "a0p",
                                            "b0p", "wr0p"]}
        m["xfp"] = pp["xfp"][c]
        maps.append(m)
    rA = run_bass_kernel_spmd(nc, maps, core_ids=cores).results

    nc = build_b()
    maps = []
    for c in cores:
        if c % 4 == 0:
            x2h = np.zeros((128, 32, 3), np.float32)
        else:
            x2h = np.ascontiguousarray(
                np.asarray(rA[c - 1]["x2t"])[:, :, 1021:1024].transpose(1, 0, 2))
        m = dict(x2t=rA[c]["x2t"], x2h=x2h, wtip=pp["wtip"], convw=pp["convw"],
                 convb=pp["convb"], dtb=pp["dtb"], alog=pp["alog"],
                 identb=pp["identb"], identf=pp["identf"], onesrow=pp["onesrow"],
                 onesrowb=pp["onesrowb"], onesb=pp["onesb"], maskc=pp["maskc"],
                 drep=pp["drep"])
        maps.append(m)
    rB = run_bass_kernel_spmd(nc, maps, core_ids=cores).results

    nc = build_c()
    maps = []
    for c in cores:
        g0 = 4 * (c // 4)
        ST4 = np.stack([np.asarray(rB[g0 + k]["STout"]) for k in range(4)])
        scl4 = np.stack([np.asarray(rB[g0 + k]["scum_o"])[:, 1023:1024]
                         for k in range(4)])
        fl = np.zeros((128, 3), np.float32)
        if c % 4 > 0:
            fl[:, c % 4 - 1] = 1.0
        m = dict(xbc=rB[c]["xbc"], dtp_o=rB[c]["dtp_o"], scum_o=rB[c]["scum_o"],
                 gate=rB[c]["gate"], x2t=rA[c]["x2t"], ST4=ST4, scl4=scl4,
                 flags4=fl, maskc=pp["maskc"], identb=pp["identb"],
                 identf=pp["identf"], onesrow=pp["onesrow"],
                 onesrowb=pp["onesrowb"], onesb=pp["onesb"], drep=pp["drep"],
                 nwp=pp["nwp"], wtop=pp["wtop"], wtbp=pp["wtbp"], bbp=pp["bbp"],
                 a1p=pp["a1p"], b1p=pp["b1p"], r1p=pp["r1p"], selp=pp["selp"])
        maps.append(m)
    rC = run_bass_kernel_spmd(nc, maps, core_ids=cores).results

    out = np.empty((2, 4096, 4096), np.float32)
    for c in cores:
        b, q = c // 4, c % 4
        out[b, q*1024:(q+1)*1024, :] = np.asarray(
            rC[c]["x4t"]).reshape(4096, 1024).T
    return out


# version 09
# speedup vs baseline: 1.3635x; 1.1832x over previous
import math

import numpy as np
import ml_dtypes
import concourse.bass as bass
import concourse.mybir as mybir
from concourse.bass_utils import run_bass_kernel_spmd

F32 = mybir.dt.float32
BF16 = mybir.dt.bfloat16
AF = mybir.ActivationFunctionType
ALU = mybir.AluOpType
AX = mybir.AxisListType

NCORE = 8
T = 1024
D = 4096
NT = 32
TT = 8
H, P, NS, G, KC = 64, 64, 128, 8, 4
I_ = H * P
GN = G * NS
CONV_DIM = I_ + 2 * GN
PROJ = I_ + CONV_DIM + H
CT_PROJ = 81
Q = 128
NCH = T // Q
EPS = 1e-5

bf16 = ml_dtypes.bfloat16


class Prog:
    ENGINES = ("sync", "gpsimd", "tensor", "vector", "scalar")
    SELF_ORDERED = {"tensor": True, "vector": False, "scalar": False,
                    "gpsimd": False, "sync": False}
    SEMS = ("sync", "gpsimd", "tensor", "vector", "scalar", "syncd", "gpsimdd")

    def __init__(self):
        self.items = {e: [] for e in self.ENGINES}
        self.count = {s: 0 for s in self.SEMS}
        self.last_w = {}
        self.readers = {}
        self.pending = {}
        self.known = {e: {s: 0 for s in self.SEMS} for e in self.ENGINES}

    def op(self, eng, fn, reads=(), writes=(), dma=False):
        sem = eng + "d" if dma else eng
        inc = 16 if dma else 1
        need = {}
        def _need(e2, s2, v2):
            if e2 == eng and s2 == eng and self.SELF_ORDERED[eng]:
                return
            need[s2] = max(need.get(s2, 0), v2)
        for r in reads:
            ev = self.last_w.get(r)
            if ev is not None:
                _need(*ev)
        for w in writes:
            ev = self.last_w.get(w)
            if ev is not None:
                _need(*ev)
            for (e2, s2), v2 in self.readers.get(w, {}).items():
                _need(e2, s2, v2)
        for s2, v2 in self.pending.pop(eng, {}).items():
            need[s2] = max(need.get(s2, 0), v2)
        if dma:
            need[sem] = max(need.get(sem, 0), self.count[sem])
        kn = self.known[eng]
        waits = {s2: v2 for s2, v2 in need.items() if v2 > kn[s2]}
        kn.update(waits)
        self.count[sem] += inc
        val = self.count[sem]
        self.items[eng].append((waits, fn, sem, inc))
        for w in writes:
            self.last_w[w] = (eng, sem, val)
            self.readers[w] = {}
        for r in reads:
            d = self.readers.setdefault(r, {})
            k2 = (eng, sem)
            d[k2] = max(d.get(k2, 0), val)
        return (eng, sem, val)

    def barrier(self):
        snapshot = dict(self.count)
        for e in self.ENGINES:
            kn = self.known[e]
            for s in self.SEMS:
                if snapshot[s] > kn[s]:
                    self.pending.setdefault(e, {})[s] = max(
                        self.pending.get(e, {}).get(s, 0), snapshot[s])

    def dma(self, eng, out, in_, reads=(), writes=(), **kw):
        return self.op(eng, lambda E, o=out, i=in_, kw=kw: E.dma_start(out=o, in_=i, **kw),
                       reads=reads, writes=writes, dma=True)

    def finalize(self, nc, block, sems):
        counts = dict(self.count)
        def emit(name):
            def inner(eng):
                for waits, fn, sem, inc in self.items[name]:
                    for s2, v2 in sorted(waits.items()):
                        eng.wait_ge(sems[s2], v2)
                    inst = fn(eng)
                    inst.then_inc(sems[sem], inc)
                if name == "vector":
                    for s2 in self.SEMS:
                        if s2 != "vector" and counts[s2] > 0:
                            eng.wait_ge(sems[s2], counts[s2])
            return inner
        if self.items["sync"]:
            block.sync(emit("sync"))
        block.vector(emit("vector"))
        if self.items["scalar"]:
            block.scalar(emit("scalar"))
        if self.items["tensor"]:
            block.tensor(emit("tensor"))
        if self.items["gpsimd"]:
            block.gpsimd(emit("gpsimd"))


_PREP = {}

def _dft_tables():
    N = 4096
    f = np.arange(N, dtype=np.float64)
    k = np.arange(2049, dtype=np.float64)
    ang = np.outer(f, k) * (2 * np.pi / N)
    Fp = np.empty((N, N), np.float64)
    Fp[:, :2049] = np.cos(ang)
    Fp[:, 2049:] = -np.sin(ang[:, 1:2048])
    ck = np.full(2049, 2.0 / N); ck[0] = 1.0 / N; ck[2048] = 1.0 / N
    d = np.arange(N, dtype=np.float64)
    ang2 = np.outer(k, d) * (2 * np.pi / N)
    Gc = np.zeros((2176, N), np.float64)
    Gc[:2049] = ck[:, None] * np.cos(ang2)
    GI = np.empty((N, N), np.float64)
    GI[:2049] = ck[:, None] * np.cos(ang2)
    GI[2049:] = -(ck[1:2048, None]) * np.sin(ang2[1:2048])
    return Fp, Gc, GI

def _tile_kpm(W):
    Kin, Mout = W.shape
    ct = (Mout + 127) // 128
    nt = Kin // 128
    Wp = np.zeros((Kin, ct * 128), W.dtype)
    Wp[:, :Mout] = W
    return np.ascontiguousarray(
        Wp.reshape(nt, 128, ct, 128).transpose(2, 1, 0, 3))

def _prep(inputs):
    key = id(inputs.get("L0_attn_x"))
    if _PREP.get("key") == key:
        return _PREP
    _PREP.clear()
    _PREP["key"] = key
    f32 = np.float32

    Fp, Gc, GI = _dft_tables()
    _PREP["Fp"] = np.ascontiguousarray(
        Fp.reshape(32, 128, 8, 512).transpose(2, 1, 0, 3)).astype(bf16)
    _PREP["Gcp"] = np.ascontiguousarray(
        Gc.reshape(17, 128, 8, 512).transpose(2, 1, 0, 3)).astype(bf16)
    _PREP["GIp"] = np.ascontiguousarray(
        GI.reshape(32, 128, 32, 128).transpose(2, 1, 0, 3)).astype(bf16)
    _PREP["identb"] = np.eye(128, dtype=bf16)
    _PREP["identf"] = np.eye(128, dtype=f32)
    _PREP["zerosb"] = np.zeros((128, 128), bf16)
    _PREP["onesb"] = np.ones((128, 1), bf16)
    _PREP["onesrow"] = np.ones((1, 128), f32)
    _PREP["onesrowb"] = np.ones((1, 128), bf16)
    mk = np.zeros((128, 128), f32)
    tt_, tau_ = np.meshgrid(np.arange(128), np.arange(128))
    mk[tau_ > tt_] = -1e9
    _PREP["maskc"] = mk
    selp = np.zeros((16, 8, 128), bf16)
    for pr_ in range(8):
        selp[2*pr_, pr_, 0:64] = bf16(1.0)
        selp[2*pr_+1, pr_, 64:128] = bf16(1.0)
    _PREP["selp"] = selp
    selE = np.zeros((4, 64), bf16)
    for e in range(4):
        selE[e, 16*e:16*e+16] = bf16(4.0)
    _PREP["selE"] = selE

    A0 = inputs["L0_ffn_A"].astype(f32)
    a0t = np.ascontiguousarray(A0.transpose(2, 0, 1).reshape(4096, 64))
    _PREP["a0p"] = np.ascontiguousarray(
        a0t.reshape(32, 128, 64).transpose(1, 0, 2))
    B0 = inputs["L0_ffn_B"].astype(f32)
    b0p = np.zeros((64, 32, 128), f32)
    for pair in range(2):
        for ep in range(2):
            e = 2 * pair + ep
            blk = B0[e].reshape(32, 128, 16).transpose(0, 2, 1)
            b0p[32*pair+16*ep:32*pair+16*ep+16] = blk.transpose(1, 0, 2)
    _PREP["b0p"] = np.ascontiguousarray(b0p)
    Wr0 = inputs["L0_ffn_Wr"].astype(f32)
    _PREP["wr0p"] = np.ascontiguousarray(
        Wr0.T.reshape(32, 128, 4).transpose(1, 0, 2))

    _PREP["wtip"] = _tile_kpm(np.ascontiguousarray(
        inputs["L1_attn_in_proj_w"].astype(f32).T))
    _PREP["wtop"] = _tile_kpm(np.ascontiguousarray(
        inputs["L1_attn_out_proj_w"].astype(f32).T))
    _PREP["convw"] = np.ascontiguousarray(
        inputs["L1_attn_conv_weight"].astype(f32).reshape(48, 128, 4).transpose(1, 0, 2))
    _PREP["convb"] = np.ascontiguousarray(
        inputs["L1_attn_conv_bias"].astype(f32).reshape(48, 128).T)
    _PREP["dtb"] = inputs["L1_attn_dt_bias"].astype(f32).reshape(64, 1)
    _PREP["alog"] = inputs["L1_attn_A_log"].astype(f32).reshape(64, 1)
    _PREP["drep"] = np.ascontiguousarray(
        np.repeat(inputs["L1_attn_D"].astype(f32), 64).reshape(32, 128).T)
    _PREP["nwp"] = np.ascontiguousarray(
        inputs["L1_attn_norm_weight"].astype(f32).reshape(32, 128).T)

    _PREP["wtbp"] = _tile_kpm(np.ascontiguousarray(
        inputs["L1_ffn_base_w"].astype(f32).T))
    _PREP["bbp"] = np.ascontiguousarray(
        inputs["L1_ffn_base_b"].astype(f32).reshape(32, 128).T)
    A1 = inputs["L1_ffn_A"].astype(f32)
    _PREP["a1p"] = np.ascontiguousarray(A1.T.reshape(32, 128, 64).transpose(1, 0, 2))
    B1 = inputs["L1_ffn_B"].astype(f32)
    b1p = np.zeros((8, 32, 128, 128), f32)
    for pair in range(8):
        for ep in range(2):
            e = 2 * pair + ep
            blk = B1[e].reshape(32, 128, 64).transpose(0, 2, 1)
            b1p[pair, :, 64*ep:64*ep+64, :] = blk
    _PREP["b1p"] = np.ascontiguousarray(b1p.transpose(1, 2, 0, 3))
    _PREP["r1p"] = np.ascontiguousarray(
        inputs["L1_ffn_router_w"].astype(f32).T.reshape(32, 128, 16).transpose(1, 0, 2))

    x = inputs["L0_attn_x"].astype(f32)
    xfp = []
    for c in range(NCORE):
        b, q = c // 4, c % 4
        xs = np.ascontiguousarray(x[b, q*1024:(q+1)*1024, :].T)
        xfp.append(np.ascontiguousarray(
            xs.reshape(32, 128, 1024).transpose(1, 0, 2)))
    _PREP["xfp"] = xfp
    return _PREP


_L1_CACHE = {}

def emit_l0(nc, pr, io, debug=False):
    def dint(name, shape, dt):
        return nc.dram_tensor(name, list(shape), dt, kind="Internal").ap()
    xfp = io["xfp"]; Fp = io["Fp"]; Gcp = io["Gcp"]; GIp = io["GIp"]
    identb = io["identb"]; zerosb = io["zerosb"]; onesb = io["onesb"]
    onesrow = io["onesrow"]; selE = io["selE"]
    a0p = io["a0p"]; b0p = io["b0p"]; wr0p = io["wr0p"]
    x2t = io["x2t"]
    if debug:
        dbg_x1 = io["dbg_x1"]; dbg_corr = io["dbg_corr"]; dbg_s = io["dbg_s"]

    Xd = dint("l0_Xd", (8, 128, 4096), BF16)
    Sd = dint("l0_Sd", (8, 128, 4096), BF16)
    r0d = dint("l0_r0d", (32, 128, 1024), F32)
    x1d = dint("l0_x1d", (32, 128, 1024), F32)

    from contextlib import ExitStack
    if True:
      with ExitStack() as ctx:
        E = ctx.enter_context
        ar1 = E(nc.sbuf_tensor("ar1", [128, 32 * 1024], BF16))
        ar2 = E(nc.sbuf_tensor("ar2", [128, 32 * 1024], BF16))
        ar3f = E(nc.sbuf_tensor("ar3f", [128, 9280], F32))
        ar3b = ar3f[:].bitcast(BF16)
        xbf = ar1[:]
        corr = ar1[:]
        pT = ar2[:, 0:17408]
        sT = ar2[:]
        x1bf = ar2[:]
        fbuf = [ar3b[:, 0:16384], ar3b[:, 0:16384]]
        gcb = [ar3b[:, 0:8704], ar3b[:, 0:8704]]
        gib = [ar3b[:, 0:4096], ar3b[:, 4096:8192]]
        pfa = ar3f[:, 0:2049]
        pfb = ar3f[:, 2049:4096]
        xt_B1 = ar3b[:, 8192:12288]
        pbf = ar3b[:, 12288:14464]
        e_f = ar3f[:, 0:4096]
        mask_f = ar3f[:, 4096:8192]
        xt_D = ar3b[:, 0:4096]
        st_ = ar3b[:, 4096:8192]
        pfaD = ar3f[:, 4096:6145]
        pfbD = ar3f[:, 6145:8192]
        a0sb = ar3b[:, 0:2048]
        b0sb = ar3b[0:64, 2048:6144]
        wr0sb = ar3b[:, 6144:6272]
        gT = ar3b[0:4, 6272:7296]
        mid_f = ar3f[0:64, 3648:4672]
        midg = ar3b[0:64, 9344:11392]
        evb = [E(nc.sbuf_tensor(f"evb{i}", [128, 512], BF16)) for i in range(4)]
        m8b = E(nc.sbuf_tensor("m8b", [128, 8], BF16))
        m8f = E(nc.sbuf_tensor("m8f", [128, 8], F32))
        e8f = E(nc.sbuf_tensor("e8f", [128, 8], F32))
        negm1 = E(nc.sbuf_tensor("negm1", [128, 1], F32))
        theta = E(nc.sbuf_tensor("theta", [128, 1], F32))
        zrow = E(nc.sbuf_tensor("zrow", [128, 1], F32))
        rz = E(nc.sbuf_tensor("rz", [128, 1], F32))
        ones_b = E(nc.sbuf_tensor("ones_b", [128, 1], BF16))
        ones_r = E(nc.sbuf_tensor("ones_r", [1, 128], F32))
        ident_b = E(nc.sbuf_tensor("ident_b", [128, 128], BF16))
        xfstr = E(nc.sbuf_tensor("xfstr", [128, 1024], F32))
        r0bf = E(nc.sbuf_tensor("r0bf", [128, 1024], BF16))
        sqbf = E(nc.sbuf_tensor("sqbf", [128, 1024], BF16))
        r0t = E(nc.sbuf_tensor("r0t", [128, 1024], F32))
        x1t = E(nc.sbuf_tensor("x1t", [128, 1024], F32))
        mrow = E(nc.sbuf_tensor("mrow", [1, 1024], F32))
        vrow = E(nc.sbuf_tensor("vrow", [1, 1024], F32))
        rsrow = E(nc.sbuf_tensor("rsrow", [1, 1024], F32))
        sel_sb = E(nc.sbuf_tensor("sel_sb", [4, 64], BF16))
        logit = E(nc.sbuf_tensor("logit", [128, 8 * 4], F32))
        gwork = E(nc.sbuf_tensor("gwork", [128, 4], F32))
        gmax = E(nc.sbuf_tensor("gmax", [128, 1], F32))
        gmax2 = E(nc.sbuf_tensor("gmax2", [128, 1], F32))
        gz = E(nc.sbuf_tensor("gz", [128, 1], F32))
        g4 = E(nc.sbuf_tensor("g4", [128, 8 * 4], F32))
        g4b = E(nc.sbuf_tensor("g4b", [128, 4], BF16))
        eps1 = E(nc.sbuf_tensor("eps1", [1, 1], F32))

        ps = [E(nc.psum_tensor(f"ps{i}", [128, 512], F32)) for i in range(4)]
        psTb = [ps[2][:].bitcast(BF16)[:, 0:128], ps[3][:].bitcast(BF16)[:, 0:128]]
        psS1 = E(nc.psum_tensor("psS1", [1, 1024], F32))
        psS2 = E(nc.psum_tensor("psS2", [1, 1024], F32))

        def ckpt(k):
            pass

        pr.op("vector", lambda E_: E_.memset(eps1[:], EPS), writes=["eps1"])
        pr.dma("sync", ones_b[:], onesb, writes=["ones_b"])
        pr.dma("sync", ones_r[:], onesrow, writes=["ones_r"])
        pr.dma("sync", ident_b[:], identb, writes=["ident_b"])
        pr.dma("sync", sel_sb[:], selE, writes=["sel_sb"])
        pr.dma("gpsimd", xbf[:], xfp.rearrange("p ft t -> p (ft t)"), writes=["xbf"])

        for kc in range(8):
            fb = fbuf[kc % 2]
            pr.dma("sync", fb[:], Fp[kc].rearrange("p ft k -> p (ft k)"),
                   reads=[], writes=["fbuf"])
            for half in range(2):
                for tt4 in range(4):
                    tt = half * 4 + tt4
                    pk = ps[tt4]
                    for ft in range(NT):
                        pr.op("tensor",
                              lambda E_, tt=tt, ft=ft, pk=pk, fb=fb, kc=kc:
                              E_.matmul(pk[:], xbf[:, ft*1024 + tt*128: ft*1024 + tt*128 + 128],
                                        fb[:, ft*512: ft*512 + 512],
                                        start=(ft == 0), stop=(ft == NT - 1)),
                              reads=["xbf", "fbuf"], writes=[f"ps{tt4}"])
                    ev = evb[tt4]
                    pr.op("scalar", lambda E_, pk=pk, ev=ev: E_.copy(ev[:], pk[:]),
                          reads=[f"ps{tt4}"], writes=[f"evb{tt4}"])
                    pr.dma("gpsimd", Xd[tt][:, kc*512:(kc+1)*512], ev[:],
                           reads=[f"evb{tt4}"], writes=[f"Xd{tt}{kc}"])

        ckpt(1)
        pr.barrier()
        pr.dma("sync", pbf[:, 2048:2176], zerosb[:, 0:128], writes=["pbf_pad"])
        for tt in range(8):
            pr.dma("sync", xt_B1[:], Xd[tt],
                   reads=[f"Xd{tt}{k}" for k in range(8)], writes=["xt_"])
            pr.op("scalar", lambda E_: E_.activation(pfa[:], xt_B1[:, 0:2049], AF.Square),
                  reads=["xt_"], writes=["pfa"])
            pr.op("scalar", lambda E_: E_.activation(pfb[:], xt_B1[:, 2049:4096], AF.Square),
                  reads=["xt_"], writes=["pfb"])
            pr.op("vector", lambda E_: E_.tensor_copy(pbf[:, 0:1], pfa[:, 0:1]),
                  reads=["pfa"], writes=["pbf_c0"])
            pr.op("vector", lambda E_: E_.tensor_copy(pbf[:, 2048:2049], pfa[:, 2048:2049]),
                  reads=["pfa"], writes=["pbf_c2048"])
            pr.op("vector", lambda E_: E_.tensor_tensor(pbf[:, 1:2048], pfa[:, 1:2048],
                                                        pfb[:], ALU.add),
                  reads=["pfa", "pfb"], writes=["pbf"])
            for kt in range(17):
                pb = psTb[kt % 2]
                pr.op("tensor", lambda E_, kt=kt, pb=pb:
                      E_.transpose(pb[:], pbf[:, kt*128:(kt+1)*128], ident_b[:]),
                      reads=["pbf", "pbf_c0", "pbf_c2048", "pbf_pad", "ident_b"], writes=[f"ps{2+kt%2}"])
                pr.op("scalar", lambda E_, tt=tt, kt=kt, pb=pb:
                      E_.copy(pT[:, tt*2176 + kt*128: tt*2176 + kt*128 + 128], pb[:]),
                      reads=[f"ps{2+kt%2}"], writes=[f"pT{tt}"])

        ckpt(2)
        pr.barrier()
        for dc in range(8):
            gb = gcb[dc % 2]
            pr.dma("sync", gb[:], Gcp[dc].rearrange("p kt k -> p (kt k)"),
                   writes=["gcb"])
            for half in range(2):
                for tt4 in range(4):
                    tt = half * 4 + tt4
                    pk = ps[tt4]
                    for kt in range(17):
                        pr.op("tensor", lambda E_, tt=tt, kt=kt, pk=pk, gb=gb:
                              E_.matmul(pk[:], pT[:, tt*2176 + kt*128: tt*2176 + kt*128 + 128],
                                        gb[:, kt*512:(kt+1)*512],
                                        start=(kt == 0), stop=(kt == 16)),
                              reads=[f"pT{tt}", "gcb"], writes=[f"ps{tt4}"])
                    pr.op("scalar", lambda E_, tt=tt, dc=dc, pk=pk:
                          E_.copy(corr[:, tt*4096 + dc*512: tt*4096 + dc*512 + 512], pk[:]),
                          reads=[f"ps{tt4}"], writes=[f"corr{tt}"])

        ckpt(3)
        pr.barrier()
        for tt in range(8):
            c_sl = corr[:, tt*4096:(tt+1)*4096]
            if debug:
                pr.dma("gpsimd", dbg_corr[tt], c_sl, reads=[f"corr{tt}"],
                       writes=[f"dbgc{tt}"])
            pr.op("vector", lambda E_, c_sl=c_sl: E_.max(out=m8b[:], in_=c_sl),
                  reads=[f"corr{tt}"], writes=["m8b"])
            pr.op("vector", lambda E_: E_.tensor_copy(m8f[:], m8b[:]),
                  reads=["m8b"], writes=["m8f"])
            pr.op("vector", lambda E_: E_.tensor_scalar(negm1[:], m8f[:, 0:1], -1.0, None,
                                                        op0=ALU.mult),
                  reads=["m8f"], writes=["negm1"])
            pr.op("vector", lambda E_: E_.tensor_copy(theta[:], m8f[:, 7:8]),
                  reads=["m8f"], writes=["theta"])
            pr.op("scalar", lambda E_: E_.activation(e8f[:], m8f[:], AF.Exp,
                                                     bias=negm1[:], accum_out=zrow[:]),
                  reads=["m8f", "negm1"], writes=["e8f", "zrow"])
            pr.op("vector", lambda E_: E_.reciprocal(rz[:], zrow[:]),
                  reads=["zrow"], writes=["rz"])
            pr.op("scalar", lambda E_, c_sl=c_sl: E_.activation(e_f[:], c_sl, AF.Exp,
                                                                bias=negm1[:]),
                  reads=[f"corr{tt}", "negm1"], writes=["e_f"])
            pr.op("vector", lambda E_, c_sl=c_sl: E_.tensor_scalar(
                mask_f[:], c_sl, theta[:], None, op0=ALU.is_ge),
                  reads=[f"corr{tt}", "theta"], writes=["mask_f"])
            pr.op("vector", lambda E_, c_sl=c_sl: E_.scalar_tensor_tensor(
                c_sl, e_f[:], rz[:], mask_f[:], op0=ALU.mult, op1=ALU.mult),
                  reads=["e_f", "rz", "mask_f"], writes=[f"corr{tt}"])
            if debug:
                pr.dma("gpsimd", dbg_s[tt], c_sl, reads=[f"corr{tt}"],
                       writes=[f"dbgs{tt}"])
            for ft in range(NT):
                pb = psTb[ft % 2]
                pr.op("tensor", lambda E_, ft=ft, pb=pb, c_sl=c_sl:
                      E_.transpose(pb[:], c_sl[:, ft*128:(ft+1)*128], ident_b[:]),
                      reads=[f"corr{tt}", "ident_b"], writes=[f"ps{2+ft%2}"])
                pr.op("scalar", lambda E_, tt=tt, ft=ft, pb=pb:
                      E_.copy(sT[:, tt*4096 + ft*128: tt*4096 + ft*128 + 128], pb[:]),
                      reads=[f"ps{2+ft%2}"], writes=[f"sT{tt}", "sTbuf"])

        ckpt(4)
        pr.barrier()
        for kc in range(8):
            fb = fbuf[kc % 2]
            pr.dma("sync", fb[:], Fp[kc].rearrange("p ft k -> p (ft k)"),
                   writes=["fbuf"])
            for half in range(2):
                for tt4 in range(4):
                    tt = half * 4 + tt4
                    pk = ps[tt4]
                    for ft in range(NT):
                        pr.op("tensor", lambda E_, tt=tt, ft=ft, pk=pk, fb=fb:
                              E_.matmul(pk[:], sT[:, tt*4096 + ft*128: tt*4096 + ft*128 + 128],
                                        fb[:, ft*512: ft*512 + 512],
                                        start=(ft == 0), stop=(ft == NT - 1)),
                              reads=[f"sT{tt}", "sTbuf", "fbuf"], writes=[f"ps{tt4}"])
                    ev = evb[tt4]
                    pr.op("scalar", lambda E_, pk=pk, ev=ev: E_.copy(ev[:], pk[:]),
                          reads=[f"ps{tt4}"], writes=[f"evb{tt4}"])
                    pr.dma("gpsimd", Sd[tt][:, kc*512:(kc+1)*512], ev[:],
                           reads=[f"evb{tt4}"], writes=[f"Sd{tt}{kc}"])

        ckpt(5)
        pr.barrier()
        for tt in range(8):
            y_sl = corr[:, tt*4096:(tt+1)*4096]
            pr.dma("sync", xt_D[:], Xd[tt],
                   reads=[f"Xd{tt}{k}" for k in range(8)], writes=["xt_"])
            pr.dma("sync", st_[:], Sd[tt],
                   reads=[f"Sd{tt}{k}" for k in range(8)], writes=["st_"])
            pr.op("vector", lambda E_, y=y_sl: E_.tensor_tensor(
                y[:, 0:2049], xt_D[:, 0:2049], st_[:, 0:2049], ALU.mult),
                  reads=["xt_", "st_"], writes=[f"corr{tt}"])
            pr.op("vector", lambda E_: E_.tensor_tensor(
                pfbD[:], xt_D[:, 2049:4096], st_[:, 2049:4096], ALU.mult),
                  reads=["xt_", "st_"], writes=["pfb"])
            pr.op("vector", lambda E_, y=y_sl: E_.tensor_tensor(
                y[:, 1:2048], y[:, 1:2048], pfbD[:], ALU.add),
                  reads=["pfb"], writes=[f"corr{tt}"])
            pr.op("vector", lambda E_: E_.tensor_tensor(
                pfaD[:, 0:2047], xt_D[:, 2049:4096], st_[:, 1:2048], ALU.mult),
                  reads=["xt_", "st_"], writes=["pfa"])
            pr.op("vector", lambda E_: E_.tensor_tensor(
                pfbD[:], xt_D[:, 1:2048], st_[:, 2049:4096], ALU.mult),
                  reads=["xt_", "st_"], writes=["pfb"])
            pr.op("vector", lambda E_, y=y_sl: E_.tensor_tensor(
                y[:, 2049:4096], pfaD[:, 0:2047], pfbD[:], ALU.subtract),
                  reads=["pfa", "pfb"], writes=[f"corr{tt}"])
            for st in range(NT):
                pb = psTb[st % 2]
                pr.op("tensor", lambda E_, st=st, pb=pb, y=y_sl:
                      E_.transpose(pb[:], y[:, st*128:(st+1)*128], ident_b[:]),
                      reads=[f"corr{tt}", "ident_b"], writes=[f"ps{2+st%2}"])
                pr.op("scalar", lambda E_, tt=tt, st=st, pb=pb:
                      E_.copy(sT[:, st*1024 + tt*128: st*1024 + tt*128 + 128], pb[:]),
                      reads=[f"ps{2+st%2}"], writes=[f"yT{st}", "sTbuf"])

        ckpt(6)
        pr.barrier()
        for dt_ in range(NT):
            gb = gib[dt_ % 2]
            pr.dma("sync", gb[:], GIp[dt_].rearrange("p st j -> p (st j)"),
                   writes=[f"gib{dt_%2}"])
            pr.dma("sync", xfstr[:], xfp[:, dt_, :], writes=["xfstr"])
            for nch in range(2):
                pk = ps[nch]
                for st in range(NT):
                    pr.op("tensor", lambda E_, st=st, nch=nch, pk=pk, gb=gb:
                          E_.matmul(pk[:], gb[:, st*128:(st+1)*128],
                                    sT[:, st*1024 + nch*512: st*1024 + nch*512 + 512],
                                    start=(st == 0), stop=(st == NT - 1)),
                          reads=[f"gib{dt_%2}", f"yT{st}", "sTbuf"],
                          writes=[f"ps{nch}"])
                pr.op("vector", lambda E_, nch=nch, pk=pk: E_.tensor_tensor(
                    r0t[:, nch*512:(nch+1)*512], pk[:], xfstr[:, nch*512:(nch+1)*512],
                    ALU.add),
                      reads=[f"ps{nch}", "xfstr"], writes=["r0t"])
            pr.dma("gpsimd", r0d[dt_], r0t[:], reads=["r0t"], writes=[f"r0d{dt_}"])
            pr.op("vector", lambda E_: E_.tensor_copy(r0bf[:], r0t[:]),
                  reads=["r0t"], writes=["r0bf"])
            pr.op("scalar", lambda E_: E_.activation(sqbf[:], r0t[:], AF.Square),
                  reads=["r0t"], writes=["sqbf"])
            for nch in range(2):
                pr.op("tensor", lambda E_, nch=nch, dt_=dt_:
                      E_.matmul(psS1[0:1, nch*512:(nch+1)*512], ones_b[:],
                                r0bf[:, nch*512:(nch+1)*512],
                                start=(dt_ == 0), stop=(dt_ == NT - 1)),
                      reads=["r0bf", "ones_b"], writes=["psS1"])
                pr.op("tensor", lambda E_, nch=nch, dt_=dt_:
                      E_.matmul(psS2[0:1, nch*512:(nch+1)*512], ones_b[:],
                                sqbf[:, nch*512:(nch+1)*512],
                                start=(dt_ == 0), stop=(dt_ == NT - 1)),
                      reads=["sqbf", "ones_b"], writes=["psS2"])

        def ln_rows(tag):
            pr.op("vector", lambda E_: E_.tensor_scalar(
                mrow[:], psS1[:], 1.0 / 4096.0, None, op0=ALU.mult),
                  reads=["psS1"], writes=["mrow", "srowH"])
            pr.op("scalar", lambda E_: E_.activation(vrow[:], mrow[:], AF.Square),
                  reads=["mrow"], writes=["vrow", "srowH"])
            pr.op("vector", lambda E_: E_.scalar_tensor_tensor(
                vrow[:], psS2[:], 1.0 / 4096.0, vrow[:], op0=ALU.mult, op1=ALU.subtract),
                  reads=["psS2", "vrow"], writes=["vrow", "srowH"])
            pr.op("scalar", lambda E_: E_.activation(vrow[:], vrow[:], AF.Ln, bias=eps1[:]),
                  reads=["vrow", "eps1"], writes=["vrow", "srowH"])
            pr.op("scalar", lambda E_: E_.activation(rsrow[:], vrow[:], AF.Exp, scale=-0.5),
                  reads=["vrow"], writes=["rsrow", "srowH"])

        def ln_bcast():
            for nch in range(2):
                pr.op("tensor", lambda E_, nch=nch: E_.matmul(
                    ps[nch][:], ones_r[0:1, :], mrow[0:1, nch*512:(nch+1)*512],
                    start=True, stop=True),
                      reads=["mrow", "ones_r"], writes=[f"ps{nch}"])
                pr.op("tensor", lambda E_, nch=nch: E_.matmul(
                    ps[2+nch][:], ones_r[0:1, :], rsrow[0:1, nch*512:(nch+1)*512],
                    start=True, stop=True),
                      reads=["rsrow", "ones_r"], writes=[f"ps{2+nch}"])

        ln_rows("ln1")
        ln_bcast()
        ckpt(7)
        pr.barrier()
        for dt_ in range(NT):
            pr.dma("sync", r0t[:], r0d[dt_], reads=[f"r0d{dt_}"], writes=["r0t"])
            for nch in range(2):
                pr.op("vector", lambda E_, nch=nch: E_.tensor_tensor(
                    x1t[:, nch*512:(nch+1)*512], r0t[:, nch*512:(nch+1)*512],
                    ps[nch][:], ALU.subtract),
                      reads=["r0t", f"ps{nch}"], writes=["x1t"])
                pr.op("vector", lambda E_, nch=nch: E_.tensor_tensor(
                    x1t[:, nch*512:(nch+1)*512], x1t[:, nch*512:(nch+1)*512],
                    ps[2+nch][:], ALU.mult),
                      reads=[f"ps{2+nch}"], writes=["x1t"])
            pr.dma("gpsimd", x1d[dt_], x1t[:], reads=["x1t"], writes=[f"x1d{dt_}"])
            if debug:
                pr.dma("gpsimd", dbg_x1[dt_], x1t[:], reads=["x1t"], writes=[f"dbgx1{dt_}"])
            pr.op("vector", lambda E_, dt_=dt_: E_.tensor_copy(
                x1bf[:, dt_*1024:(dt_+1)*1024], x1t[:]),
                  reads=["x1t"], writes=[f"x1bf{dt_}"])

        ckpt(8)
        pr.barrier()
        pr.dma("gpsimd", a0sb[:], a0p.rearrange("p ft j -> p (ft j)"), writes=["a0sb"])
        pr.dma("gpsimd", b0sb[:], b0p.rearrange("r ot j -> r (ot j)"), writes=["b0sb"])
        pr.dma("gpsimd", wr0sb[:], wr0p.rearrange("p ft j -> p (ft j)"), writes=["wr0sb"])
        for tt in range(8):
            for ft in range(NT):
                pr.op("tensor", lambda E_, tt=tt, ft=ft: E_.matmul(
                    ps[0][:, 0:4], x1bf[:, ft*1024 + tt*128: ft*1024 + tt*128 + 128],
                    wr0sb[:, ft*4:(ft+1)*4], start=(ft == 0), stop=(ft == NT - 1)),
                      reads=[f"x1bf{ft}", "wr0sb"], writes=["ps0"])
            pr.op("scalar", lambda E_, tt=tt: E_.copy(logit[:, tt*4:(tt+1)*4],
                                                      ps[0][:, 0:4]),
                  reads=["ps0"], writes=["logit"])
        ckpt(9)
        for tt in range(8):
            lg = logit[:, tt*4:(tt+1)*4]
            pr.op("vector", lambda E_, lg=lg: E_.tensor_reduce(
                gmax[:], lg, AX.X, ALU.max), reads=["logit"], writes=["gmax"])
            pr.op("vector", lambda E_, lg=lg: E_.tensor_scalar(
                gwork[:], lg, gmax[:], -1e30, op0=ALU.is_ge, op1=ALU.mult),
                  reads=["logit", "gmax"], writes=["gwork"])
            pr.op("vector", lambda E_, lg=lg: E_.tensor_tensor(
                gwork[:], gwork[:], lg, ALU.add), reads=["logit"], writes=["gwork"])
            pr.op("vector", lambda E_: E_.tensor_reduce(
                gmax2[:], gwork[:], AX.X, ALU.max), reads=["gwork"], writes=["gmax2"])
            pr.op("vector", lambda E_: E_.tensor_scalar(
                negm1[:], gmax[:], -1.0, None, op0=ALU.mult),
                  reads=["gmax"], writes=["negm1"])
            pr.op("scalar", lambda E_, lg=lg, tt=tt: E_.activation(
                g4[:, tt*4:(tt+1)*4], lg, AF.Exp, bias=negm1[:]),
                  reads=["logit", "negm1"], writes=["g4"])
            pr.op("vector", lambda E_, lg=lg: E_.tensor_scalar(
                gwork[:], lg, gmax2[:], None, op0=ALU.is_ge),
                  reads=["logit", "gmax2"], writes=["gwork"])
            pr.op("vector", lambda E_, tt=tt: E_.tensor_tensor(
                g4[:, tt*4:(tt+1)*4], g4[:, tt*4:(tt+1)*4], gwork[:], ALU.mult),
                  reads=["gwork"], writes=["g4"])
            pr.op("vector", lambda E_, tt=tt: E_.tensor_reduce(
                gz[:], g4[:, tt*4:(tt+1)*4], AX.X, ALU.add),
                  reads=["g4"], writes=["gz"])
            pr.op("vector", lambda E_: E_.reciprocal(rz[:], gz[:]),
                  reads=["gz"], writes=["rz"])
            pr.op("vector", lambda E_, tt=tt: E_.tensor_scalar(
                g4[:, tt*4:(tt+1)*4], g4[:, tt*4:(tt+1)*4], rz[:], None, op0=ALU.mult),
                  reads=["rz"], writes=["g4"])
            pr.op("vector", lambda E_, tt=tt: E_.tensor_copy(
                g4b[:], g4[:, tt*4:(tt+1)*4]), reads=["g4"], writes=["g4b"])
            pr.op("tensor", lambda E_: E_.transpose(
                psTb[0][0:4, 0:128], g4b[:], ident_b[:]),
                  reads=["g4b", "ident_b"], writes=["ps2"])
            pr.op("scalar", lambda E_, tt=tt: E_.copy(
                gT[0:4, tt*128:(tt+1)*128], psTb[0][0:4, 0:128]),
                  reads=["ps2"], writes=["gT"])
        ckpt(10)
        for nch in range(2):
            pr.op("tensor", lambda E_, nch=nch: E_.matmul(
                ps[2][0:64, :], sel_sb[:], gT[:, nch*512:(nch+1)*512],
                start=True, stop=True),
                  reads=["sel_sb", "gT"], writes=["ps2"])
            for ft in range(NT):
                pr.op("tensor", lambda E_, ft=ft, nch=nch: E_.matmul(
                    ps[3][0:64, :], a0sb[:, ft*64:(ft+1)*64],
                    x1bf[:, ft*1024 + nch*512: ft*1024 + nch*512 + 512],
                    start=(ft == 0), stop=(ft == NT - 1)),
                      reads=["a0sb", f"x1bf{ft}"], writes=["ps3"])
            pr.op("scalar", lambda E_, nch=nch: E_.copy(
                mid_f[:, nch*512:(nch+1)*512], ps[3][0:64, :]),
                  reads=["ps3"], writes=["mid_f"])
            pr.op("vector", lambda E_, nch=nch: E_.tensor_tensor(
                midg[:, nch*512:(nch+1)*512], mid_f[:, nch*512:(nch+1)*512],
                ps[2][0:64, :], ALU.mult),
                  reads=["mid_f", "ps2"], writes=["midg"])

        ckpt(11)
        for ot in range(NT):
            pr.dma("sync", xfstr[:], x1d[ot], reads=[f"x1d{ot}"], writes=["xfstr"])
            for nch in range(2):
                pk = ps[nch]
                pr.op("tensor", lambda E_, ot=ot, nch=nch, pk=pk:
                      E_.matmul(pk[:], b0sb[0:64, ot*128:(ot+1)*128],
                                midg[0:64, nch*512:(nch+1)*512],
                                start=True, stop=True),
                      reads=["b0sb", "midg"], writes=[f"ps{nch}"])
                pr.op("vector", lambda E_, nch=nch, pk=pk: E_.tensor_tensor(
                    r0t[:, nch*512:(nch+1)*512], pk[:], xfstr[:, nch*512:(nch+1)*512],
                    ALU.add),
                      reads=[f"ps{nch}", "xfstr"], writes=["r0t"])
            pr.dma("gpsimd", r0d[ot], r0t[:], reads=["r0t"], writes=[f"r0d{ot}"])
            pr.op("vector", lambda E_: E_.tensor_copy(r0bf[:], r0t[:]),
                  reads=["r0t"], writes=["r0bf"])
            pr.op("scalar", lambda E_: E_.activation(sqbf[:], r0t[:], AF.Square),
                  reads=["r0t"], writes=["sqbf"])
            for nch in range(2):
                pr.op("tensor", lambda E_, nch=nch, ot=ot: E_.matmul(
                    psS1[0:1, nch*512:(nch+1)*512], ones_b[:],
                    r0bf[:, nch*512:(nch+1)*512], start=(ot == 0), stop=(ot == NT - 1)),
                      reads=["r0bf", "ones_b"], writes=["psS1"])
                pr.op("tensor", lambda E_, nch=nch, ot=ot: E_.matmul(
                    psS2[0:1, nch*512:(nch+1)*512], ones_b[:],
                    sqbf[:, nch*512:(nch+1)*512], start=(ot == 0), stop=(ot == NT - 1)),
                      reads=["sqbf", "ones_b"], writes=["psS2"])
        ckpt(12)
        ln_rows("ln2")
        ln_bcast()
        for dt_ in range(NT):
            pr.dma("sync", r0t[:], r0d[dt_], reads=[f"r0d{dt_}"], writes=["r0t"])
            for nch in range(2):
                pr.op("vector", lambda E_, nch=nch: E_.tensor_tensor(
                    x1t[:, nch*512:(nch+1)*512], r0t[:, nch*512:(nch+1)*512],
                    ps[nch][:], ALU.subtract),
                      reads=["r0t", f"ps{nch}"], writes=["x1t"])
                pr.op("vector", lambda E_, nch=nch: E_.tensor_tensor(
                    x1t[:, nch*512:(nch+1)*512], x1t[:, nch*512:(nch+1)*512],
                    ps[2+nch][:], ALU.mult),
                      reads=[f"ps{2+nch}"], writes=["x1t"])
            pr.dma("gpsimd", x2t[dt_], x1t[:], reads=["x1t"], writes=[f"x2t{dt_}"])

    return


def emit_inproj(nc, pr, io, halo_sb=None):
    x2t = io["x2t"]; wtip = io["wtip"]
    gate = io["gate"]; hbc = io["hbc"]; dt_o = io["dt_o"]
    halo_out = io.get("halo_out")
    from contextlib import ExitStack
    with ExitStack() as ctx:
        E = ctx.enter_context
        if halo_sb is not None:
            x2h = io["x2h"]
            x2hbf = E(nc.sbuf_tensor("x2hbf", [128, 96], BF16))
            psH = E(nc.psum_tensor("psH", [128, 512], F32))
            pr.dma("gpsimd", x2hbf[:], x2h.rearrange("p ft j -> p (ft j)"),
                   writes=["x2hbf"])
        x2bf = E(nc.sbuf_tensor("x2bf", [128, 32 * 1024], BF16))
        wst = [E(nc.sbuf_tensor(f"wst{i}", [128, 32 * 128], BF16)) for i in range(2)]
        ev = [E(nc.sbuf_tensor(f"ev{i}", [128, 1024], F32)) for i in range(2)]
        ps = [E(nc.psum_tensor(f"ps{i}", [128, 512], F32)) for i in range(4)]
        pr.dma("gpsimd", x2bf[:].rearrange("p (ft t) -> p ft t", ft=32),
               x2t.transpose([1, 0, 2]), writes=["x2bfA"])
        XR = ["x2bfA"]
        for ct in range(CT_PROJ):
            w = wst[ct % 2]
            pr.dma("gpsimd", w[:], wtip[ct].rearrange("p ft j -> p (ft j)"),
                   writes=[f"wst{ct%2}"])
            ee = ev[ct % 2]
            for nch in range(2):
                pk = ps[(ct % 2) * 2 + nch]
                for ft in range(NT):
                    pr.op("tensor", lambda E_, ft=ft, nch=nch, pk=pk, w=w:
                          E_.matmul(pk[:], w[:, ft*128:(ft+1)*128],
                                    x2bf[:, ft*1024 + nch*512: ft*1024 + nch*512 + 512],
                                    start=(ft == 0), stop=(ft == NT - 1)),
                          reads=[f"wst{ct%2}"] + XR, writes=[f"ps{(ct%2)*2+nch}"])
                pr.op("scalar", lambda E_, nch=nch, pk=pk, ee=ee:
                      E_.copy(ee[:, nch*512:(nch+1)*512], pk[:]),
                      reads=[f"ps{(ct%2)*2+nch}"], writes=[f"ev{ct%2}"])
            if ct < 32:
                pr.dma("sync", gate[ct], ee[:], reads=[f"ev{ct%2}"], writes=[f"o{ct}"])
            elif ct < 80:
                pr.dma("sync", hbc[ct - 32], ee[:], reads=[f"ev{ct%2}"], writes=[f"o{ct}"])
                if halo_out is not None:
                    j = ct - 32
                    pr.dma("sync", halo_out[:, j*3:(j+1)*3], ee[:, 1021:1024],
                           reads=[f"ev{ct%2}"], writes=[f"ho{ct}"])
                if halo_sb is not None:
                    j = ct - 32
                    for ft in range(NT):
                        pr.op("tensor", lambda E_, ft=ft, w=w, ct=ct: E_.matmul(
                            psH[:, 0:3], w[:, ft*128:(ft+1)*128],
                            x2hbf[:, ft*3:(ft+1)*3],
                            start=(ft == 0), stop=(ft == NT - 1)),
                              reads=[f"wst{ct%2}", "x2hbf"], writes=["psH"])
                    pr.op("scalar", lambda E_, j=j: E_.copy(
                        halo_sb[:, j*3:(j+1)*3], psH[:, 0:3]),
                          reads=["psH"], writes=["halo_sel"])
            else:
                pr.dma("sync", dt_o, ee[0:64, :], reads=[f"ev{ct%2}"], writes=[f"o{ct}"])
    return


def emit_conv(nc, pr, io, halo_fn):
    hbc = io["hbc"]; dt_i = io["dt_i"]
    convw = io["convw"]; convb = io["convb"]; dtb = io["dtb"]; alog = io["alog"]
    xbc = io["xbc"]; dtp_o = io["dtp_o"]; scum_o = io["scum_o"]
    from contextlib import ExitStack
    with ExitStack() as ctx:
        E = ctx.enter_context
        halo_sb, HKEYS = halo_fn(E, pr)
        hb = [E(nc.sbuf_tensor(f"hb{i}", [128, 1027], F32)) for i in range(2)]
        cw = E(nc.sbuf_tensor("cw", [128, 48 * 4], F32))
        cb = E(nc.sbuf_tensor("cb", [128, 48], F32))
        acc = [E(nc.sbuf_tensor(f"acc{i}", [128, 1024], F32)) for i in range(2)]
        xb = [E(nc.sbuf_tensor(f"xb{i}", [128, 1024], BF16)) for i in range(2)]
        dts = E(nc.sbuf_tensor("dts", [64, 1024], F32))
        d1 = E(nc.sbuf_tensor("d1", [64, 1024], F32))
        zz = E(nc.sbuf_tensor("zz", [64, 1024], F32))
        l1p = E(nc.sbuf_tensor("l1p", [64, 1024], F32))
        ezz = E(nc.sbuf_tensor("ezz", [64, 1024], F32))
        rl = E(nc.sbuf_tensor("rl", [64, 1024], F32))
        dtp = E(nc.sbuf_tensor("dtp", [64, 1024], F32))
        aa = E(nc.sbuf_tensor("aa", [64, 1024], F32))
        scum = E(nc.sbuf_tensor("scum", [64, 1024], F32))
        dtbs = E(nc.sbuf_tensor("dtbs", [64, 1], F32))
        als = E(nc.sbuf_tensor("als", [64, 1], F32))
        nega = E(nc.sbuf_tensor("nega", [64, 1], F32))
        one64 = E(nc.sbuf_tensor("one64", [64, 1], F32))
        pr.op("vector", lambda E_: E_.memset(one64[:], 1.0), writes=["one64"])
        pr.dma("sync", cw[:], convw.rearrange("p ct k -> p (ct k)"), writes=["cw"])
        pr.dma("sync", cb[:], convb, writes=["cb"])
        pr.dma("sync", dts[:], dt_i, writes=["dts"])
        pr.dma("sync", dtbs[:], dtb, writes=["dtbs"])
        pr.dma("sync", als[:], alog, writes=["als"])
        for ct in range(48):
            h = hb[ct % 2]; a = acc[ct % 2]; x = xb[ct % 2]
            pr.op("vector", lambda E_, h=h, ct=ct: E_.tensor_copy(
                h[:, 0:3], halo_sb[:, ct*3:(ct+1)*3]),
                  reads=HKEYS, writes=[f"hb{ct%2}"])
            pr.dma("sync", h[:, 3:1027], hbc[ct], writes=[f"hb{ct%2}"])
            pr.op("vector", lambda E_, h=h, a=a, ct=ct: E_.tensor_scalar(
                a[:], h[:, 0:1024], cw[:, ct*4:ct*4+1], cb[:, ct:ct+1],
                op0=ALU.mult, op1=ALU.add),
                  reads=[f"hb{ct%2}", "cw", "cb"], writes=[f"acc{ct%2}"])
            for k in range(1, 4):
                pr.op("vector", lambda E_, h=h, a=a, k=k, ct=ct: E_.scalar_tensor_tensor(
                    a[:], h[:, k:1024+k], cw[:, ct*4+k:ct*4+k+1], a[:],
                    op0=ALU.mult, op1=ALU.add),
                      reads=[f"hb{ct%2}", "cw", f"acc{ct%2}"], writes=[f"acc{ct%2}"])
            pr.op("scalar", lambda E_, a=a, x=x: E_.activation(x[:], a[:], AF.Silu),
                  reads=[f"acc{ct%2}"], writes=[f"xb{ct%2}"])
            pr.dma("gpsimd", xbc[:, :, ct, :],
                   x[:].rearrange("p (c t) -> p c t", c=8),
                   reads=[f"xb{ct%2}"], writes=[f"xo{ct}"])
        pr.op("vector", lambda E_: E_.tensor_scalar(d1[:], dts[:], dtbs[:], None,
                                                    op0=ALU.add),
              reads=["dts", "dtbs"], writes=["d1"])
        pr.op("scalar", lambda E_: E_.activation(zz[:], d1[:], AF.Abs),
              reads=["d1"], writes=["zz"])
        pr.op("scalar", lambda E_: E_.activation(ezz[:], zz[:], AF.Exp, scale=-1.0),
              reads=["zz"], writes=["ezz"])
        pr.op("scalar", lambda E_: E_.activation(l1p[:], ezz[:], AF.Ln, bias=one64[:]),
              reads=["ezz", "one64"], writes=["l1p"])
        pr.op("vector", lambda E_: E_.tensor_scalar(rl[:], d1[:], 0.0, None,
                                                    op0=ALU.max),
              reads=["d1"], writes=["rl"])
        pr.op("vector", lambda E_: E_.tensor_tensor(dtp[:], rl[:], l1p[:], ALU.add),
              reads=["l1p", "rl"], writes=["dtp"])
        pr.dma("gpsimd", dtp_o, dtp[:], reads=["dtp"], writes=["dtpo"])
        pr.op("scalar", lambda E_: E_.activation(nega[:], als[:], AF.Exp),
              reads=["als"], writes=["nega"])
        pr.op("vector", lambda E_: E_.tensor_scalar(nega[:], nega[:], -1.0, None,
                                                    op0=ALU.mult),
              reads=["nega"], writes=["nega"])
        pr.op("vector", lambda E_: E_.tensor_scalar(aa[:], dtp[:], nega[:], None,
                                                    op0=ALU.mult),
              reads=["dtp", "nega"], writes=["aa"])
        pr.op("vector", lambda E_: E_.tensor_tensor_scan(scum[:], aa[:], aa[:], 0.0,
                                                         op0=ALU.add, op1=ALU.bypass),
              reads=["aa"], writes=["scum"])
        pr.dma("gpsimd", scum_o, scum[:], reads=["scum"], writes=["scumo"])
    return




def emit_scan(nc, pr, io, with_init, debug=False, tag="", y_work=True):
    def dint(name, shape, dt=F32):
        return nc.dram_tensor(name, list(shape), dt, kind="Internal").ap()
    xbc = io["xbc"]; dtp_i = io["dtp_i"]; scum_i = io["scum_i"]
    maskc = io["maskc"]; identb = io["identb"]; identf = io["identf"]
    onesrow = io["onesrow"]; onesrowb = io["onesrowb"]; onesb = io["onesb"]
    drep = io["drep"]
    if with_init:
        st_slot = io["st_slot"]
        flags4 = io["flags4"]
        nwp = io["nwp"]; gate = io["gate"]; wtop = io["wtop"]; x2t = io["x2t"]
        x3t = io["x3t"]
        if debug:
            dbg_att = io["dbg_att"]
        r0d = dint(tag + "s_r0d", (32, 128, 1024), F32)
    else:
        STout = io["STout"]
    eD = dint(tag + "s_eD", (64, 128), F32)
    eDb = dint(tag + "s_eDb", (64, 128), BF16)

    from contextlib import ExitStack
    with ExitStack() as ctx:
        E = ctx.enter_context
        scum = E(nc.sbuf_tensor("scum", [64, 1024], F32))
        dtp = E(nc.sbuf_tensor("dtp", [64, 1024], F32))
        ldt = E(nc.sbuf_tensor("ldt", [64, 1024], F32))
        ST = E(nc.sbuf_tensor("ST", [128, 4096], F32))
        STb = E(nc.sbuf_tensor("STb", [128, 4096], BF16))
        yn = E(nc.sbuf_tensor("yn", [128, 32 * 1024], BF16))
        ynf = yn[:].bitcast(F32)
        R4 = ynf[:, 0:4096]
        S4s = ynf[:, 4096:8192]
        xbcC0 = E(nc.sbuf_tensor("xbcC0", [128, 6144], BF16))
        xbcC = [xbcC0, xbcC0]
        arD = E(nc.sbuf_tensor("arD", [128, 8192], F32))
        yfin = arD[:, 0:4096]
        Dh = arD[:, 4096:8192]
        arB = E(nc.sbuf_tensor("arB", [128, 4096], BF16))
        arB2 = E(nc.sbuf_tensor("arB2", [128, 4096], BF16))
        xT = E(nc.sbuf_tensor("xT", [128, 4096], BF16))
        xwT = arB[:]
        scT = E(nc.sbuf_tensor("scT", [128, 64], F32))
        ldtT = E(nc.sbuf_tensor("ldtT", [128, 64], F32))
        scldT = E(nc.sbuf_tensor("scldT", [128, 64], F32))
        wmat = E(nc.sbuf_tensor("wmat", [64, 128], F32))
        wdt = E(nc.sbuf_tensor("wdt", [64, 128], F32))
        wdtTb = E(nc.sbuf_tensor("wdtTb", [128, 64], BF16))
        emat = E(nc.sbuf_tensor("emat", [64, 128], F32))
        srowH = E(nc.sbuf_tensor("srowH", [1, 4096], F32))
        Ceh = E(nc.sbuf_tensor("Ceh", [128, 4096], BF16))
        drsb = E(nc.sbuf_tensor("drsb", [128, 32], BF16))
        dtrow = E(nc.sbuf_tensor("dtrow", [1, 64], F32))
        dtbc = E(nc.sbuf_tensor("dtbc", [128, 64], F32))
        maskS = E(nc.sbuf_tensor("maskS", [128, 128], F32))
        idb = E(nc.sbuf_tensor("idb", [128, 128], BF16))
        idf = E(nc.sbuf_tensor("idf", [128, 128], F32))
        onr = E(nc.sbuf_tensor("onr", [1, 128], F32))
        onrb = E(nc.sbuf_tensor("onrb", [1, 128], BF16))
        onb = E(nc.sbuf_tensor("onb", [128, 1], BF16))
        drs = E(nc.sbuf_tensor("drs", [128, 32], F32))
        nsp = E(nc.sbuf_tensor("nsp", [64, 1], F32))
        sclE = E(nc.sbuf_tensor("sclE", [64, 1], F32))
        d4r = E(nc.sbuf_tensor("d4r", [1, 64], F32))
        d4bc = E(nc.sbuf_tensor("d4bc", [128, 64], F32))
        fl = E(nc.sbuf_tensor("fl", [128, 3], F32))
        ceb = [E(nc.sbuf_tensor(f"ceb{i}", [128, 128], BF16)) for i in range(2)]
        sqb = E(nc.sbuf_tensor("sqb", [128, 128], BF16))
        vrow = srowH[0:1, 1024:2048]
        rsr = srowH[0:1, 2048:3072]
        rsrb = srowH[0:1, :].bitcast(BF16)[0:1, 6144:7168]
        eps1 = E(nc.sbuf_tensor("eps1", [1, 1], F32))
        if with_init:
            nws = E(nc.sbuf_tensor("nws", [128, 32], F32))
            wostr = [arB[:], arB2[:]]
            xstr = arD[:, 0:1024]
            r0t = arD[:, 1024:2048]
            sgt = arD[:, 2048:3072]
            sgb = arD[:].bitcast(BF16)[:, 6144:7168]
            gtt = arD[:, 4096:5120]
            x1t = arD[:, 5120:6144]
            r0bf = STb[:, 0:1024]
            sq2 = STb[:, 1024:2048]
            mrow = srowH[0:1, 3072:4096]
            rsrow = rsr

        pk = [E(nc.psum_tensor(f"pk{i}", [128, 512], F32)) for i in range(6)]
        pkb = [pk[2][:].bitcast(BF16)[:, 0:128], pk[3][:].bitcast(BF16)[:, 0:128]]
        psV = E(nc.psum_tensor("psV", [1, 1024], F32))

        pr.op("vector", lambda E_: E_.memset(eps1[:], EPS), writes=["eps1"])
        pr.dma("sync", scum[:], scum_i, writes=["scum"])
        pr.dma("sync", dtp[:], dtp_i, writes=["dtp"])
        pr.dma("sync", maskS[:], maskc, writes=["maskS"])
        pr.dma("sync", idb[:], identb, writes=["idb"])
        pr.dma("sync", idf[:], identf, writes=["idf"])
        pr.dma("sync", onr[:], onesrow, writes=["onr"])
        pr.dma("sync", onrb[:], onesrowb, writes=["onrb"])
        pr.dma("sync", onb[:], onesb, writes=["onb"])
        pr.dma("sync", drs[:], drep, writes=["drs"])
        pr.op("scalar", lambda E_: E_.activation(ldt[:], dtp[:], AF.Ln),
              reads=["dtp"], writes=["ldt"])
        pr.op("vector", lambda E_: E_.tensor_copy(drsb[:], drs[:]),
              reads=["drs"], writes=["drsb"])
        pr.op("gpsimd", lambda E_: E_.memset(ST[:], 0.0), writes=["ST"])
        if with_init:
            pr.dma("sync", nws[:], nwp, writes=["nws"])
            pr.op("gpsimd", lambda E_: E_.memset(R4[:], 0.0), writes=["R4"])
            pr.dma("sync", fl[:], flags4, writes=["fl"])
            for kq in range(3):
                st_ap, scl_ap = st_slot(kq)
                pr.dma("sync", S4s[:], st_ap, writes=["S4s"])
                pr.dma("sync", sclE[:], scl_ap, writes=["sclE"])
                if kq > 0:
                    pr.op("scalar", lambda E_: E_.activation(sclE[:], sclE[:], AF.Exp),
                          reads=["sclE"], writes=["sclE"])
                    pr.op("tensor", lambda E_: E_.transpose(
                        pk[2][0:1, 0:64], sclE[:], idf[0:64, 0:64]),
                          reads=["sclE", "idf"], writes=["pk2"])
                    pr.op("vector", lambda E_: E_.tensor_copy(d4r[0:1, :], pk[2][0:1, 0:64]),
                          reads=["pk2"], writes=["d4r"])
                    pr.op("tensor", lambda E_: E_.matmul(pk[3][:, 0:64], onr[0:1, :],
                                                         d4r[0:1, :], start=True, stop=True),
                          reads=["d4r", "onr"], writes=["pk3"])
                    pr.op("vector", lambda E_: E_.tensor_copy(d4bc[:], pk[3][:, 0:64]),
                          reads=["pk3"], writes=["d4bc"])
                    pr.op("vector", lambda E_: E_.tensor_tensor(
                        R4.rearrange("p (h q) -> p h q", q=64),
                        R4.rearrange("p (h q) -> p h q", q=64),
                        d4bc[:].unsqueeze(-1).to_broadcast([128, 64, 64]),
                        ALU.mult),
                          reads=["R4", "d4bc"], writes=["R4"])
                pr.op("vector", lambda E_: E_.tensor_tensor(R4, R4, S4s, ALU.add),
                      reads=["R4", "S4s"], writes=["R4"])
                pr.op("vector", lambda E_, kq=kq: E_.scalar_tensor_tensor(
                    ST[:], R4, fl[:, kq:kq+1], ST[:], op0=ALU.mult, op1=ALU.add),
                      reads=["R4", "fl", "ST"], writes=["ST"])
            pr.barrier()

        for c in range(NCH):
            xc = xbcC[0]
            XK = "xbcC0"
            pr.dma("sync", xc[:], xbc[:, c].rearrange("p ct t -> p (ct t)"),
                   writes=[XK])
            c0 = c * 128
            pr.op("vector", lambda E_: E_.tensor_copy(STb[:, 0:4096], ST[:]),
                  reads=["ST"], writes=["STb"])
            pr.op("tensor", lambda E_, c0=c0: E_.transpose(
                pk[2][:, 0:64], scum[:, c0:c0+128], idf[0:64, 0:64]),
                  reads=["scum", "idf"], writes=["pk2"])
            pr.op("scalar", lambda E_: E_.copy(scT[:], pk[2][:, 0:64]),
                  reads=["pk2"], writes=["scT"])
            pr.op("tensor", lambda E_, c0=c0: E_.transpose(
                pk[3][:, 0:64], ldt[:, c0:c0+128], idf[0:64, 0:64]),
                  reads=["ldt", "idf"], writes=["pk3"])
            pr.op("scalar", lambda E_: E_.copy(ldtT[:], pk[3][:, 0:64]),
                  reads=["pk3"], writes=["ldtT"])
            pr.op("vector", lambda E_: E_.tensor_tensor(scldT[:], scT[:], ldtT[:],
                                                        ALU.subtract),
                  reads=["scT", "ldtT"], writes=["scldT"])
            if c == 0:
                pr.op("scalar", lambda E_, c0=c0: E_.activation(
                    emat[:], scum[:, c0:c0+128], AF.Exp),
                      reads=["scum"], writes=["emat"])
            else:
                pr.op("vector", lambda E_, c0=c0: E_.tensor_scalar(
                    nsp[:], scum[:, c0-1:c0], -1.0, None, op0=ALU.mult),
                      reads=["scum"], writes=["nsp"])
                pr.op("scalar", lambda E_, c0=c0: E_.activation(
                    emat[:], scum[:, c0:c0+128], AF.Exp, bias=nsp[:]),
                      reads=["scum", "nsp"], writes=["emat"])
            pr.op("scalar", lambda E_, c0=c0: E_.activation(
                wmat[:], scum[:, c0:c0+128], AF.Exp, scale=-1.0,
                bias=scum[:, c0+127:c0+128]),
                  reads=["scum"], writes=["wmat"])
            pr.op("vector", lambda E_, c0=c0: E_.tensor_tensor(
                wdt[:], wmat[:], dtp[:, c0:c0+128], ALU.mult),
                  reads=["wmat", "dtp"], writes=["wdt"])
            pr.op("tensor", lambda E_: E_.transpose(pk[2][:, 0:64], wdt[:], idf[0:64, 0:64]),
                  reads=["wdt", "idf"], writes=["pk2"])
            pr.op("scalar", lambda E_: E_.copy(wdtTb[:], pk[2][:, 0:64]),
                  reads=["pk2"], writes=["wdtTb"])
            pr.dma("gpsimd", eDb, emat[:], reads=["emat"], writes=["eDb"])
            pr.op("tensor", lambda E_: E_.transpose(pk[3][0:1, 0:64],
                                                    emat[:, 127:128], idf[0:64, 0:64]),
                  reads=["emat", "idf"], writes=["pk3"])
            pr.op("scalar", lambda E_: E_.copy(dtrow[0:1, :], pk[3][0:1, 0:64]),
                  reads=["pk3"], writes=["dtrow"])
            pr.op("tensor", lambda E_: E_.matmul(pk[2][:, 0:64], onr[0:1, :],
                                                 dtrow[0:1, :], start=True, stop=True),
                  reads=["dtrow", "onr"], writes=["pk2"])
            pr.op("scalar", lambda E_: E_.copy(dtbc[:], pk[2][:, 0:64]),
                  reads=["pk2"], writes=["dtbc"])
            for ft in range(NT):
                pb = pkb[ft % 2]
                pr.op("tensor", lambda E_, ft=ft, pb=pb, xc=xc: E_.transpose(
                    pb[:], xc[:, ft*128:(ft+1)*128], idb[:]),
                      reads=[XK, "idb"], writes=[f"pk{2+ft%2}"])
                pr.op("scalar", lambda E_, ft=ft, pb=pb: E_.copy(
                    xT[:, ft*128:(ft+1)*128], pb[:]),
                      reads=[f"pk{2+ft%2}"], writes=["xT"])
            if y_work:
                pr.op("vector", lambda E_, xc=xc: E_.tensor_tensor(
                    yfin.rearrange("p (ft t) -> p ft t", ft=32),
                    xc[:, 0:4096].rearrange("p (ft t) -> p ft t", ft=32),
                    drsb[:].unsqueeze(-1).to_broadcast([128, 32, 128]), ALU.mult),
                      reads=[XK, "drsb"], writes=["yfin"])
            for half in range(2 if y_work else 0):
                ho = half * 32
                pr.dma("sync", arB2[:].rearrange("p (h t) -> p h t", h=32),
                       eDb[ho:ho+32, :].partition_broadcast(128),
                       reads=["eDb"], writes=["arB2"])
                pr.op("vector", lambda E_, xc=xc, half=half: E_.tensor_tensor(
                    Ceh[:].rearrange("p (g j t) -> p g j t", g=4, j=8),
                    xc[:, (40+half*4)*128:(44+half*4)*128].rearrange(
                        "p (g t) -> p g t", g=4).unsqueeze(2).to_broadcast(
                        [128, 4, 8, 128]),
                    arB2[:].rearrange("p (g j t) -> p g j t", g=4, j=8), ALU.mult),
                      reads=[XK, "arB2"], writes=["Ceh"])
                pr.dma("sync", Dh.rearrange("p (h t) -> p h t", h=32),
                       scum_i[ho:ho+32, c0:c0+128].partition_broadcast(128),
                       writes=["Dh"])
                pr.op("vector", lambda E_, ho=ho: E_.tensor_tensor(
                    Dh.rearrange("p (h t) -> p h t", h=32),
                    Dh.rearrange("p (h t) -> p h t", h=32),
                    scldT[:, ho:ho+32].unsqueeze(-1).to_broadcast([128, 32, 128]),
                    ALU.subtract),
                      reads=["Dh", "scldT"], writes=["Dh"])
                pr.op("vector", lambda E_: E_.tensor_tensor(
                    Dh.rearrange("p (h t) -> p h t", h=32),
                    Dh.rearrange("p (h t) -> p h t", h=32),
                    maskS[:].unsqueeze(1).to_broadcast([128, 32, 128]), ALU.add),
                      reads=["Dh", "maskS"], writes=["Dh"])
                pr.op("scalar", lambda E_: E_.activation(arB[:], Dh[:], AF.Exp),
                      reads=["Dh"], writes=["arB"])
                pA = pk[half]
                for gg in range(4):
                    g = half * 4 + gg
                    pr.op("tensor", lambda E_, g=g, gg=gg, pA=pA, xc=xc: E_.matmul(
                        pA[:, gg*128:(gg+1)*128],
                        xc[:, (32+g)*128:(33+g)*128], xc[:, (40+g)*128:(41+g)*128],
                        start=True, stop=True),
                          reads=[XK], writes=[f"pk{half}"])
                pr.op("vector", lambda E_, pA=pA: E_.tensor_tensor(
                    arB2[:].rearrange("p (gg j t) -> p gg j t", gg=4, j=8),
                    arB[:].rearrange("p (gg j t) -> p gg j t", gg=4, j=8),
                    pA[:].rearrange("p (gg t) -> p gg t", gg=4).unsqueeze(2).to_broadcast([128, 4, 8, 128]),
                    ALU.mult),
                      reads=["arB", f"pk{half}"], writes=["arB2"])
                for hh in range(32):
                    h = ho + hh
                    hb = hh % 2
                    pY = pk[4 + hb]
                    pr.op("tensor", lambda E_, h=h, hh=hh, pY=pY: E_.matmul(
                        pY[0:64, 0:128], xT[:, h*64:(h+1)*64],
                        arB2[:, hh*128:(hh+1)*128], start=True, stop=False),
                          reads=["xT", "arB2"], writes=[f"pk{4+hb}"])
                    pr.op("tensor", lambda E_, h=h, hh=hh, pY=pY: E_.matmul(
                        pY[0:64, 0:128], STb[:, h*64:(h+1)*64],
                        Ceh[:, hh*128:(hh+1)*128], start=False, stop=True),
                          reads=["STb", "Ceh"], writes=[f"pk{4+hb}"])
                    ft = h // 2
                    rr0 = (h % 2) * 64
                    pr.op("vector", lambda E_, ft=ft, rr0=rr0, pY=pY:
                          E_.tensor_tensor(
                              yfin[rr0:rr0+64, ft*128:(ft+1)*128],
                              yfin[rr0:rr0+64, ft*128:(ft+1)*128],
                              pY[0:64, 0:128], ALU.add),
                          reads=[f"pk{4+hb}", "yfin"], writes=["yfin"])
            for ft in range(NT if y_work else 0):
                pr.op("scalar", lambda E_, ft=ft: E_.activation(
                    sqb[:], yfin[:, ft*128:(ft+1)*128], AF.Square),
                      reads=["yfin"], writes=["sqb"])
                pr.op("tensor", lambda E_, ft=ft: E_.matmul(
                    psV[0:1, (ft//4)*128:(ft//4)*128+128], onb[:], sqb[:],
                    start=(ft % 4 == 0), stop=(ft % 4 == 3)),
                      reads=["sqb", "onb"], writes=["psV"])
            if y_work:
                pr.op("scalar", lambda E_: E_.activation(vrow[:], psV[:], AF.Ln,
                                                         scale=1.0/512.0, bias=eps1[:]),
                      reads=["psV", "eps1"], writes=["vrow", "srowH"])
                pr.op("scalar", lambda E_: E_.activation(rsr[:], vrow[:], AF.Exp,
                                                         scale=-0.5),
                      reads=["vrow"], writes=["rsr", "srowH"])
                pr.op("vector", lambda E_: E_.tensor_copy(rsrb[:], rsr[:]),
                      reads=["rsr"], writes=["rsrb", "srowH"])
            if with_init:
                for g in range(8):
                    pe = pk[2 + (g % 2)]
                    pr.op("tensor", lambda E_, g=g, pe=pe: E_.matmul(
                        pe[:, 0:128], onrb[0:1, :], rsrb[0:1, g*128:(g+1)*128],
                        start=True, stop=True),
                          reads=["rsrb", "onrb"], writes=[f"pk{2+g%2}"])
                    for ft in range(g*4, g*4+4):
                        pr.op("vector", lambda E_, ft=ft, g=g, pe=pe, c=c:
                              E_.scalar_tensor_tensor(
                                  yn[:, ft*1024 + c*128: ft*1024 + c*128 + 128],
                                  yfin[:, ft*128:(ft+1)*128], nws[:, ft:ft+1],
                                  pe[:, 0:128], op0=ALU.mult, op1=ALU.mult),
                              reads=["yfin", "nws", f"pk{2+g%2}"], writes=[f"yn{ft}"])
            pr.op("vector", lambda E_: E_.tensor_tensor(
                xwT.rearrange("p (h q) -> p h q", q=64),
                xT[:].rearrange("p (h q) -> p h q", q=64),
                wdtTb[:].unsqueeze(-1).to_broadcast([128, 64, 64]),
                ALU.mult),
                  reads=["xT", "wdtTb"], writes=["xwT", "arB"])
            for g in range(8):
                pb = pkb[g % 2]
                pr.op("tensor", lambda E_, g=g, pb=pb, xc=xc: E_.transpose(
                    pb[:], xc[:, (32+g)*128:(33+g)*128], idb[:]),
                      reads=[XK, "idb"], writes=[f"pk{2+g%2}"])
                bt = ceb[g % 2]
                pr.op("scalar", lambda E_, pb=pb, bt=bt: E_.copy(bt[:], pb[:]),
                      reads=[f"pk{2+g%2}"], writes=[f"ceb{g%2}"])
                pT_ = pk[4 + (g % 2)]
                for hh in range(8):
                    h = g * 8 + hh
                    pr.op("tensor", lambda E_, h=h, hh=hh, pT_=pT_, bt=bt: E_.matmul(
                        pT_[:, hh*64:(hh+1)*64], bt[:], xwT[:, h*64:(h+1)*64],
                        start=(hh == 0), stop=(hh == 7)),
                          reads=[f"ceb{g%2}", "xwT", "arB"], writes=[f"pk{4+g%2}"])
                pr.op("vector", lambda E_, g=g: E_.tensor_tensor(
                    ST[:, g*512:(g+1)*512].rearrange("p (hh q) -> p hh q", q=64),
                    ST[:, g*512:(g+1)*512].rearrange("p (hh q) -> p hh q", q=64),
                    dtbc[:, g*8:(g+1)*8].unsqueeze(-1).to_broadcast([128, 8, 64]),
                    ALU.mult),
                      reads=["ST", "dtbc"], writes=["ST"])
                pr.op("vector", lambda E_, g=g, pT_=pT_: E_.tensor_tensor(
                    ST[:, g*512:(g+1)*512], ST[:, g*512:(g+1)*512], pT_[:, 0:512],
                    ALU.add),
                      reads=["ST", f"pk{4+g%2}"], writes=["ST"])

        if not with_init:
            pr.dma("gpsimd", STout, ST[:], reads=["ST"], writes=["STo"])
        else:
            pr.barrier()
            YNK = [f"yn{ft}" for ft in range(NT)]
            for it in range(NT):
                pr.dma("sync", gtt[:], gate[it], writes=["gtt"])
                pr.op("scalar", lambda E_: E_.activation(sgb[:], gtt[:], AF.Silu),
                      reads=["gtt"], writes=["sgb"])
                pr.op("vector", lambda E_, it=it: E_.tensor_tensor(
                    yn[:, it*1024:(it+1)*1024], yn[:, it*1024:(it+1)*1024],
                    sgb[:], ALU.mult),
                      reads=["sgb", f"yn{it}"], writes=[f"yn{it}"])
            for dt_ in range(NT):
                w0 = wostr[dt_ % 2]
                WK = f"wostr{dt_%2}"
                pr.dma("gpsimd", w0, wtop[dt_].rearrange("p it j -> p (it j)"),
                       writes=[WK])
                pr.dma("sync", xstr, x2t[dt_], writes=["xstr"])
                for nch in range(2):
                    pq = pk[(dt_ % 2) * 2 + nch]
                    PK = f"pk{(dt_%2)*2+nch}"
                    for it in range(NT):
                        pr.op("tensor", lambda E_, it=it, nch=nch, pq=pq, w0=w0:
                              E_.matmul(pq[:], w0[:, it*128:(it+1)*128],
                                        yn[:, it*1024 + nch*512: it*1024 + nch*512 + 512],
                                        start=(it == 0), stop=(it == NT - 1)),
                              reads=[WK] + YNK, writes=[PK])
                    pr.op("vector", lambda E_, nch=nch, pq=pq: E_.tensor_tensor(
                        r0t[:, nch*512:(nch+1)*512], pq[:],
                        xstr[:, nch*512:(nch+1)*512], ALU.add),
                          reads=[PK, "xstr"], writes=["r0t"])
                pr.dma("gpsimd", r0d[dt_], r0t, reads=["r0t"], writes=[f"r0d{dt_}"])
                if debug:
                    pr.dma("gpsimd", dbg_att[dt_], r0t, reads=["r0t"],
                           writes=[f"dba{dt_}"])
                pr.op("vector", lambda E_: E_.tensor_copy(r0bf, r0t),
                      reads=["r0t"], writes=["r0bf"])
                pr.op("scalar", lambda E_: E_.activation(sq2, r0t, AF.Square),
                      reads=["r0t"], writes=["sq2"])
                for nch in range(2):
                    pr.op("tensor", lambda E_, nch=nch, dt_=dt_: E_.matmul(
                        psV[0:1, nch*512:(nch+1)*512], onb[:],
                        r0bf[:, nch*512:(nch+1)*512],
                        start=(dt_ == 0), stop=(dt_ == NT - 1)),
                          reads=["r0bf", "onb"], writes=["psV"])
                    pq2 = pk[4 + nch]
                    pr.op("tensor", lambda E_, nch=nch, dt_=dt_, pq2=pq2: E_.matmul(
                        pq2[0:1, :], onb[:], sq2[:, nch*512:(nch+1)*512],
                        start=(dt_ == 0), stop=(dt_ == NT - 1)),
                          reads=["sq2", "onb"], writes=[f"pk{4+nch}"])
            pr.op("vector", lambda E_: E_.tensor_scalar(
                mrow[:], psV[:], 1.0 / 4096.0, None, op0=ALU.mult),
                  reads=["psV"], writes=["mrow", "srowH"])
            pr.op("scalar", lambda E_: E_.activation(vrow[:], mrow[:], AF.Square),
                  reads=["mrow"], writes=["vrow", "srowH"])
            for nch in range(2):
                pq2 = pk[4 + nch]
                pr.op("vector", lambda E_, nch=nch, pq2=pq2: E_.scalar_tensor_tensor(
                    vrow[:, nch*512:(nch+1)*512], pq2[0:1, :], 1.0 / 4096.0,
                    vrow[:, nch*512:(nch+1)*512], op0=ALU.mult, op1=ALU.subtract),
                      reads=[f"pk{4+nch}", "vrow"], writes=["vrow", "srowH"])
            pr.op("scalar", lambda E_: E_.activation(vrow[:], vrow[:], AF.Ln,
                                                     bias=eps1[:]),
                  reads=["vrow", "eps1"], writes=["vrow", "srowH"])
            pr.op("scalar", lambda E_: E_.activation(rsrow[:], vrow[:], AF.Exp,
                                                     scale=-0.5),
                  reads=["vrow"], writes=["rsrow", "srowH"])
            for nch in range(2):
                pr.op("tensor", lambda E_, nch=nch: E_.matmul(
                    pk[nch][:], onr[0:1, :], mrow[0:1, nch*512:(nch+1)*512],
                    start=True, stop=True),
                      reads=["mrow", "onr"], writes=[f"pk{nch}"])
                pr.op("tensor", lambda E_, nch=nch: E_.matmul(
                    pk[2+nch][:], onr[0:1, :], rsrow[0:1, nch*512:(nch+1)*512],
                    start=True, stop=True),
                      reads=["rsrow", "onr"], writes=[f"pk{2+nch}"])
            for dt_ in range(NT):
                pr.dma("sync", r0t, r0d[dt_], reads=[f"r0d{dt_}"], writes=["r0t"])
                for nch in range(2):
                    pr.op("vector", lambda E_, nch=nch: E_.tensor_tensor(
                        x1t[:, nch*512:(nch+1)*512], r0t[:, nch*512:(nch+1)*512],
                        pk[nch][:], ALU.subtract),
                          reads=["r0t", f"pk{nch}"], writes=["x1t"])
                    pr.op("vector", lambda E_, nch=nch: E_.tensor_tensor(
                        x1t[:, nch*512:(nch+1)*512], x1t[:, nch*512:(nch+1)*512],
                        pk[2+nch][:], ALU.mult),
                          reads=[f"pk{2+nch}"], writes=["x1t"])
                pr.dma("gpsimd", x3t[dt_], x1t, reads=["x1t"], writes=[f"x3o{dt_}"])
    return



def emit_ffn(nc, pr, io):
    def dint(name, shape, dt=F32):
        return nc.dram_tensor(name, list(shape), dt, kind="Internal").ap()
    x3t = io["x3t"]; wtbp = io["wtbp"]; bbp = io["bbp"]; a1p = io["a1p"]
    b1p = io["b1p"]; r1p = io["r1p"]; selp = io["selp"]
    identb = io["identb"]; onesrow = io["onesrow"]; onesb = io["onesb"]
    x4t = io["x4t"]
    r0d = dint("f_r0d", (32, 128, 1024), F32)
    from contextlib import ExitStack
    with ExitStack() as ctx:
        E = ctx.enter_context
        x3bf = E(nc.sbuf_tensor("x3bf", [128, 32 * 1024], BF16))
        a1sb = E(nc.sbuf_tensor("a1sb", [128, 32 * 64], BF16))
        r1sb = E(nc.sbuf_tensor("r1sb", [128, 32 * 16], BF16))
        selsb = E(nc.sbuf_tensor("selsb", [16, 8 * 128], BF16))
        wstr = [E(nc.sbuf_tensor(f"wstr{i}", [128, 32 * 128], BF16)) for i in range(2)]
        bstr = [E(nc.sbuf_tensor(f"bstr{i}", [128, 8 * 128], BF16)) for i in range(2)]
        bbs = E(nc.sbuf_tensor("bbs", [128, 32], F32))
        idb = E(nc.sbuf_tensor("idb", [128, 128], BF16))
        onr = E(nc.sbuf_tensor("onr", [1, 128], F32))
        onb = E(nc.sbuf_tensor("onb", [128, 1], BF16))
        lg16 = E(nc.sbuf_tensor("lg16", [128, 8 * 16], F32))
        prob = E(nc.sbuf_tensor("prob", [128, 16], F32))
        gw = E(nc.sbuf_tensor("gw", [128, 16], F32))
        gm = E(nc.sbuf_tensor("gm", [128, 1], F32))
        gm2 = E(nc.sbuf_tensor("gm2", [128, 1], F32))
        gz = E(nc.sbuf_tensor("gz", [128, 1], F32))
        ngm = E(nc.sbuf_tensor("ngm", [128, 1], F32))
        rz = E(nc.sbuf_tensor("rz", [128, 1], F32))
        cmb = E(nc.sbuf_tensor("cmb", [128, 16], F32))
        cmbb = E(nc.sbuf_tensor("cmbb", [128, 16], BF16))
        cmbT = E(nc.sbuf_tensor("cmbT", [16, 1024], BF16))
        midT = E(nc.sbuf_tensor("midT", [64, 1024], F32))
        midg = E(nc.sbuf_tensor("midg", [128, 8 * 1024], BF16))
        xstr = E(nc.sbuf_tensor("xstr", [128, 1024], F32))
        r0t = E(nc.sbuf_tensor("r0t", [128, 1024], F32))
        r0bf = E(nc.sbuf_tensor("r0bf", [128, 1024], BF16))
        sq2 = E(nc.sbuf_tensor("sq2", [128, 1024], BF16))
        x1t = E(nc.sbuf_tensor("x1t", [128, 1024], F32))
        mrow = E(nc.sbuf_tensor("mrow", [1, 1024], F32))
        vrow = E(nc.sbuf_tensor("vrow", [1, 1024], F32))
        rsrow = E(nc.sbuf_tensor("rsrow", [1, 1024], F32))
        eps1 = E(nc.sbuf_tensor("eps1", [1, 1], F32))
        pk = [E(nc.psum_tensor(f"pk{i}", [128, 512], F32)) for i in range(6)]
        pkb = [pk[2][:].bitcast(BF16)[:, 0:128], pk[3][:].bitcast(BF16)[:, 0:128]]
        psV = E(nc.psum_tensor("psV", [1, 1024], F32))

        pr.op("vector", lambda E_: E_.memset(eps1[:], EPS), writes=["eps1"])
        pr.dma("gpsimd", x3bf[:].rearrange("p (ft t) -> p ft t", ft=32),
               x3t.transpose([1, 0, 2]), writes=["x3bfA"])
        XK = ["x3bfA"]
        pr.dma("gpsimd", a1sb[:], a1p.rearrange("p ft j -> p (ft j)"), writes=["a1sb"])
        pr.dma("gpsimd", r1sb[:], r1p.rearrange("p ft j -> p (ft j)"), writes=["r1sb"])
        pr.dma("sync", selsb[:], selp.rearrange("e pr j -> e (pr j)"), writes=["selsb"])
        pr.dma("sync", idb[:], identb, writes=["idb"])
        pr.dma("sync", onr[:], onesrow, writes=["onr"])
        pr.dma("sync", onb[:], onesb, writes=["onb"])
        pr.dma("sync", bbs[:], bbp, writes=["bbs"])
        for tt in range(8):
            for ft in range(NT):
                pr.op("tensor", lambda E_, tt=tt, ft=ft: E_.matmul(
                    pk[0][:, 0:16], x3bf[:, ft*1024 + tt*128: ft*1024 + tt*128 + 128],
                    r1sb[:, ft*16:(ft+1)*16], start=(ft == 0), stop=(ft == NT - 1)),
                      reads=["x3bfA", "r1sb"], writes=["pk0"])
            pr.op("scalar", lambda E_, tt=tt: E_.copy(lg16[:, tt*16:(tt+1)*16],
                                                      pk[0][:, 0:16]),
                  reads=["pk0"], writes=["lg16"])
        for tt in range(8):
            lg = lg16[:, tt*16:(tt+1)*16]
            pr.op("vector", lambda E_, lg=lg: E_.tensor_reduce(gm[:], lg, AX.X, ALU.max),
                  reads=["lg16"], writes=["gm"])
            pr.op("vector", lambda E_: E_.tensor_scalar(ngm[:], gm[:], -1.0, None,
                                                        op0=ALU.mult),
                  reads=["gm"], writes=["ngm"])
            pr.op("scalar", lambda E_, lg=lg: E_.activation(prob[:], lg, AF.Exp,
                                                            bias=ngm[:],
                                                            accum_out=gz[:]),
                  reads=["lg16", "ngm"], writes=["prob", "gz"])
            pr.op("vector", lambda E_: E_.reciprocal(rz[:], gz[:]),
                  reads=["gz"], writes=["rz"])
            pr.op("vector", lambda E_: E_.tensor_scalar(prob[:], prob[:], rz[:], None,
                                                        op0=ALU.mult),
                  reads=["prob", "rz"], writes=["prob"])
            pr.op("vector", lambda E_: E_.tensor_reduce(gm[:], prob[:], AX.X, ALU.max),
                  reads=["prob"], writes=["gm"])
            pr.op("vector", lambda E_: E_.tensor_scalar(gw[:], prob[:], gm[:], -1e30,
                                                        op0=ALU.is_ge, op1=ALU.mult),
                  reads=["prob", "gm"], writes=["gw"])
            pr.op("vector", lambda E_: E_.tensor_tensor(gw[:], gw[:], prob[:], ALU.add),
                  reads=["prob", "gw"], writes=["gw"])
            pr.op("vector", lambda E_: E_.tensor_reduce(gm2[:], gw[:], AX.X, ALU.max),
                  reads=["gw"], writes=["gm2"])
            pr.op("vector", lambda E_: E_.tensor_tensor(gz[:], gm[:], gm2[:], ALU.add),
                  reads=["gm", "gm2"], writes=["gz"])
            pr.op("vector", lambda E_: E_.tensor_scalar(gz[:], gz[:], 1e-6, None,
                                                        op0=ALU.add),
                  reads=["gz"], writes=["gz"])
            pr.op("vector", lambda E_: E_.reciprocal(rz[:], gz[:]),
                  reads=["gz"], writes=["rz"])
            pr.op("vector", lambda E_: E_.tensor_scalar(cmb[:], prob[:], gm2[:], None,
                                                        op0=ALU.is_ge),
                  reads=["prob", "gm2"], writes=["cmb"])
            pr.op("vector", lambda E_: E_.tensor_tensor(cmb[:], cmb[:], prob[:],
                                                        ALU.mult),
                  reads=["prob", "cmb"], writes=["cmb"])
            pr.op("vector", lambda E_: E_.tensor_scalar(cmb[:], cmb[:], rz[:], 0.5,
                                                        op0=ALU.mult, op1=ALU.mult),
                  reads=["cmb", "rz"], writes=["cmb"])
            pr.op("vector", lambda E_: E_.tensor_copy(cmbb[:], cmb[:]),
                  reads=["cmb"], writes=["cmbb"])
            pr.op("tensor", lambda E_: E_.transpose(
                pk[2][:].bitcast(BF16)[0:16, 0:128], cmbb[:], idb[:]),
                  reads=["cmbb", "idb"], writes=["pk2"])
            pr.op("scalar", lambda E_, tt=tt: E_.copy(cmbT[0:16, tt*128:(tt+1)*128],
                                                      pk[2][:].bitcast(BF16)[0:16, 0:128]),
                  reads=["pk2"], writes=["cmbT"])
        for nch in range(2):
            for ft in range(NT):
                pr.op("tensor", lambda E_, ft=ft, nch=nch: E_.matmul(
                    pk[1][0:64, :], a1sb[:, ft*64:(ft+1)*64],
                    x3bf[:, ft*1024 + nch*512: ft*1024 + nch*512 + 512],
                    start=(ft == 0), stop=(ft == NT - 1)),
                      reads=["x3bfA", "a1sb"], writes=["pk1"])
            pr.op("scalar", lambda E_, nch=nch: E_.copy(midT[:, nch*512:(nch+1)*512],
                                                        pk[1][0:64, :]),
                  reads=["pk1"], writes=["midT"])
        for pr_ in range(8):
            for nch in range(2):
                pr.op("tensor", lambda E_, pr_=pr_, nch=nch: E_.matmul(
                    pk[4][:], selsb[:, pr_*128:(pr_+1)*128],
                    cmbT[:, nch*512:(nch+1)*512], start=True, stop=True),
                      reads=["selsb", "cmbT"], writes=["pk4"])
                for half in range(2):
                    pr.op("vector", lambda E_, pr_=pr_, nch=nch, half=half:
                          E_.tensor_tensor(
                              midg[half*64:(half+1)*64,
                                   pr_*1024 + nch*512: pr_*1024 + nch*512 + 512],
                              midT[:, nch*512:(nch+1)*512],
                              pk[4][half*64:(half+1)*64, :], ALU.mult),
                          reads=["midT", "pk4"], writes=[f"midg{pr_}"])
        for ot in range(NT):
            w0 = wstr[ot % 2]; b0 = bstr[ot % 2]
            WK, BK = f"wstr{ot%2}", f"bstr{ot%2}"
            pr.dma("gpsimd", w0[:], wtbp[ot].rearrange("p it j -> p (it j)"), writes=[WK])
            pr.dma("gpsimd", b0[:], b1p[ot].rearrange("p pr j -> p (pr j)"), writes=[BK])
            pr.dma("sync", xstr[:], x3t[ot], writes=["xstr"])
            for nch in range(2):
                pq = pk[(ot % 2) * 2 + nch]
                PK = f"pk{(ot%2)*2+nch}"
                for it in range(NT):
                    pr.op("tensor", lambda E_, it=it, nch=nch, pq=pq, w0=w0: E_.matmul(
                        pq[:], w0[:, it*128:(it+1)*128],
                        x3bf[:, it*1024 + nch*512: it*1024 + nch*512 + 512],
                        start=(it == 0), stop=False),
                          reads=[WK, "x3bfA"], writes=[PK])
                for pr_ in range(8):
                    pr.op("tensor", lambda E_, pr_=pr_, nch=nch, pq=pq, b0=b0: E_.matmul(
                        pq[:], b0[:, pr_*128:(pr_+1)*128],
                        midg[:, pr_*1024 + nch*512: pr_*1024 + nch*512 + 512],
                        start=False, stop=(pr_ == 7)),
                          reads=[BK, f"midg{pr_}"], writes=[PK])
                pr.op("vector", lambda E_, nch=nch, pq=pq, ot=ot:
                      E_.scalar_tensor_tensor(
                          r0t[:, nch*512:(nch+1)*512], pq[:], bbs[:, ot:ot+1],
                          xstr[:, nch*512:(nch+1)*512], op0=ALU.add, op1=ALU.add),
                      reads=[PK, "bbs", "xstr"], writes=["r0t"])
            pr.dma("gpsimd", r0d[ot], r0t[:], reads=["r0t"], writes=[f"r0d{ot}"])
            pr.op("vector", lambda E_: E_.tensor_copy(r0bf[:], r0t[:]),
                  reads=["r0t"], writes=["r0bf"])
            pr.op("scalar", lambda E_: E_.activation(sq2[:], r0t[:], AF.Square),
                  reads=["r0t"], writes=["sq2"])
            for nch in range(2):
                pr.op("tensor", lambda E_, nch=nch, ot=ot: E_.matmul(
                    psV[0:1, nch*512:(nch+1)*512], onb[:],
                    r0bf[:, nch*512:(nch+1)*512], start=(ot == 0), stop=(ot == NT - 1)),
                      reads=["r0bf", "onb"], writes=["psV"])
                pq2 = pk[4 + nch]
                pr.op("tensor", lambda E_, nch=nch, ot=ot, pq2=pq2: E_.matmul(
                    pq2[0:1, :], onb[:], sq2[:, nch*512:(nch+1)*512],
                    start=(ot == 0), stop=(ot == NT - 1)),
                      reads=["sq2", "onb"], writes=[f"pk{4+nch}"])
        pr.op("vector", lambda E_: E_.tensor_scalar(
            mrow[:], psV[:], 1.0 / 4096.0, None, op0=ALU.mult),
              reads=["psV"], writes=["mrow"])
        pr.op("scalar", lambda E_: E_.activation(vrow[:], mrow[:], AF.Square),
              reads=["mrow"], writes=["vrow"])
        for nch in range(2):
            pq2 = pk[4 + nch]
            pr.op("vector", lambda E_, nch=nch, pq2=pq2: E_.scalar_tensor_tensor(
                vrow[:, nch*512:(nch+1)*512], pq2[0:1, :], 1.0 / 4096.0,
                vrow[:, nch*512:(nch+1)*512], op0=ALU.mult, op1=ALU.subtract),
                  reads=[f"pk{4+nch}", "vrow"], writes=["vrow"])
        pr.op("scalar", lambda E_: E_.activation(vrow[:], vrow[:], AF.Ln, bias=eps1[:]),
              reads=["vrow", "eps1"], writes=["vrow"])
        pr.op("scalar", lambda E_: E_.activation(rsrow[:], vrow[:], AF.Exp, scale=-0.5),
              reads=["vrow"], writes=["rsrow"])
        for nch in range(2):
            pr.op("tensor", lambda E_, nch=nch: E_.matmul(
                pk[nch][:], onr[0:1, :], mrow[0:1, nch*512:(nch+1)*512],
                start=True, stop=True),
                  reads=["mrow", "onr"], writes=[f"pk{nch}"])
            pr.op("tensor", lambda E_, nch=nch: E_.matmul(
                pk[2+nch][:], onr[0:1, :], rsrow[0:1, nch*512:(nch+1)*512],
                start=True, stop=True),
                  reads=["rsrow", "onr"], writes=[f"pk{2+nch}"])
        for dt_ in range(NT):
            pr.dma("sync", r0t[:], r0d[dt_], reads=[f"r0d{dt_}"], writes=["r0t"])
            for nch in range(2):
                pr.op("vector", lambda E_, nch=nch: E_.tensor_tensor(
                    x1t[:, nch*512:(nch+1)*512], r0t[:, nch*512:(nch+1)*512],
                    pk[nch][:], ALU.subtract),
                      reads=["r0t", f"pk{nch}"], writes=["x1t"])
                pr.op("vector", lambda E_, nch=nch: E_.tensor_tensor(
                    x1t[:, nch*512:(nch+1)*512], x1t[:, nch*512:(nch+1)*512],
                    pk[2+nch][:], ALU.mult),
                      reads=[f"pk{2+nch}"], writes=["x1t"])
            pr.dma("gpsimd", x4t[dt_], x1t[:], reads=["x1t"], writes=[f"x4o{dt_}"])
    return


RG = [[0, 1, 2, 3], [4, 5, 6, 7]]


class _NCProxy:
    def __init__(self, nc, prefix):
        self._nc = nc
        self._p = prefix
    def sbuf_tensor(self, name, *a, **k):
        return self._nc.sbuf_tensor(self._p + name, *a, **k)
    def psum_tensor(self, name, *a, **k):
        return self._nc.psum_tensor(self._p + name, *a, **k)
    def __getattr__(self, x):
        return getattr(self._nc, x)


def build_fused(debug=False):
    key = ("fused", debug)
    if key in _L1_CACHE:
        return _L1_CACHE[key]
    nc = bass.Bass("TRN2", target_bir_lowering=False)
    pr = Prog()
    def din(name, shape, dt=F32):
        return nc.dram_tensor(name, list(shape), dt, kind="ExternalInput").ap()
    def dout(name, shape, dt=F32):
        return nc.dram_tensor(name, list(shape), dt, kind="ExternalOutput").ap()
    def dint(name, shape, dt=F32):
        return nc.dram_tensor(name, list(shape), dt, kind="Internal").ap()

    io = {}
    for name, shape, dt in [
        ("xfp", (128, 32, 1024), F32), ("Fp", (8, 128, 32, 512), BF16),
        ("Gcp", (8, 128, 17, 512), BF16), ("GIp", (32, 128, 32, 128), BF16),
        ("identb", (128, 128), BF16), ("zerosb", (128, 128), BF16),
        ("onesb", (128, 1), BF16), ("onesrow", (1, 128), F32),
        ("selE", (4, 64), BF16), ("a0p", (128, 32, 64), F32),
        ("b0p", (64, 32, 128), F32), ("wr0p", (128, 32, 4), F32),
        ("wtip", (81, 128, 32, 128), F32), ("convw", (48, 128, 4), F32),
        ("convb", (48, 128, 1), F32), ("dtb", (64, 1), F32),
        ("alog", (64, 1), F32), ("maskc", (128, 128), F32),
        ("identf", (128, 128), F32), ("onesrowb", (1, 128), BF16),
        ("drep", (128, 32), F32), ("nwp", (128, 32), F32),
        ("wtop", (32, 128, 32, 128), F32), ("flags4", (128, 3), F32),
        ("halosel", (128, 4), F32), ("wtbp", (32, 128, 32, 128), F32),
        ("bbp", (128, 32), F32), ("a1p", (128, 32, 64), F32),
        ("b1p", (32, 128, 8, 128), F32), ("r1p", (128, 32, 16), F32),
        ("selp", (16, 8, 128), BF16),
    ]:
        io[name] = din(name, shape, dt)
    io["x4t"] = dout("x4t", (32, 128, 1024))
    io["x2t"] = dint("x2t_d", (32, 128, 1024))
    io["gate"] = dint("gate_d", (32, 128, 1024))
    io["hbc"] = dint("hbc_d", (48, 128, 1024))
    io["dt_o"] = io["dt_i"] = dint("dt_d", (64, 1024))
    io["xbc"] = dint("xbc_d", (128, 8, 48, 128), BF16)
    io["dtp_i"] = io["dtp_o"] = dint("dtp_d", (64, 1024))
    io["scum_i"] = io["scum_o"] = dint("scum_d", (64, 1024))
    io["x3t"] = dint("x3t_d", (32, 128, 1024))
    io["halo_out"] = dint("halo_in_d", (128, 144))
    halo_ag = dint("halo_ag_d", (512, 144))
    io["STout"] = dint("stloc_d", (128, 4096))
    scll = dint("scll_d", (64, 1))
    st_ag = dint("st_ag_d", (512, 4096))
    scl_ag = dint("scl_ag_d", (256, 1))
    io["st_slot"] = lambda k: (st_ag[k*128:(k+1)*128, :], scl_ag[k*64:(k+1)*64, :])
    if debug:
        io["dbg_x1"] = dout("dbg_x1", (32, 128, 1024))
        io["dbg_corr"] = dout("dbg_corr", (8, 128, 4096))
        io["dbg_s"] = dout("dbg_s", (8, 128, 4096))
        io["dbg_att"] = dout("dbg_att", (32, 128, 1024))

    from contextlib import ExitStack
    with ExitStack() as octx:
        sems = {e: octx.enter_context(nc.semaphore(f"sem_{e}")) for e in Prog.SEMS}
        block = octx.enter_context(nc.Block())

        emit_l0(_NCProxy(nc, "A"), pr, io, debug=debug)
        pr.barrier()
        emit_inproj(_NCProxy(nc, "B"), pr, io)
        pr.barrier()
        HOK = [f"ho{ct}" for ct in range(32, 80)]
        pr.op("gpsimd", lambda E_: E_.collective_compute(
            "AllGather", ALU.bypass, replica_groups=RG,
            ins=[io["halo_out"]], outs=[halo_ag]),
              reads=HOK, writes=["halo_ag"], dma=True)
        pr.barrier()

        def halo_fn(E, pr):
            hsl = E(nc.sbuf_tensor("Xhsl", [128, 576], F32))
            hout = E(nc.sbuf_tensor("Xhout", [128, 144], F32))
            hfl = E(nc.sbuf_tensor("Xhfl", [128, 4], F32))
            pr.dma("sync", hfl[:], io["halosel"], writes=["hfl"])
            for k in range(4):
                pr.dma("sync", hsl[:, k*144:(k+1)*144], halo_ag[k*128:(k+1)*128, :],
                       reads=["halo_ag"], writes=["hsl"])
            pr.op("vector", lambda E_: E_.memset(hout[:], 0.0), writes=["halo_sel"])
            for k in range(4):
                pr.op("vector", lambda E_, k=k: E_.scalar_tensor_tensor(
                    hout[:], hsl[:, k*144:(k+1)*144], hfl[:, k:k+1], hout[:],
                    op0=ALU.mult, op1=ALU.add),
                      reads=["hsl", "hfl", "halo_sel"], writes=["halo_sel"])
            return hout[:], ["halo_sel"]

        emit_conv(_NCProxy(nc, "C"), pr, io, halo_fn)
        pr.barrier()
        emit_scan(_NCProxy(nc, "D"), pr, io, with_init=False, tag="a_", y_work=False)
        pr.dma("gpsimd", scll.unsqueeze(-1), io["scum_o"][:, 1023:1024].unsqueeze(-1),
               reads=["scumo"], writes=["scll"], allow_slow_non_contiguous=True)
        pr.barrier()
        pr.op("gpsimd", lambda E_: E_.collective_compute(
            "AllGather", ALU.bypass, replica_groups=RG,
            ins=[io["STout"]], outs=[st_ag]),
              reads=["STo"], writes=["st_ag"], dma=True)
        pr.op("gpsimd", lambda E_: E_.collective_compute(
            "AllGather", ALU.bypass, replica_groups=RG,
            ins=[scll], outs=[scl_ag]),
              reads=["scll"], writes=["st_ag"], dma=True)
        pr.barrier()
        emit_scan(_NCProxy(nc, "E"), pr, io, with_init=True, debug=debug, tag="b_")
        pr.barrier()
        emit_ffn(_NCProxy(nc, "F"), pr, io)
        pr.finalize(nc, block, sems)
    _L1_CACHE[key] = nc
    return nc


def _declare(nc, io, names, dout_names=()):
    SHAPES = {
        "xfp": ((128, 32, 1024), F32), "Fp": ((8, 128, 32, 512), BF16),
        "Gcp": ((8, 128, 17, 512), BF16), "GIp": ((32, 128, 32, 128), BF16),
        "identb": ((128, 128), BF16), "zerosb": ((128, 128), BF16),
        "onesb": ((128, 1), BF16), "onesrow": ((1, 128), F32),
        "selE": ((4, 64), BF16), "a0p": ((128, 32, 64), F32),
        "b0p": ((64, 32, 128), F32), "wr0p": ((128, 32, 4), F32),
        "wtip": ((81, 128, 32, 128), F32), "convw": ((128, 48, 4), F32),
        "convb": ((128, 48), F32), "dtb": ((64, 1), F32),
        "alog": ((64, 1), F32), "maskc": ((128, 128), F32),
        "identf": ((128, 128), F32), "onesrowb": ((1, 128), BF16),
        "drep": ((128, 32), F32), "nwp": ((128, 32), F32),
        "wtop": ((32, 128, 32, 128), F32), "flags4": ((128, 3), F32),
        "wtbp": ((32, 128, 32, 128), F32), "bbp": ((128, 32), F32),
        "a1p": ((128, 32, 64), F32), "b1p": ((32, 128, 8, 128), F32),
        "r1p": ((128, 32, 16), F32), "selp": ((16, 8, 128), BF16),
        "x2t": ((32, 128, 1024), F32), "x2h": ((128, 32, 3), F32),
        "gate": ((32, 128, 1024), F32), "hbc": ((48, 128, 1024), F32),
        "dt_o": ((64, 1024), F32), "xbc": ((128, 8, 48, 128), BF16),
        "dtp_o": ((64, 1024), F32), "scum_o": ((64, 1024), F32),
        "STout": ((128, 4096), F32), "ST4": ((4, 128, 4096), F32),
        "scl4": ((4, 64, 1), F32), "x3t": ((32, 128, 1024), F32),
        "x4t": ((32, 128, 1024), F32),
    }
    for n in names:
        sh, dt = SHAPES[n]
        io[n] = nc.dram_tensor(n, list(sh), dt, kind="ExternalInput").ap()
    for n in dout_names:
        sh, dt = SHAPES[n]
        io[n] = nc.dram_tensor(n, list(sh), dt, kind="ExternalOutput").ap()


def _wrap(build_fn, key):
    if key in _L1_CACHE:
        return _L1_CACHE[key]
    nc = bass.Bass("TRN2", target_bir_lowering=False)
    pr = Prog()
    from contextlib import ExitStack
    with ExitStack() as octx:
        sems = {e: octx.enter_context(nc.semaphore(f"sem_{e}")) for e in Prog.SEMS}
        block = octx.enter_context(nc.Block())
        build_fn(nc, pr)
        pr.finalize(nc, block, sems)
    _L1_CACHE[key] = nc
    return nc


def build_a():
    def body(nc, pr):
        io = {}
        _declare(nc, io, ["xfp", "Fp", "Gcp", "GIp", "identb", "zerosb", "onesb",
                          "onesrow", "selE", "a0p", "b0p", "wr0p"], ["x2t"])
        emit_l0(_NCProxy(nc, "A"), pr, io)
    return _wrap(body, "A")


def build_b():
    def body(nc, pr):
        io = {}
        _declare(nc, io, ["x2t", "x2h", "wtip", "convw", "convb", "dtb", "alog",
                          "identb", "identf", "onesrow", "onesrowb", "onesb",
                          "maskc", "drep"],
                 ["gate", "hbc", "dt_o", "xbc", "dtp_o", "scum_o", "STout"])
        io["dt_i"] = io["dt_o"]
        io["dtp_i"] = io["dtp_o"]
        io["scum_i"] = io["scum_o"]
        from contextlib import ExitStack
        with ExitStack() as hstack:
            halo_sb = hstack.enter_context(nc.sbuf_tensor("haloSB", [128, 144], F32))
            emit_inproj(_NCProxy(nc, "B"), pr, io, halo_sb=halo_sb[:])
            pr.barrier()
            emit_conv(_NCProxy(nc, "C"), pr, io,
                      lambda E, pr_: (halo_sb[:], ["halo_sel"]))
            pr.barrier()
            emit_scan(_NCProxy(nc, "D"), pr, io, with_init=False, tag="a_",
                      y_work=False)
    return _wrap(body, "B")


def build_c(debug=False):
    def body(nc, pr):
        io = {}
        _declare(nc, io, ["xbc", "dtp_o", "scum_o", "gate", "x2t", "ST4", "scl4",
                          "flags4", "maskc", "identb", "identf", "onesrow",
                          "onesrowb", "onesb", "drep", "nwp", "wtop", "wtbp",
                          "bbp", "a1p", "b1p", "r1p", "selp"], ["x4t"])
        io["dtp_i"] = io["dtp_o"]
        io["scum_i"] = io["scum_o"]
        io["x3t"] = nc.dram_tensor("x3t_d", [32, 128, 1024], F32, kind="Internal").ap()
        io["st_slot"] = lambda k: (io["ST4"][k], io["scl4"][k])
        emit_scan(_NCProxy(nc, "E"), pr, io, with_init=True, tag="b_")
        pr.barrier()
        emit_ffn(_NCProxy(nc, "F"), pr, io)
    return _wrap(body, "C")


_CONST_ALL = ["Fp", "Gcp", "GIp", "identb", "zerosb", "onesb", "onesrow", "selE",
              "a0p", "b0p", "wr0p", "wtip", "convw", "convb", "dtb", "alog",
              "maskc", "identf", "onesrowb", "drep", "nwp", "wtop", "wtbp",
              "bbp", "a1p", "b1p", "r1p", "selp"]


def kernel(**inputs):
    pp = _prep(inputs)
    cores = list(range(NCORE))

    nc = build_a()
    maps = []
    for c in cores:
        m = {k: np.asarray(pp[k]) for k in ["Fp", "Gcp", "GIp", "identb", "zerosb",
                                            "onesb", "onesrow", "selE", "a0p",
                                            "b0p", "wr0p"]}
        m["xfp"] = pp["xfp"][c]
        maps.append(m)
    rA = run_bass_kernel_spmd(nc, maps, core_ids=cores).results

    nc = build_b()
    maps = []
    for c in cores:
        if c % 4 == 0:
            x2h = np.zeros((128, 32, 3), np.float32)
        else:
            x2h = np.ascontiguousarray(
                np.asarray(rA[c - 1]["x2t"])[:, :, 1021:1024].transpose(1, 0, 2))
        m = dict(x2t=rA[c]["x2t"], x2h=x2h, wtip=pp["wtip"], convw=pp["convw"],
                 convb=pp["convb"], dtb=pp["dtb"], alog=pp["alog"],
                 identb=pp["identb"], identf=pp["identf"], onesrow=pp["onesrow"],
                 onesrowb=pp["onesrowb"], onesb=pp["onesb"], maskc=pp["maskc"],
                 drep=pp["drep"])
        maps.append(m)
    rB = run_bass_kernel_spmd(nc, maps, core_ids=cores).results

    nc = build_c()
    maps = []
    for c in cores:
        g0 = 4 * (c // 4)
        ST4 = np.stack([np.asarray(rB[g0 + k]["STout"]) for k in range(4)])
        scl4 = np.stack([np.asarray(rB[g0 + k]["scum_o"])[:, 1023:1024]
                         for k in range(4)])
        fl = np.zeros((128, 3), np.float32)
        if c % 4 > 0:
            fl[:, c % 4 - 1] = 1.0
        m = dict(xbc=rB[c]["xbc"], dtp_o=rB[c]["dtp_o"], scum_o=rB[c]["scum_o"],
                 gate=rB[c]["gate"], x2t=rA[c]["x2t"], ST4=ST4, scl4=scl4,
                 flags4=fl, maskc=pp["maskc"], identb=pp["identb"],
                 identf=pp["identf"], onesrow=pp["onesrow"],
                 onesrowb=pp["onesrowb"], onesb=pp["onesb"], drep=pp["drep"],
                 nwp=pp["nwp"], wtop=pp["wtop"], wtbp=pp["wtbp"], bbp=pp["bbp"],
                 a1p=pp["a1p"], b1p=pp["b1p"], r1p=pp["r1p"], selp=pp["selp"])
        maps.append(m)
    rC = run_bass_kernel_spmd(nc, maps, core_ids=cores).results

    out = np.empty((2, 4096, 4096), np.float32)
    for c in cores:
        b, q = c // 4, c % 4
        out[b, q*1024:(q+1)*1024, :] = np.asarray(
            rC[c]["x4t"]).reshape(4096, 1024).T
    return out
